# version 03
# baseline (speedup 1.0000x reference)
import contextlib
import math
import numpy as np
import ml_dtypes
import concourse.bass as bass
import concourse.mybir as mybir
from concourse.bass_utils import run_bass_kernel_spmd

F32 = mybir.dt.float32
BF16 = mybir.dt.bfloat16
AF = mybir.ActivationFunctionType
ALU = mybir.AluOpType
AX = mybir.AxisListType
ts = bass.ts

T = 4096
D = 2048
NCORES = 8
EPS = 1e-5


def make_identities(nc, g, aps, fence_sem, fence_base=0):
    for i, ap in enumerate(aps):
        ins = g.memset(ap, 0.0)
        if i == len(aps) - 1:
            ins.then_inc(fence_sem, 1)
    g.wait_ge(fence_sem, fence_base + 1)
    for i, ap in enumerate(aps):
        sq = ap.shape[0]
        ins = g.affine_select(out=ap, in_=ap, compare_op=ALU.not_equal,
                              fill=1.0, base=0, pattern=[[-1, sq]],
                              channel_multiplier=1)
        if i == len(aps) - 1:
            ins.then_inc(fence_sem, 1)
    g.wait_ge(fence_sem, fence_base + 2)


class Chain:

    def __init__(self, eng, sem):
        self.eng = eng
        self.sem = sem
        self.n = 0

    def op(self, ins):
        ins.then_inc(self.sem, 1)
        self.n += 1
        return ins

    def wait(self):
        self.eng.wait_ge(self.sem, self.n)


def build_l1ffn(stage=5):
    nc = bass.Bass("TRN2", target_bir_lowering=False)
    x3 = nc.dram_tensor("x3", [T, D], F32, kind="ExternalInput").ap()
    factors = nc.dram_tensor("factors", [8, D, D], F32, kind="ExternalInput").ap()
    bias = nc.dram_tensor("bias", [D], F32, kind="ExternalInput").ap()
    out = nc.dram_tensor("out", [T, D], F32, kind="ExternalOutput").ap()

    CT = D // 128
    ET = D // 128
    TCH = 8
    TC = T // TCH
    NSUB = TC // 128
    NTI = CT * NSUB
    NTO = ET * NSUB

    es = contextlib.ExitStack()
    sb = lambda *a: es.enter_context(nc.sbuf_tensor(*a))
    ps = lambda *a: es.enter_context(nc.psum_tensor(*a))
    t_ws32 = [sb(f"ws32_{i}", [128, D], F32) for i in range(2)]
    t_wsbf = sb("wsbf", [128, CT * D], BF16)
    t_bias = sb("biascol", [128, ET], F32)
    t_id = sb("ident", [128, 128], F32)
    t_idb = sb("identb", [128, 128], BF16)
    t_x3tok = sb("x3tok", [128, NSUB * D], F32)
    t_x3T = sb("x3T", [128, CT * TC], BF16)
    t_mixT = sb("mixT", [128, ET * TC], BF16)
    t_s = sb("s_res", [128, NSUB * D], F32)
    t_stats = sb("stats", [128, 4 * 24], F32)
    t_mv = sb("mv", [128, 4 * 2], F32)
    st = {k: sb("st_" + k, [128, 4], F32) for k in ["rstd", "lnv"]}
    st["eps"] = sb("st_eps", [128, 1], F32)
    p_ti = [ps(f"p_ti{i}", [128, 512], F32) for i in range(2)]
    p_mm = [ps(f"p_mm{i}", [128, 512], F32) for i in range(2)]
    p_to = [ps(f"p_to{i}", [128, 1024], BF16) for i in range(2)]

    names = ["sWD", "sWC", "sB", "sX", "sTIr", "sTIe", "sMM", "sEV",
             "sTOr", "sTOc", "sv", "ssc", "sLN", "sOUT", "sID", "fG", "fV", "fS"]
    sms = {n: es.enter_context(nc.semaphore(n)) for n in names}
    (sWD, sWC, sB, sX, sTIr, sTIe, sMM, sEV, sTOr, sTOc, sv, ssc, sLN, sOUT,
     sID, fG, fV, fS) = (sms[n] for n in names)

    with es, nc.Block() as blk:
        @blk.gpsimd
        def _(g):
            g.memset(st["eps"][:], EPS)
            make_identities(nc, g, [t_id[:], t_idb[:]], fG)
            g.nop().then_inc(sID, 1)
            for c in range(CT):
                if c >= 2:
                    g.wait_ge(sWC, c - 1)
                for r in range(8):
                    if r > 0 or c > 0:
                        g.wait_ge(sWD, 16 * (c * 8 + r))
                    kw = {} if r == 0 else dict(accum_op=ALU.add)
                    g.dma_start(out=t_ws32[c % 2][:], in_=factors[r, ts(c, 128), :],
                                **kw).then_inc(sWD, 16)

        @blk.sync
        def _(s):
            for e in range(ET):
                s.dma_start(out=t_bias[:, e:e + 1],
                            in_=bias[ts(e, 128)].rearrange("(p o) -> p o", o=1)).then_inc(sB, 16)
            for ch in range(TCH):
                if ch > 0:
                    s.wait_ge(sLN, ch * NSUB)
                    s.wait_ge(sOUT, 16 * ch * NSUB)
                for tsub in range(NSUB):
                    s.dma_start(out=t_x3tok[:, tsub * D:(tsub + 1) * D],
                                in_=x3[ts(ch * NSUB + tsub, 128), :]).then_inc(sX, 16)
                for tsub in range(NSUB):
                    k = ch * NSUB + tsub
                    s.wait_ge(sLN, k + 1)
                    s.dma_start(out=out[ts(k, 128), :],
                                in_=t_x3tok[:, tsub * D:(tsub + 1) * D]).then_inc(sOUT, 16)
            s.wait_ge(sOUT, 16 * TCH * NSUB)

        @blk.tensor
        def _(t):
            t.wait_ge(sID, 1)
            for ch in range(TCH):
                t.wait_ge(sX, 16 * (ch + 1) * NSUB)
                for j in range(NTI):
                    tsub, ct = divmod(j, CT)
                    if ch * NTI + j >= 2:
                        t.wait_ge(sTIe, ch * NTI + j - 1)
                    t.transpose(p_ti[j % 2][:, 0:128],
                                t_x3tok[:, tsub * D + ct * 128: tsub * D + (ct + 1) * 128],
                                t_id[:]).then_inc(sTIr, 1)
                if stage < 2:
                    continue
                if ch == 0:
                    t.wait_ge(sWC, CT)
                t.wait_ge(sTIe, (ch + 1) * NTI)
                for et in range(ET):
                    if ch * ET + et >= 2:
                        t.wait_ge(sEV, ch * ET + et - 1)
                    for c in range(CT):
                        ins = t.matmul(p_mm[et % 2][:],
                                       t_wsbf[:, c * D + et * 128: c * D + (et + 1) * 128],
                                       t_x3T[:, c * TC:(c + 1) * TC],
                                       start=(c == 0), stop=(c == CT - 1))
                    ins.then_inc(sMM, 1)
                if stage < 3:
                    continue
                for j in range(NTO):
                    et, tsub = divmod(j, NSUB)
                    t.wait_ge(sEV, ch * ET + et + 1)
                    if ch * NTO + j >= 2:
                        t.wait_ge(sTOc, ch * NTO + j - 1)
                    t.transpose(p_to[j % 2][:, 0:128],
                                t_mixT[:, et * TC + tsub * 128: et * TC + (tsub + 1) * 128],
                                t_idb[:]).then_inc(sTOr, 1)

        @blk.scalar
        def _(s):
            s.wait_ge(sID, 1)
            for ch in range(TCH):
                for j in range(NTI):
                    tsub, ct = divmod(j, CT)
                    s.wait_ge(sTIr, ch * NTI + j + 1)
                    s.activation(
                        t_x3T[:, ct * TC + tsub * 128: ct * TC + (tsub + 1) * 128],
                        p_ti[j % 2][:, 0:128], AF.Copy).then_inc(sTIe, 1)
                if stage < 5:
                    continue
                s.wait_ge(sv, ch + 1)
                for tsub in range(NSUB):
                    ins = s.activation(st["lnv"][:, tsub:tsub + 1],
                                       t_mv[:, 2 * tsub + 1:2 * tsub + 2],
                                       AF.Ln, bias=st["eps"][:])
                    if tsub == NSUB - 1:
                        ins.then_inc(fS, 1)
                s.wait_ge(fS, ch + 1)
                for tsub in range(NSUB):
                    ins = s.activation(st["rstd"][:, tsub:tsub + 1],
                                       st["lnv"][:, tsub:tsub + 1], AF.Exp, scale=-0.5)
                    if tsub == NSUB - 1:
                        ins.then_inc(ssc, 1)

        @blk.vector
        def _(v):
            v.wait_ge(sB, 16 * ET)
            for c in range(CT):
                v.wait_ge(sWD, (c + 1) * 8 * 16)
                v.tensor_copy(t_wsbf[:, c * D:(c + 1) * D],
                              t_ws32[c % 2][:]).then_inc(sWC, 1)
            for ch in range(TCH):
                if stage < 3:
                    v.wait_ge(sTIe, (ch + 1) * NTI)
                    for tsub in range(NSUB):
                        v.engine_nop().then_inc(sLN, 1)
                if stage < 2:
                    continue
                for et in range(ET):
                    v.wait_ge(sMM, ch * ET + et + 1)
                    v.tensor_scalar(t_mixT[:, et * TC:(et + 1) * TC], p_mm[et % 2][:],
                                    1.0 / math.sqrt(8.0), t_bias[:, et:et + 1],
                                    ALU.mult, ALU.add).then_inc(sEV, 1)
                if stage < 3:
                    continue
                for j in range(NTO):
                    et, tsub = divmod(j, NSUB)
                    if j == 0 and stage >= 5:
                        v.wait_ge(ssc, ch)
                    v.wait_ge(sTOr, ch * NTO + j + 1)
                    ins = v.scalar_tensor_tensor(
                        t_s[:, tsub * D + et * 128: tsub * D + (et + 1) * 128],
                        t_x3tok[:, tsub * D + et * 128: tsub * D + (et + 1) * 128],
                        1.0, p_to[j % 2][:, 0:128],
                        ALU.mult, ALU.add)
                    ins.then_inc(sTOc, 1)
                    if j == NTO - 1:
                        v.wait_ge(sTOc, (ch + 1) * NTO)
                if stage < 4:
                    for tsub in range(NSUB):
                        v.engine_nop().then_inc(sLN, 1)
                    continue
                for tsub in range(NSUB):
                    for gix in range(4):
                        ins = v.bn_stats(
                            t_stats[:, tsub * 24 + gix * 6: tsub * 24 + gix * 6 + 6],
                            t_s[:, tsub * D + gix * 512: tsub * D + (gix + 1) * 512])
                        if tsub == NSUB - 1 and gix == 3:
                            ins.then_inc(fV, 1)
                v.wait_ge(fV, ch + 1)
                for tsub in range(NSUB):
                    ins_ = v.bn_aggr(t_mv[:, 2 * tsub:2 * tsub + 2],
                                     t_stats[:, tsub * 24:(tsub + 1) * 24])
                    if tsub == NSUB - 1:
                        ins_.then_inc(sv, 1)
                if stage < 5:
                    for tsub in range(NSUB):
                        v.engine_nop().then_inc(sLN, 1)
                    continue
                v.wait_ge(ssc, ch + 1)
                for tsub in range(NSUB):
                    s_ap = t_s[:, tsub * D:(tsub + 1) * D]
                    o_ap = t_x3tok[:, tsub * D:(tsub + 1) * D]
                    v.tensor_scalar(o_ap, s_ap, t_mv[:, 2 * tsub:2 * tsub + 1],
                                    st["rstd"][:, tsub:tsub + 1],
                                    ALU.subtract, ALU.mult).then_inc(sLN, 1)

    return nc


def kernel(**inputs):
    raise NotImplementedError("phases under construction")


def build_l1preA():
    nc = bass.Bass("TRN2", target_bir_lowering=False)
    x2 = nc.dram_tensor("x2", [T, D], F32, kind="ExternalInput").ap()
    inprojT = nc.dram_tensor("inprojT", [D, 8192], F32, kind="ExternalInput").ap()
    convw = nc.dram_tensor("convw", [4096, 4], F32, kind="ExternalInput").ap()
    convb = nc.dram_tensor("convb", [4096], F32, kind="ExternalInput").ap()
    ln1w = nc.dram_tensor("ln1w", [D], F32, kind="ExternalInput").ap()
    ln1b = nc.dram_tensor("ln1b", [D], F32, kind="ExternalInput").ap()
    xcT = nc.dram_tensor("xcT", [4096, T], BF16, kind="ExternalOutput").ap()
    zT = nc.dram_tensor("zT", [4096, T], BF16, kind="ExternalOutput").ap()

    CT = D // 128
    NTT = T // 128
    NDG = 32
    NTCH = 8
    NCONST = 16 * 2 + 1 + 32

    es = contextlib.ExitStack()
    sb = lambda *a: es.enter_context(nc.sbuf_tensor(*a))
    ps = lambda *a: es.enter_context(nc.psum_tensor(*a))
    t_id = sb("ident", [128, 128], F32)
    t_x2 = [sb(f"x2b{i}", [128, D], F32) for i in range(2)]
    t_u0 = sb("u0b", [128, D], F32)
    t_u1T = sb("u1T", [128, CT * T], BF16)
    t_stats = sb("stats", [128, 24], F32)
    t_mv = sb("mv", [128, 2], F32)
    t_lnv = sb("lnv", [128, 1], F32)
    t_rstd = sb("rstd", [128, 1], F32)
    t_eps = sb("eps", [128, 1], F32)
    t_lnw = sb("lnw", [128, CT], F32)
    t_lnb = sb("lnb", [128, CT], F32)
    t_W = [sb(f"Wc{i}", [128, 16 * 256], BF16) for i in range(2)]
    t_slab = [sb(f"slab{i}", [128, T + 4], BF16) for i in range(2)]
    t_acc = [sb(f"acc{i}", [128, 1024], F32) for i in range(2)]
    t_xc = [sb(f"xcsb{i}", [128, 1024], BF16) for i in range(2)]
    t_z = [sb(f"zb{i}", [128, 512], BF16) for i in range(2)]
    t_cw = sb("cwcol", [128, 4 * 32], F32)
    t_cb = sb("cbcol", [128, 32], F32)
    p4 = [ps(f"pb{i}", [128, 1024], F32) for i in range(4)]

    names = ["sX", "svA", "sscA", "sAP", "sTIr", "sTIe", "sID", "sMMb",
             "sEVb", "svB", "sscB", "fV", "fS", "fG",
             "sXa", "sXb", "sWa", "sWb", "sZa", "sZb", "sXCa", "sXCb"]
    sms = {n: es.enter_context(nc.semaphore(n)) for n in names}
    (sX, svA, sscA, sAP, sTIr, sTIe, sID, sMMb, sEVb, svB, sscB,
     fV, fS, fG, sXa, sXb, sWa, sWb, sZa, sZb, sXCa, sXCb) = (
        sms[n] for n in names)
    sX2 = [sXa, sXb]
    sW2 = [sWa, sWb]
    sZ2 = [sZa, sZb]
    sXC2 = [sXCa, sXCb]

    z_list = []
    for dg in range(16, NDG):
        for tch in range(NTCH):
            for ab in range(2):
                gi = dg * NTCH + tch
                z_list.append((dg, tch, ab, 2 * gi + ab + 1))

    def pslot(i):
        return p4[i // 2][:, (i % 2) * 512:(i % 2 + 1) * 512]

    with es, nc.Block() as blk:
        @blk.gpsimd
        def _(g):
            g.memset(t_eps[:], EPS)
            g.memset(t_slab[0][:, 0:3], 0.0)
            g.memset(t_slab[1][:, 0:3], 0.0)
            make_identities(nc, g, [t_id[:]], fG)
            g.nop().then_inc(sID, 1)
            for dg in range(NDG):
                if dg >= 2:
                    g.wait_ge(sMMb, 2 * NTCH * (dg - 1))
                g.dma_start(
                    out=t_W[dg % 2][:],
                    in_=inprojT.rearrange("(c p) n -> p c n", p=128)[
                        :, :, dg * 256:(dg + 1) * 256]
                ).then_inc(sW2[dg % 2], 16)


        @blk.sync
        def _(s):
            for c in range(CT):
                s.dma_start(out=t_lnw[:, c:c + 1],
                            in_=ln1w[ts(c, 128)].rearrange("(p o) -> p o", o=1)).then_inc(sX, 16)
                s.dma_start(out=t_lnb[:, c:c + 1],
                            in_=ln1b[ts(c, 128)].rearrange("(p o) -> p o", o=1)).then_inc(sX, 16)
            s.dma_start(out=t_cw[:],
                        in_=convw.rearrange("(i p) j -> p i j", p=128)).then_inc(sX, 16)
            for i_ in range(32):
                s.dma_start(out=t_cb[:, i_:i_ + 1],
                            in_=convb[ts(i_, 128)].rearrange("(p o) -> p o", o=1)
                            ).then_inc(sX, 16)
            for i in range(NTT):
                if i >= 2:
                    s.wait_ge(sAP, i - 1)
                s.dma_start(out=t_x2[i % 2][:],
                            in_=x2[ts(i, 128), :]).then_inc(sX2[i % 2], 16)
            for dt_ in range(32):
                for h in range(4):
                    q = 4 * dt_ + h
                    s.wait_ge(sscB, q + 1)
                    s.dma_start(out=xcT[ts(dt_, 128), h * 1024:(h + 1) * 1024],
                                in_=t_xc[q % 2][:]).then_inc(sXC2[q % 2], 16)
            for zi, (dg, tch, ab, ev) in enumerate(z_list):
                s.wait_ge(sEVb, ev)
                ztile = 2 * dg + ab - 32
                s.dma_start(out=zT[ts(ztile, 128), tch * 512:(tch + 1) * 512],
                            in_=t_z[ab][:]).then_inc(sZ2[ab], 16)
            s.wait_ge(sXCa, 64 * 16)
            s.wait_ge(sXCb, 64 * 16)
            s.wait_ge(sZa, len(z_list) // 2 * 16)
            s.wait_ge(sZb, len(z_list) // 2 * 16)

        @blk.tensor
        def _(t):
            t.wait_ge(sID, 1)
            for i in range(NTT):
                t.wait_ge(sAP, i + 1)
                for c in range(CT):
                    j = i * CT + c
                    if j >= 2:
                        t.wait_ge(sTIe, j - 1)
                    t.transpose(p4[j % 2][:, 0:128],
                                t_u0[:, c * 128:(c + 1) * 128],
                                t_id[:]).then_inc(sTIr, 1)
            t.wait_ge(sTIe, NTT * CT)
            for dg in range(NDG):
                t.wait_ge(sW2[dg % 2], 16 * (dg // 2 + 1))
                for tch in range(NTCH):
                    gi = dg * NTCH + tch
                    pp = gi % 2
                    if gi >= 2:
                        t.wait_ge(sEVb, 2 * (gi - 1))
                    for c in range(CT):
                        for ab in range(2):
                            ins = t.matmul(
                                pslot(pp * 2 + ab),
                                t_W[dg % 2][:, c * 256 + ab * 128: c * 256 + (ab + 1) * 128],
                                t_u1T[:, c * T + tch * 512: c * T + (tch + 1) * 512],
                                start=(c == 0), stop=(c == CT - 1))
                            if c == CT - 1:
                                ins.then_inc(sMMb, 1)

        @blk.scalar
        def _(s):
            s.wait_ge(sID, 1)
            s.wait_ge(sX, NCONST * 16)
            cS = Chain(s, fS)
            for i in range(NTT):
                s.wait_ge(svA, i + 1)
                cS.wait(); cS.op(s.activation(t_lnv[:], t_mv[:, 1:2], AF.Ln, bias=t_eps[:]))
                cS.wait()
                s.activation(t_rstd[:], t_lnv[:], AF.Exp, scale=-0.5).then_inc(sscA, 1)
                s.wait_ge(sscA, i + 1)
            for dt_ in range(32):
                for h in range(4):
                    q = 4 * dt_ + h
                    if q >= 2:
                        s.wait_ge(sXC2[q % 2], 16 * (q // 2))
                    s.wait_ge(svB, q + 1)
                    s.activation(t_xc[q % 2][:], t_acc[1][:], AF.Silu,
                                 bias=t_cb[:, dt_:dt_ + 1]).then_inc(sscB, 1)

        @blk.vector
        def _(v):
            v.wait_ge(sX, NCONST * 16)
            cV = Chain(v, fV)
            for i in range(NTT):
                v.wait_ge(sX2[i % 2], (i // 2 + 1) * 16)
                v.wait_ge(svA, i)
                for gix in range(4):
                    cV.op(v.bn_stats(t_stats[:, gix * 6:(gix + 1) * 6],
                                     t_x2[i % 2][:, gix * 512:(gix + 1) * 512]))
                cV.wait()
                v.bn_aggr(t_mv[:], t_stats[:]).then_inc(svA, 1)
                v.wait_ge(sscA, i + 1)
                if i >= 1:
                    v.wait_ge(sTIr, i * CT)
                v.wait_ge(fV, cV.n)
                v.tensor_scalar(t_u0[:], t_x2[i % 2][:], t_mv[:, 0:1],
                                t_rstd[:], ALU.subtract, ALU.mult).then_inc(sAP, 1)
                for c in range(CT):
                    j = i * CT + c
                    v.wait_ge(sTIr, j + 1)
                    v.tensor_scalar(t_u1T[:, c * T + i * 128: c * T + (i + 1) * 128],
                                    p4[j % 2][:, 0:128], t_lnw[:, c:c + 1],
                                    t_lnb[:, c:c + 1], ALU.mult,
                                    ALU.add).then_inc(sTIe, 1)
            nzb = [0, 0]
            for dg in range(NDG):
                for tch in range(NTCH):
                    gi = dg * NTCH + tch
                    pp = gi % 2
                    for ab in range(2):
                        dout = 2 * dg + ab
                        v.wait_ge(sMMb, 2 * gi + ab + 1)
                        if dout < 32:
                            v.tensor_copy(
                                t_slab[ab][:, 3 + tch * 512: 3 + (tch + 1) * 512],
                                pslot(pp * 2 + ab)).then_inc(sEVb, 1)
                        else:
                            nzb[ab] += 1
                            if nzb[ab] > 1:
                                v.wait_ge(sZ2[ab], 16 * (nzb[ab] - 1))
                            v.tensor_copy(t_z[ab][:],
                                          pslot(pp * 2 + ab)).then_inc(sEVb, 1)
                if 2 * dg < 32:
                    v.wait_ge(sEVb, 2 * (dg * NTCH + NTCH - 1) + 2)
                    for ab in range(2):
                        dt_ = 2 * dg + ab
                        sl = t_slab[ab]
                        for h in range(4):
                            base = h * 1024
                            cV.wait()
                            cV.op(v.tensor_scalar(
                                t_acc[0][:], sl[:, base:base + 1024],
                                t_cw[:, 4 * dt_:4 * dt_ + 1], None, ALU.mult))
                            cV.wait()
                            cV.op(v.scalar_tensor_tensor(
                                t_acc[1][:], sl[:, base + 1:base + 1 + 1024],
                                t_cw[:, 4 * dt_ + 1:4 * dt_ + 2], t_acc[0][:],
                                ALU.mult, ALU.add))
                            cV.wait()
                            cV.op(v.scalar_tensor_tensor(
                                t_acc[0][:], sl[:, base + 2:base + 2 + 1024],
                                t_cw[:, 4 * dt_ + 2:4 * dt_ + 3], t_acc[1][:],
                                ALU.mult, ALU.add))
                            cV.wait()
                            ins = v.scalar_tensor_tensor(
                                t_acc[1][:], sl[:, base + 3:base + 3 + 1024],
                                t_cw[:, 4 * dt_ + 3:4 * dt_ + 4], t_acc[0][:],
                                ALU.mult, ALU.add)
                            ins.then_inc(svB, 1)
                            v.wait_ge(sscB, 4 * dt_ + h + 1)

    return nc


def build_l1preB():
    nc = bass.Bass("TRN2", target_bir_lowering=False)
    xcT = nc.dram_tensor("xcT", [4096, T], BF16, kind="ExternalInput").ap()
    xprojT = nc.dram_tensor("xprojT", [4096, 160], F32, kind="ExternalInput").ap()
    dtprojT = nc.dram_tensor("dtprojT", [128, 4096], F32, kind="ExternalInput").ap()
    dtprojb = nc.dram_tensor("dtprojb", [4096], F32, kind="ExternalInput").ap()
    dtT = nc.dram_tensor("dtT", [4096, T], F32, kind="ExternalOutput").ap()
    BT = nc.dram_tensor("BT", [16, T], F32, kind="ExternalOutput").ap()
    CTo = nc.dram_tensor("CTo", [16, T], F32, kind="ExternalOutput").ap()

    NTCH = 8
    TC = T // NTCH

    es = contextlib.ExitStack()
    sb = lambda *a: es.enter_context(nc.sbuf_tensor(*a))
    ps = lambda *a: es.enter_context(nc.psum_tensor(*a))
    t_xp = sb("xpc", [128, 32 * 160], BF16)
    t_dp = sb("dpc", [128, 4096], BF16)
    t_db = sb("dbcol", [128, 32], F32)
    t_one = sb("onecol", [128, 1], F32)
    t_xcb = [sb(f"xcb{i}", [128, TC], BF16) for i in range(2)]
    t_draw = sb("draw", [128, TC], BF16)
    t_bc = sb("bcsb", [128, TC], F32)
    t_y = sb("y", [128, TC], F32)
    t_ny = sb("ny", [128, TC], F32)
    t_ay = sb("ay", [128, TC], F32)
    t_e = sb("e", [128, TC], F32)
    t_l = sb("l", [128, TC], F32)
    t_r = sb("r", [128, TC], F32)
    t_do = [sb(f"dto{i}", [128, TC], F32) for i in range(2)]
    p4 = [ps(f"pb{i}", [128, 1024], F32) for i in range(4)]

    def pslot(i):
        return p4[i // 2][:, (i % 2) * 512:(i % 2 + 1) * 512]

    names = ["sC", "sDB", "sMMx", "sEVx", "sBC", "sMMd", "sPD", "svP", "sscP",
             "fV", "fS", "sDTa", "sDTb", "sXca", "sXcb"]
    sms = {n: es.enter_context(nc.semaphore(n)) for n in names}
    (sC, sDB, sMMx, sEVx, sBC, sMMd, sPD, svP, sscP, fV, fS, sDTa, sDTb,
     sXca, sXcb) = (sms[n] for n in names)
    sDT2 = [sDTa, sDTb]
    sXc2 = [sXca, sXcb]

    with es, nc.Block() as blk:
        @blk.gpsimd
        def _(g):
            g.memset(t_one[:], 1.0)
            g.dma_start(out=t_xp[:],
                        in_=xprojT.rearrange("(i p) n -> p i n", p=128)).then_inc(sC, 16)
            g.dma_start(out=t_dp[:], in_=dtprojT).then_inc(sC, 16)

        @blk.sync
        def _(s):
            for i_ in range(32):
                s.dma_start(out=t_db[:, i_:i_ + 1],
                            in_=dtprojb[ts(i_, 128)].rearrange("(p o) -> p o", o=1)
                            ).then_inc(sDB, 16)
            for tch in range(NTCH):
                for dt_ in range(32):
                    li = tch * 32 + dt_
                    if li >= 2:
                        s.wait_ge(sMMx, li - 1)
                    s.dma_start(out=t_xcb[li % 2][:],
                                in_=xcT[ts(dt_, 128), tch * TC:(tch + 1) * TC]
                                ).then_inc(sXc2[li % 2], 16)
                s.wait_ge(sEVx, 2 * tch + 2)
                s.dma_start(out=BT[:, tch * TC:(tch + 1) * TC],
                            in_=t_bc[0:16, :]).then_inc(sBC, 16)
                s.dma_start(out=CTo[:, tch * TC:(tch + 1) * TC],
                            in_=t_bc[16:32, :]).then_inc(sBC, 16)
                for do in range(32):
                    q = tch * 32 + do
                    s.wait_ge(svP, 2 * q + 2)
                    s.dma_start(out=dtT[ts(do, 128), tch * TC:(tch + 1) * TC],
                                in_=t_do[q % 2][:]).then_inc(sDT2[q % 2], 16)
            s.wait_ge(sDTa, 128 * 16)
            s.wait_ge(sDTb, 128 * 16)
            s.wait_ge(sBC, NTCH * 2 * 16)

        @blk.tensor
        def _(t):
            t.wait_ge(sC, 32)
            for tch in range(NTCH):
                if tch >= 1:
                    t.wait_ge(sEVx, 2 * tch)
                for dt_ in range(32):
                    li = tch * 32 + dt_
                    t.wait_ge(sXc2[li % 2], 16 * (li // 2 + 1))
                    t.matmul(pslot(0), t_xp[:, dt_ * 160: dt_ * 160 + 128],
                             t_xcb[li % 2][:], start=(dt_ == 0), stop=(dt_ == 31))
                    ins = t.matmul(pslot(1)[0:32, :],
                                   t_xp[:, dt_ * 160 + 128: dt_ * 160 + 160],
                                   t_xcb[li % 2][:], start=(dt_ == 0), stop=(dt_ == 31))
                    ins.then_inc(sMMx, 1)
                t.wait_ge(sEVx, 2 * tch + 1)
                for do in range(32):
                    q = tch * 32 + do
                    if q >= 2:
                        t.wait_ge(sPD, q - 1)
                    t.matmul(pslot(2 + q % 2), t_dp[:, do * 128:(do + 1) * 128],
                             t_draw[:], start=True, stop=True).then_inc(sMMd, 1)

        @blk.scalar
        def _(s):
            cS = Chain(s, fS)
            for tch in range(NTCH):
                for do in range(32):
                    q = tch * 32 + do
                    s.wait_ge(svP, 2 * q + 1)
                    cS.wait()
                    cS.op(s.activation(t_e[:], t_ay[:], AF.Exp, scale=-1.0))
                    cS.wait()
                    s.activation(t_l[:], t_e[:], AF.Ln,
                                 bias=t_one[:]).then_inc(sscP, 1)
                    s.wait_ge(sscP, q + 1)

        @blk.vector
        def _(v):
            v.wait_ge(sDB, 32 * 16)
            cV = Chain(v, fV)
            for tch in range(NTCH):
                v.wait_ge(sMMx, 32 * (tch + 1))
                if tch >= 1:
                    v.wait_ge(sMMd, 32 * tch)
                v.tensor_copy(t_draw[:], pslot(0)).then_inc(sEVx, 1)
                if tch >= 1:
                    v.wait_ge(sBC, 2 * 16 * tch)
                v.tensor_copy(t_bc[0:32, :], pslot(1)[0:32, :]).then_inc(sEVx, 1)
                for do in range(32):
                    q = tch * 32 + do
                    v.wait_ge(sMMd, q + 1)
                    if q >= 2:
                        v.wait_ge(sDT2[q % 2], 16 * (q // 2))
                    cV.wait()
                    cV.op(v.tensor_scalar(t_y[:], pslot(2 + q % 2),
                                          t_db[:, do:do + 1], None, ALU.add))
                    v.wait_ge(fV, cV.n)
                    v.engine_nop().then_inc(sPD, 1)
                    cV.wait()
                    cV.op(v.tensor_scalar(t_ny[:], t_y[:], -1.0, None, ALU.mult))
                    cV.wait()
                    ins = v.tensor_tensor(t_ay[:], t_y[:], t_ny[:], ALU.max)
                    ins.then_inc(svP, 1)
                    cV.n += 0
                    cV.wait()
                    cV.op(v.tensor_scalar(t_r[:], t_y[:], 0.0, None, ALU.max))
                    v.wait_ge(sscP, q + 1)
                    cV.wait()
                    ins = v.tensor_tensor(t_do[q % 2][:], t_r[:], t_l[:], ALU.add)
                    ins.then_inc(svP, 1)
                    v.wait_ge(svP, 2 * q + 2)

    return nc


def build_l1scan():
    nc = bass.Bass("TRN2", target_bir_lowering=False)
    xcT = nc.dram_tensor("xcT", [4096, T], BF16, kind="ExternalInput").ap()
    dtT = nc.dram_tensor("dtT", [4096, T], F32, kind="ExternalInput").ap()
    BT = nc.dram_tensor("BT", [16, T], F32, kind="ExternalInput").ap()
    CTo = nc.dram_tensor("CTo", [16, T], F32, kind="ExternalInput").ap()
    zT = nc.dram_tensor("zT", [4096, T], BF16, kind="ExternalInput").ap()
    A_log = nc.dram_tensor("A_log", [4096, 16], F32, kind="ExternalInput").ap()
    D_skip = nc.dram_tensor("D_skip", [4096], F32, kind="ExternalInput").ap()
    yfT = nc.dram_tensor("yfT", [4096, T], BF16, kind="ExternalOutput").ap()

    NTC = 4
    TC = 1024
    ND = 32
    NS = 16

    es = contextlib.ExitStack()
    sb = lambda *a: es.enter_context(nc.sbuf_tensor(*a))
    ps = lambda *a: es.enter_context(nc.psum_tensor(*a))
    t_Araw = sb("Araw", [128, 512], F32)
    t_Ae = sb("Ae", [128, 512], F32)
    t_A = sb("Acol", [128, 512], F32)
    t_D = sb("Dcol", [128, 32], F32)
    t_ones = sb("ones1", [1, 128], F32)
    t_state = sb("state", [128, 512], F32)
    t_row = [sb(f"row{i}", [1, TC], F32) for i in range(2)]
    t_Bbc = sb("Bbc", [128, NS * TC], BF16)
    t_Cbc = sb("Cbc", [128, NS * TC], BF16)
    t_h = sb("hall", [128, NS * TC], BF16)
    t_dt = [sb(f"dtb{i}", [128, TC], F32) for i in range(2)]
    t_xc = [sb(f"xcb{i}", [128, TC], BF16) for i in range(2)]
    t_z = [sb(f"zb{i}", [128, TC], BF16) for i in range(2)]
    t_u = sb("u", [128, TC], BF16)
    t_dA = [sb(f"dA{i}", [128, TC], F32) for i in range(2)]
    t_uB = sb("uB", [128, TC], BF16)
    t_g = sb("g", [128, TC], BF16)
    t_y = [sb(f"y{i}", [128, TC], F32) for i in range(2)]
    t_sz = sb("sz", [128, TC], BF16)
    t_yf = [sb(f"yf{i}", [128, TC], BF16) for i in range(2)]
    p4 = [ps(f"pb{i}", [128, 1024], F32) for i in range(2)]

    def pslot(i):
        return p4[i // 2][:, (i % 2) * 512:(i % 2 + 1) * 512]

    names = ["sAC", "sDC", "sST", "sBRa", "sBRb", "sMMB", "sEVB", "sLa", "sLb",
             "sZa", "sZb", "svS", "sscS", "szS", "svY", "sYFa", "sYFb",
             "fV", "fS", "sAr"]
    sms = {n: es.enter_context(nc.semaphore(n)) for n in names}
    (sAC, sDC, sST, sBRa, sBRb, sMMB, sEVB, sLa, sLb, sZa, sZb, svS, sscS, szS,
     svY, sYFa, sYFb, fV, fS, sAr) = (sms[n] for n in names)
    sBR2 = [sBRa, sBRb]
    sL2 = [sLa, sLb]
    sZ2 = [sZa, sZb]
    sYF2 = [sYFa, sYFb]

    with es, nc.Block() as blk:
        @blk.gpsimd
        def _(g):
            g.memset(t_ones[:], 1.0)
            g.memset(t_state[:], 0.0)
            g.nop().then_inc(sST, 1)
            for i_ in range(32):
                g.dma_start(out=t_Araw[:, i_ * 16:(i_ + 1) * 16],
                            in_=A_log[ts(i_, 128), :]).then_inc(sAC, 16)
            for i_ in range(32):
                g.dma_start(out=t_D[:, i_:i_ + 1],
                            in_=D_skip[ts(i_, 128)].rearrange("(p o) -> p o", o=1)
                            ).then_inc(sDC, 16)
            for tc in range(NTC):
                for s_ in range(NS):
                    for bc in range(2):
                        j = (tc * NS + s_) * 2 + bc
                        if j >= 2:
                            g.wait_ge(sMMB, 2 * (j - 1))
                        srcrow = BT if bc == 0 else CTo
                        g.dma_start(out=t_row[j % 2][:],
                                    in_=srcrow[s_:s_ + 1, tc * TC:(tc + 1) * TC]
                                    ).then_inc(sBR2[j % 2], 16)

        @blk.sync
        def _(s):
            def store_yf(bj):
                tcj, dtj = divmod(bj, ND)
                s.wait_ge(svY, bj + 1)
                s.dma_start(out=yfT[ts(dtj, 128), tcj * TC:(tcj + 1) * TC],
                            in_=t_yf[bj % 2][:]).then_inc(sYF2[bj % 2], 16)

            for tc in range(NTC):
                for dt_ in range(ND):
                    bi = tc * ND + dt_
                    if bi >= 2:
                        s.wait_ge(svY, bi - 1)
                        s.wait_ge(sscS, 16 * (bi - 1))
                        s.wait_ge(szS, bi - 1)
                    s.dma_start(out=t_dt[bi % 2][:],
                                in_=dtT[ts(dt_, 128), tc * TC:(tc + 1) * TC]
                                ).then_inc(sL2[bi % 2], 16)
                    s.dma_start(out=t_xc[bi % 2][:],
                                in_=xcT[ts(dt_, 128), tc * TC:(tc + 1) * TC]
                                ).then_inc(sL2[bi % 2], 16)
                    s.dma_start(out=t_z[bi % 2][:],
                                in_=zT[ts(dt_, 128), tc * TC:(tc + 1) * TC]
                                ).then_inc(sZ2[bi % 2], 16)
                    if bi >= 1:
                        store_yf(bi - 1)
            store_yf(NTC * ND - 1)
            s.wait_ge(sYFa, 64 * 16)
            s.wait_ge(sYFb, 64 * 16)

        @blk.tensor
        def _(t):
            for tc in range(NTC):
                for s_ in range(NS):
                    for bc in range(2):
                        j = (tc * NS + s_) * 2 + bc
                        t.wait_ge(sBR2[j % 2], 16 * (j // 2 + 1))
                        for hh in range(2):
                            k = 2 * j + hh
                            if k >= 2:
                                t.wait_ge(sEVB, k - 1)
                            t.matmul(pslot(k % 2), t_ones[:],
                                     t_row[j % 2][0:1, hh * 512:(hh + 1) * 512],
                                     start=True, stop=True).then_inc(sMMB, 1)

        @blk.scalar
        def _(s):
            cS = Chain(s, fS)
            s.wait_ge(sAC, 32 * 16)
            cS.wait(); cS.op(s.activation(t_Ae[:], t_Araw[:], AF.Exp))
            cS.wait()
            s.mul(t_A[:], t_Ae[:], -1.0).then_inc(sAr, 1)
            s.wait_ge(sAr, 1)
            for tc in range(NTC):
                for dt_ in range(ND):
                    bi = tc * ND + dt_
                    s.wait_ge(sL2[bi % 2], 32 * (bi // 2 + 1))
                    for s_ in range(NS):
                        k = bi * 16 + s_
                        if s_ >= 2 or bi >= 1:
                            s.wait_ge(svS, k - 1)
                        s.activation(t_dA[s_ % 2][:], t_dt[bi % 2][:], AF.Exp,
                                     scale=t_A[:, dt_ * 16 + s_: dt_ * 16 + s_ + 1]
                                     ).then_inc(sscS, 1)
                    s.wait_ge(sZ2[bi % 2], 16 * (bi // 2 + 1))
                    if bi >= 1:
                        s.wait_ge(svY, bi)
                    s.activation(t_sz[:], t_z[bi % 2][:], AF.Silu).then_inc(szS, 1)

        @blk.vector
        def _(v):
            cV = Chain(v, fV)
            v.wait_ge(sDC, 32 * 16)
            v.wait_ge(sST, 1)
            for tc in range(NTC):
                for s_ in range(NS):
                    for bc in range(2):
                        dst = t_Bbc if bc == 0 else t_Cbc
                        for hh in range(2):
                            k = 2 * ((tc * NS + s_) * 2 + bc) + hh
                            v.wait_ge(sMMB, k + 1)
                            v.tensor_copy(
                                dst[:, s_ * TC + hh * 512: s_ * TC + (hh + 1) * 512],
                                pslot(k % 2)).then_inc(sEVB, 1)
                v.wait_ge(sEVB, 64 * (tc + 1))
                for dt_ in range(ND):
                    bi = tc * ND + dt_
                    v.wait_ge(sL2[bi % 2], 32 * (bi // 2 + 1))
                    cV.wait()
                    cV.op(v.tensor_tensor(t_u[:], t_dt[bi % 2][:], t_xc[bi % 2][:],
                                          ALU.mult))
                    for s_ in range(NS):
                        k = bi * 16 + s_
                        v.wait_ge(sscS, k + 1)
                        cV.wait()
                        cV.op(v.tensor_tensor(t_uB[:], t_u[:],
                                              t_Bbc[:, s_ * TC:(s_ + 1) * TC],
                                              ALU.mult))
                        cV.wait()
                        ins = v.tensor_tensor_scan(
                            t_h[:, s_ * TC:(s_ + 1) * TC], t_dA[s_ % 2][:],
                            t_uB[:], t_state[:, dt_ * 16 + s_: dt_ * 16 + s_ + 1],
                            ALU.mult, ALU.add)
                        ins.then_inc(svS, 1)
                        cV.n += 0
                        v.wait_ge(svS, k + 1)
                        if s_ == 0:
                            cV.op(v.tensor_tensor(t_y[0][:],
                                                  t_h[:, 0:TC],
                                                  t_Cbc[:, 0:TC], ALU.mult))
                        else:
                            cV.wait()
                            cV.op(v.tensor_tensor(t_g[:],
                                                  t_h[:, s_ * TC:(s_ + 1) * TC],
                                                  t_Cbc[:, s_ * TC:(s_ + 1) * TC],
                                                  ALU.mult))
                            cV.wait()
                            cV.op(v.tensor_tensor(t_y[s_ % 2][:], t_y[(s_ - 1) % 2][:],
                                                  t_g[:], ALU.mult if False else ALU.add))
                    cV.wait()
                    cV.op(v.tensor_copy(t_state[:, dt_ * 16:(dt_ + 1) * 16],
                                        t_h[:, TC - 1::TC]))
                    cV.wait()
                    cV.op(v.scalar_tensor_tensor(t_y[(NS - 1) % 2 ^ 1][:],
                                                 t_xc[bi % 2][:],
                                                 t_D[:, dt_:dt_ + 1],
                                                 t_y[(NS - 1) % 2][:],
                                                 ALU.mult, ALU.add))
                    v.wait_ge(szS, bi + 1)
                    if bi >= 2:
                        v.wait_ge(sYF2[bi % 2], 16 * (bi // 2))
                    cV.wait()
                    ins = v.tensor_tensor(t_yf[bi % 2][:], t_y[(NS - 1) % 2 ^ 1][:],
                                          t_sz[:], ALU.mult)
                    ins.then_inc(svY, 1)
                    v.wait_ge(svY, bi + 1)

    return nc


def build_l1out():
    nc = bass.Bass("TRN2", target_bir_lowering=False)
    yfT = nc.dram_tensor("yfT", [4096, T], BF16, kind="ExternalInput").ap()
    outprojT = nc.dram_tensor("outprojT", [4096, D], F32, kind="ExternalInput").ap()
    x2 = nc.dram_tensor("x2", [T, D], F32, kind="ExternalInput").ap()
    xmid = nc.dram_tensor("xmid", [T, D], F32, kind="ExternalOutput").ap()

    NTCH = 8
    TC = T // NTCH
    NSUB = TC // 128
    ET = D // 128
    NC_ = 32

    es = contextlib.ExitStack()
    sb = lambda *a: es.enter_context(nc.sbuf_tensor(*a))
    ps = lambda *a: es.enter_context(nc.psum_tensor(*a))
    t_id = sb("ident", [128, 128], F32)
    t_yf = sb("yfc", [128, NC_ * TC], BF16)
    t_W = [sb(f"W{i}", [128, NC_ * 128], BF16) for i in range(2)]
    t_mixT = sb("mixT", [128, ET * TC], F32)
    t_x2t = sb("x2t", [128, NSUB * D], F32)
    p4 = [ps(f"pb{i}", [128, 1024], F32) for i in range(2)]

    def pslot(i):
        return p4[i // 2][:, (i % 2) * 512:(i % 2 + 1) * 512]

    names = ["sYL", "sWa", "sWb", "sMM", "sEV", "sX2", "sTOr", "sTOc",
             "sRES", "sOUT", "sID", "fG"]
    sms = {n: es.enter_context(nc.semaphore(n)) for n in names}
    (sYL, sWa, sWb, sMM, sEV, sX2, sTOr, sTOc, sRES, sOUT, sID, fG) = (
        sms[n] for n in names)
    sW2 = [sWa, sWb]

    with es, nc.Block() as blk:
        @blk.gpsimd
        def _(g):
            make_identities(nc, g, [t_id[:]], fG)
            g.nop().then_inc(sID, 1)
            for tch in range(NTCH):
                for cout in range(ET):
                    wi = tch * ET + cout
                    if wi >= 2:
                        g.wait_ge(sMM, wi - 1)
                    g.dma_start(
                        out=t_W[wi % 2][:],
                        in_=outprojT.rearrange("(i p) n -> p i n", p=128)[
                            :, :, cout * 128:(cout + 1) * 128]
                    ).then_inc(sW2[wi % 2], 16)

        @blk.sync
        def _(s):
            for tch in range(NTCH):
                if tch >= 1:
                    s.wait_ge(sMM, ET * tch)
                for ct in range(NC_):
                    s.dma_start(out=t_yf[:, ct * TC:(ct + 1) * TC],
                                in_=yfT[ts(ct, 128), tch * TC:(tch + 1) * TC]
                                ).then_inc(sYL, 16)
                if tch >= 1:
                    s.wait_ge(sOUT, 16 * NSUB * tch)
                for tsub in range(NSUB):
                    s.dma_start(out=t_x2t[:, tsub * D:(tsub + 1) * D],
                                in_=x2[ts(tch * NSUB + tsub, 128), :]
                                ).then_inc(sX2, 16)
                for tsub in range(NSUB):
                    k = tch * NSUB + tsub
                    s.wait_ge(sTOc, 64 * (tch + 1))
                    s.dma_start(out=xmid[ts(k, 128), :],
                                in_=t_x2t[:, tsub * D:(tsub + 1) * D]
                                ).then_inc(sOUT, 16)
            s.wait_ge(sOUT, 16 * NTCH * NSUB)

        @blk.tensor
        def _(t):
            t.wait_ge(sID, 1)
            for tch in range(NTCH):
                t.wait_ge(sYL, 16 * NC_ * (tch + 1))
                for cout in range(ET):
                    wi = tch * ET + cout
                    t.wait_ge(sW2[wi % 2], 16 * (wi // 2 + 1))
                    if wi >= 2:
                        t.wait_ge(sEV, wi - 1)
                    for ct in range(NC_):
                        ins = t.matmul(pslot(wi % 2),
                                       t_W[wi % 2][:, ct * 128:(ct + 1) * 128],
                                       t_yf[:, ct * TC:(ct + 1) * TC],
                                       start=(ct == 0), stop=(ct == NC_ - 1))
                    ins.then_inc(sMM, 1)
                for j in range(ET * NSUB):
                    et, tsub = divmod(j, NSUB)
                    jj = tch * ET * NSUB + j
                    t.wait_ge(sEV, tch * ET + et + 1)
                    if jj >= 2:
                        t.wait_ge(sTOc, jj - 1)
                    t.transpose(pslot(2 + jj % 2)[:, 0:128],
                                t_mixT[:, et * TC + tsub * 128: et * TC + (tsub + 1) * 128],
                                t_id[:]).then_inc(sTOr, 1)

        @blk.vector
        def _(v):
            for tch in range(NTCH):
                for cout in range(ET):
                    wi = tch * ET + cout
                    v.wait_ge(sMM, wi + 1)
                    v.tensor_copy(t_mixT[:, cout * TC:(cout + 1) * TC],
                                  pslot(wi % 2)).then_inc(sEV, 1)
                v.wait_ge(sX2, 16 * NSUB * (tch + 1))
                for j in range(ET * NSUB):
                    et, tsub = divmod(j, NSUB)
                    jj = tch * ET * NSUB + j
                    v.wait_ge(sTOr, jj + 1)
                    v.scalar_tensor_tensor(
                        t_x2t[:, tsub * D + et * 128: tsub * D + (et + 1) * 128],
                        t_x2t[:, tsub * D + et * 128: tsub * D + (et + 1) * 128],
                        1.0, pslot(2 + jj % 2)[:, 0:128],
                        ALU.mult, ALU.add).then_inc(sTOc, 1)

    return nc


def build_l1ffA():
    nc = bass.Bass("TRN2", target_bir_lowering=False)
    xmid = nc.dram_tensor("xmid", [T, D], F32, kind="ExternalInput").ap()
    ffprojT = nc.dram_tensor("ffprojT", [D, 8192], F32, kind="ExternalInput").ap()
    ln2w = nc.dram_tensor("ln2w", [D], F32, kind="ExternalInput").ap()
    ln2b = nc.dram_tensor("ln2b", [D], F32, kind="ExternalInput").ap()
    hsT = nc.dram_tensor("hsT", [4096, T], BF16, kind="ExternalOutput").ap()

    CT = D // 128
    NTT = T // 128
    NPG = 32
    NTCH = 8
    TC = T // NTCH

    es = contextlib.ExitStack()
    sb = lambda *a: es.enter_context(nc.sbuf_tensor(*a))
    ps = lambda *a: es.enter_context(nc.psum_tensor(*a))
    t_id = sb("ident", [128, 128], F32)
    t_x = [sb(f"xb{i}", [128, D], F32) for i in range(2)]
    t_u0 = sb("u0b", [128, D], F32)
    t_uT = sb("uT", [128, CT * T], BF16)
    t_stats = sb("stats", [128, 24], F32)
    t_mv = sb("mv", [128, 2], F32)
    t_lnv = sb("lnv", [128, 1], F32)
    t_rstd = sb("rstd", [128, 1], F32)
    t_eps = sb("eps", [128, 1], F32)
    t_lnw = sb("lnw", [128, CT], F32)
    t_lnb = sb("lnb", [128, CT], F32)
    t_Wa = [sb(f"Wa{i}", [128, 16 * 128], BF16) for i in range(2)]
    t_Wg = [sb(f"Wg{i}", [128, 16 * 128], BF16) for i in range(2)]
    t_sg = sb("sg", [128, TC], BF16)
    t_hs = [sb(f"hs{i}", [128, TC], BF16) for i in range(2)]
    p4 = [ps(f"pb{i}", [128, 1024], F32) for i in range(2)]

    def pslot(i):
        return p4[i // 2][:, (i % 2) * 512:(i % 2 + 1) * 512]

    names = ["sX", "svA", "sscA", "sAP", "sTIr", "sTIe", "sID", "sMMb", "sEVb",
             "sscB", "fV", "fS", "fG", "sXa", "sXb", "sWa_", "sWb_", "sHa", "sHb"]
    sms = {n: es.enter_context(nc.semaphore(n)) for n in names}
    (sX, svA, sscA, sAP, sTIr, sTIe, sID, sMMb, sEVb, sscB, fV, fS, fG,
     sXa, sXb, sWa_, sWb_, sHa, sHb) = (sms[n] for n in names)
    sX2 = [sXa, sXb]
    sW2 = [sWa_, sWb_]
    sH2 = [sHa, sHb]

    with es, nc.Block() as blk:
        @blk.gpsimd
        def _(g):
            g.memset(t_eps[:], EPS)
            make_identities(nc, g, [t_id[:]], fG)
            g.nop().then_inc(sID, 1)
            for pg in range(NPG):
                if pg >= 2:
                    g.wait_ge(sMMb, 2 * NTCH * (pg - 1))
                g.dma_start(
                    out=t_Wa[pg % 2][:],
                    in_=ffprojT.rearrange("(c p) n -> p c n", p=128)[
                        :, :, pg * 128:(pg + 1) * 128]).then_inc(sW2[pg % 2], 16)
                g.dma_start(
                    out=t_Wg[pg % 2][:],
                    in_=ffprojT.rearrange("(c p) n -> p c n", p=128)[
                        :, :, (pg + 32) * 128:(pg + 33) * 128]
                    ).then_inc(sW2[pg % 2], 16)

        @blk.sync
        def _(s):
            for c in range(CT):
                s.dma_start(out=t_lnw[:, c:c + 1],
                            in_=ln2w[ts(c, 128)].rearrange("(p o) -> p o", o=1)).then_inc(sX, 16)
                s.dma_start(out=t_lnb[:, c:c + 1],
                            in_=ln2b[ts(c, 128)].rearrange("(p o) -> p o", o=1)).then_inc(sX, 16)
            for i in range(NTT):
                if i >= 2:
                    s.wait_ge(sAP, i - 1)
                s.dma_start(out=t_x[i % 2][:],
                            in_=xmid[ts(i, 128), :]).then_inc(sX2[i % 2], 16)
            for pg in range(NPG):
                for tch in range(NTCH):
                    q = pg * NTCH + tch
                    s.wait_ge(sscB, q + 1)
                    s.dma_start(out=hsT[ts(pg, 128), tch * TC:(tch + 1) * TC],
                                in_=t_hs[q % 2][:]).then_inc(sH2[q % 2], 16)
            s.wait_ge(sHa, NPG * NTCH // 2 * 16)
            s.wait_ge(sHb, NPG * NTCH // 2 * 16)

        @blk.tensor
        def _(t):
            t.wait_ge(sID, 1)
            for i in range(NTT):
                t.wait_ge(sAP, i + 1)
                for c in range(CT):
                    j = i * CT + c
                    if j >= 2:
                        t.wait_ge(sTIe, j - 1)
                    t.transpose(p4[0][:, (j % 2) * 512:(j % 2) * 512 + 128],
                                t_u0[:, c * 128:(c + 1) * 128],
                                t_id[:]).then_inc(sTIr, 1)
            t.wait_ge(sTIe, NTT * CT)
            for pg in range(NPG):
                t.wait_ge(sW2[pg % 2], 16 * 2 * (pg // 2 + 1))
                for tch in range(NTCH):
                    gi = pg * NTCH + tch
                    pp = gi % 2
                    if gi >= 2:
                        t.wait_ge(sEVb, gi - 1)
                        t.wait_ge(sscB, gi - 1)
                    for c in range(CT):
                        insa = t.matmul(pslot(2 * pp),
                                        t_Wa[pg % 2][:, c * 128:(c + 1) * 128],
                                        t_uT[:, c * T + tch * TC: c * T + (tch + 1) * TC],
                                        start=(c == 0), stop=(c == CT - 1))
                        ins = t.matmul(pslot(2 * pp + 1),
                                       t_Wg[pg % 2][:, c * 128:(c + 1) * 128],
                                       t_uT[:, c * T + tch * TC: c * T + (tch + 1) * TC],
                                       start=(c == 0), stop=(c == CT - 1))
                        if c == CT - 1:
                            insa.then_inc(sMMb, 1)
                            ins.then_inc(sMMb, 1)

        @blk.scalar
        def _(s):
            s.wait_ge(sID, 1)
            cS = Chain(s, fS)
            for i in range(NTT):
                s.wait_ge(svA, i + 1)
                cS.wait(); cS.op(s.activation(t_lnv[:], t_mv[:, 1:2], AF.Ln, bias=t_eps[:]))
                cS.wait()
                s.activation(t_rstd[:], t_lnv[:], AF.Exp, scale=-0.5).then_inc(sscA, 1)
                s.wait_ge(sscA, i + 1)
            for pg in range(NPG):
                for tch in range(NTCH):
                    gi = pg * NTCH + tch
                    s.wait_ge(sMMb, 2 * gi + 2)
                    if gi >= 1:
                        s.wait_ge(sscB, gi)
                    s.activation(t_sg[:], pslot(2 * (gi % 2) + 1),
                                 AF.Silu).then_inc(sEVb, 1)

        @blk.vector
        def _(v):
            v.wait_ge(sX, 32 * 16)
            cV = Chain(v, fV)
            for i in range(NTT):
                v.wait_ge(sX2[i % 2], (i // 2 + 1) * 16)
                v.wait_ge(svA, i)
                for gix in range(4):
                    cV.op(v.bn_stats(t_stats[:, gix * 6:(gix + 1) * 6],
                                     t_x[i % 2][:, gix * 512:(gix + 1) * 512]))
                cV.wait()
                v.bn_aggr(t_mv[:], t_stats[:]).then_inc(svA, 1)
                v.wait_ge(sscA, i + 1)
                if i >= 1:
                    v.wait_ge(sTIr, i * CT)
                v.wait_ge(fV, cV.n)
                v.tensor_scalar(t_u0[:], t_x[i % 2][:], t_mv[:, 0:1],
                                t_rstd[:], ALU.subtract, ALU.mult).then_inc(sAP, 1)
                for c in range(CT):
                    j = i * CT + c
                    v.wait_ge(sTIr, j + 1)
                    v.tensor_scalar(t_uT[:, c * T + i * 128: c * T + (i + 1) * 128],
                                    p4[0][:, (j % 2) * 512:(j % 2) * 512 + 128],
                                    t_lnw[:, c:c + 1], t_lnb[:, c:c + 1],
                                    ALU.mult, ALU.add).then_inc(sTIe, 1)
            for pg in range(NPG):
                for tch in range(NTCH):
                    gi = pg * NTCH + tch
                    q = gi
                    v.wait_ge(sEVb, gi + 1)
                    if q >= 2:
                        v.wait_ge(sH2[q % 2], 16 * (q // 2))
                    v.scalar_tensor_tensor(t_hs[q % 2][:], pslot(2 * (gi % 2)),
                                           1.0, t_sg[:], ALU.mult,
                                           ALU.mult).then_inc(sscB, 1)

    return nc


def build_l1ffB(stage=3):
    nc = bass.Bass("TRN2", target_bir_lowering=False)
    hsT = nc.dram_tensor("hsT", [4096, T], BF16, kind="ExternalInput").ap()
    ffoutT = nc.dram_tensor("ffoutT", [4096, D], F32, kind="ExternalInput").ap()
    x2 = nc.dram_tensor("x2", [T, D], F32, kind="ExternalInput").ap()
    xmid = nc.dram_tensor("xmid", [T, D], F32, kind="ExternalInput").ap()
    x3 = nc.dram_tensor("x3", [T, D], F32, kind="ExternalOutput").ap()

    NTCH = 8
    TC = T // NTCH
    NSUB = TC // 128
    ET = D // 128
    NC_ = 32

    es = contextlib.ExitStack()
    sb = lambda *a: es.enter_context(nc.sbuf_tensor(*a))
    ps = lambda *a: es.enter_context(nc.psum_tensor(*a))
    t_id = sb("ident", [128, 128], F32)
    t_hs = sb("hsc", [128, NC_ * TC], BF16)
    t_W = [sb(f"W{i}", [128, NC_ * 128], BF16) for i in range(2)]
    t_mixT = sb("mixT", [128, ET * TC], F32)
    t_x2t = sb("x2t", [128, NSUB * D], F32)
    t_xmt = sb("xmt", [128, NSUB * D], F32)
    t_s = sb("s_res", [128, NSUB * D], F32)
    t_stats = sb("stats", [128, 4 * 24], F32)
    t_mv = sb("mv", [128, 4 * 2], F32)
    t_lnv = sb("lnv", [128, 4], F32)
    t_rstd = sb("rstd", [128, 4], F32)
    t_eps = sb("eps", [128, 1], F32)
    p4 = [ps(f"pb{i}", [128, 1024], F32) for i in range(2)]

    def pslot(i):
        return p4[i // 2][:, (i % 2) * 512:(i % 2 + 1) * 512]

    names = ["sYL", "sWa", "sWb", "sMM", "sEV", "sX2", "sTOr", "sTOc",
             "sv", "ssc", "sLN", "sOUT", "sID", "fG", "fV", "fS"]
    sms = {n: es.enter_context(nc.semaphore(n)) for n in names}
    (sYL, sWa, sWb, sMM, sEV, sX2, sTOr, sTOc, sv, ssc, sLN, sOUT, sID,
     fG, fV, fS) = (sms[n] for n in names)
    sW2 = [sWa, sWb]

    with es, nc.Block() as blk:
        @blk.gpsimd
        def _(g):
            g.memset(t_eps[:], EPS)
            make_identities(nc, g, [t_id[:]], fG)
            g.nop().then_inc(sID, 1)
            for tch in range(NTCH):
                for cout in range(ET):
                    wi = tch * ET + cout
                    if wi >= 2:
                        g.wait_ge(sMM, wi - 1)
                    g.dma_start(
                        out=t_W[wi % 2][:],
                        in_=ffoutT.rearrange("(i p) n -> p i n", p=128)[
                            :, :, cout * 128:(cout + 1) * 128]
                    ).then_inc(sW2[wi % 2], 16)

        @blk.sync
        def _(s):
            for tch in range(NTCH):
                if tch >= 1:
                    s.wait_ge(sMM, ET * tch)
                for ct in range(NC_):
                    s.dma_start(out=t_hs[:, ct * TC:(ct + 1) * TC],
                                in_=hsT[ts(ct, 128), tch * TC:(tch + 1) * TC]
                                ).then_inc(sYL, 16)
                if tch >= 1:
                    s.wait_ge(sLN, NSUB * tch)
                    s.wait_ge(sOUT, 16 * NSUB * tch)
                for tsub in range(NSUB):
                    s.dma_start(out=t_x2t[:, tsub * D:(tsub + 1) * D],
                                in_=x2[ts(tch * NSUB + tsub, 128), :]
                                ).then_inc(sX2, 16)
                    if stage >= 2:
                        s.dma_start(out=t_xmt[:, tsub * D:(tsub + 1) * D],
                                    in_=xmid[ts(tch * NSUB + tsub, 128), :]
                                    ).then_inc(sX2, 16)
                for tsub in range(NSUB):
                    k = tch * NSUB + tsub
                    s.wait_ge(sLN, k + 1)
                    s.dma_start(out=x3[ts(k, 128), :],
                                in_=t_x2t[:, tsub * D:(tsub + 1) * D]
                                ).then_inc(sOUT, 16)
            s.wait_ge(sOUT, 16 * NTCH * NSUB)

        @blk.tensor
        def _(t):
            t.wait_ge(sID, 1)
            for tch in range(NTCH):
                t.wait_ge(sYL, 16 * NC_ * (tch + 1))
                for cout in range(ET):
                    wi = tch * ET + cout
                    t.wait_ge(sW2[wi % 2], 16 * (wi // 2 + 1))
                    if wi >= 2:
                        t.wait_ge(sEV, wi - 1)
                    for ct in range(NC_):
                        ins = t.matmul(pslot(wi % 2),
                                       t_W[wi % 2][:, ct * 128:(ct + 1) * 128],
                                       t_hs[:, ct * TC:(ct + 1) * TC],
                                       start=(ct == 0), stop=(ct == NC_ - 1))
                    ins.then_inc(sMM, 1)
                if stage >= 1:
                    for j in range(ET * NSUB):
                        et, tsub = divmod(j, NSUB)
                        jj = tch * ET * NSUB + j
                        t.wait_ge(sEV, tch * ET + et + 1)
                        if jj >= 2:
                            t.wait_ge(sTOc, jj - 1)
                        t.transpose(pslot(2 + jj % 2)[:, 0:128],
                                    t_mixT[:, et * TC + tsub * 128: et * TC + (tsub + 1) * 128],
                                    t_id[:]).then_inc(sTOr, 1)

        @blk.scalar
        def _(s):
            s.wait_ge(sID, 1)
            cS = Chain(s, fS)
            for tch in range(NTCH):
                if stage < 3:
                    continue
                s.wait_ge(sv, tch + 1)
                for tsub in range(NSUB):
                    cS.wait()
                    cS.op(s.activation(t_lnv[:, tsub:tsub + 1],
                                       t_mv[:, 2 * tsub + 1:2 * tsub + 2],
                                       AF.Ln, bias=t_eps[:]))
                cS.wait()
                for tsub in range(NSUB):
                    ins = s.activation(t_rstd[:, tsub:tsub + 1],
                                       t_lnv[:, tsub:tsub + 1], AF.Exp, scale=-0.5)
                    if tsub == NSUB - 1:
                        ins.then_inc(ssc, 1)
                s.wait_ge(ssc, tch + 1)

        @blk.vector
        def _(v):
            cV = Chain(v, fV)
            for tch in range(NTCH):
                for cout in range(ET):
                    wi = tch * ET + cout
                    v.wait_ge(sMM, wi + 1)
                    v.tensor_copy(t_mixT[:, cout * TC:(cout + 1) * TC],
                                  pslot(wi % 2)).then_inc(sEV, 1)
                v.wait_ge(sX2, (2 if stage >= 2 else 1) * 16 * NSUB * (tch + 1))
                if stage < 2:
                    v.wait_ge(sEV, (tch + 1) * ET)
                    if stage >= 1:
                        for j in range(ET * NSUB):
                            v.wait_ge(sTOr, tch * ET * NSUB + j + 1)
                            v.engine_nop().then_inc(sTOc, 1)
                    for tsub in range(NSUB):
                        v.engine_nop().then_inc(sLN, 1)
                    continue
                for tsub in range(NSUB):
                    cV.op(v.tensor_tensor(t_xmt[:, tsub * D:(tsub + 1) * D],
                                          t_x2t[:, tsub * D:(tsub + 1) * D],
                                          t_xmt[:, tsub * D:(tsub + 1) * D],
                                          ALU.add))
                cV.wait()
                for j in range(ET * NSUB):
                    et, tsub = divmod(j, NSUB)
                    jj = tch * ET * NSUB + j
                    v.wait_ge(sTOr, jj + 1)
                    v.scalar_tensor_tensor(
                        t_s[:, tsub * D + et * 128: tsub * D + (et + 1) * 128],
                        t_xmt[:, tsub * D + et * 128: tsub * D + (et + 1) * 128],
                        1.0, pslot(2 + jj % 2)[:, 0:128],
                        ALU.mult, ALU.add).then_inc(sTOc, 1)
                v.wait_ge(sTOc, (tch + 1) * ET * NSUB)
                if stage < 3:
                    for tsub in range(NSUB):
                        v.engine_nop().then_inc(sLN, 1)
                    continue
                for tsub in range(NSUB):
                    for gix in range(4):
                        ins = v.bn_stats(
                            t_stats[:, tsub * 24 + gix * 6: tsub * 24 + gix * 6 + 6],
                            t_s[:, tsub * D + gix * 512: tsub * D + (gix + 1) * 512])
                        cV.op(ins)
                cV.wait()
                for tsub in range(NSUB):
                    ins_ = v.bn_aggr(t_mv[:, 2 * tsub:2 * tsub + 2],
                                     t_stats[:, tsub * 24:(tsub + 1) * 24])
                    if tsub == NSUB - 1:
                        ins_.then_inc(sv, 1)
                v.wait_ge(ssc, tch + 1)
                for tsub in range(NSUB):
                    v.tensor_scalar(t_x2t[:, tsub * D:(tsub + 1) * D],
                                    t_s[:, tsub * D:(tsub + 1) * D],
                                    t_mv[:, 2 * tsub:2 * tsub + 1],
                                    t_rstd[:, tsub:tsub + 1],
                                    ALU.subtract, ALU.mult).then_inc(sLN, 1)

    return nc


def build_l0ffnA():
    nc = bass.Bass("TRN2", target_bir_lowering=False)
    x1 = nc.dram_tensor("x1", [T, D], F32, kind="ExternalInput").ap()
    routerT = nc.dram_tensor("routerT", [D, 8], F32, kind="ExternalInput").ap()
    sgT = nc.dram_tensor("sgT", [D, 1408], F32, kind="ExternalInput").ap()
    suT = nc.dram_tensor("suT", [D, 1408], F32, kind="ExternalInput").ap()
    wgT = nc.dram_tensor("wgT", [8, D, 1408], F32, kind="ExternalInput").ap()
    wuT = nc.dram_tensor("wuT", [8, D, 1408], F32, kind="ExternalInput").ap()
    hsT_all = nc.dram_tensor("hsT_all", [9 * 1408, T], BF16, kind="ExternalOutput").ap()
    comb_d = nc.dram_tensor("comb_d", [8, T], F32, kind="Internal").ap()

    CT = D // 128
    NTT = T // 128
    NIO = 11
    NTCH = 8
    TC = T // NTCH

    es = contextlib.ExitStack()
    sb = lambda *a: es.enter_context(nc.sbuf_tensor(*a))
    ps = lambda *a: es.enter_context(nc.psum_tensor(*a))
    t_id = sb("ident", [128, 128], F32)
    t_x = [sb(f"xb{i}", [128, D], F32) for i in range(2)]
    t_xT = sb("xT", [128, CT * T], BF16)
    t_rc = sb("rc", [128, CT * 8], BF16)
    t_w8 = sb("w8", [128, 8], F32)
    t_exp = sb("exp8", [128, 8], F32)
    t_rden = sb("rden", [128, 1], F32)
    t_wsm = sb("wsm", [128, 8], F32)
    t_mask = sb("mask8", [128, 8], F32)
    t_w2 = sb("w28", [128, 8], F32)
    t_keep = sb("keep8", [128, 8], F32)
    t_comb = sb("comb8", [128, 8], F32)
    t_nm = sb("nm", [128, 1], F32)
    t_den = sb("den", [128, 1], F32)
    t_m1 = sb("m1", [128, 1], F32)
    t_m2 = sb("m2", [128, 1], F32)
    t_ctT = sb("ctT", [128, T], F32)
    t_crow = [sb(f"crow{i}", [1, TC], F32) for i in range(2)]
    t_cbc = sb("cbc", [128, T], BF16)
    t_ones = sb("ones1", [1, 128], F32)
    t_Wg = [sb(f"Wg{i}", [128, 16 * 128], BF16) for i in range(2)]
    t_Wu = [sb(f"Wu{i}", [128, 16 * 128], BF16) for i in range(2)]
    t_sg = sb("sg", [128, TC], BF16)
    t_ue = sb("ue", [128, TC], BF16)
    t_hs = [sb(f"hs{i}", [128, TC], BF16) for i in range(2)]
    p4 = [ps(f"pb{i}", [128, 1024], F32) for i in range(3)]

    def pslot(i):
        return p4[i // 2][:, (i % 2) * 512:(i % 2 + 1) * 512]

    names = ["sXa", "sXb", "sTIr", "sTIe", "sID", "sRC", "sRM", "svR", "sscR",
             "sCT", "sEVC", "sCD", "sCRa", "sCRb", "sBC", "sEVC2",
             "sWa_", "sWb_", "sMMb", "sEVg", "sUE", "sscB", "sHa", "sHb",
             "fV", "fS", "fG"]
    sms = {n: es.enter_context(nc.semaphore(n)) for n in names}
    (sXa, sXb, sTIr, sTIe, sID, sRC, sRM, svR, sscR, sCT, sEVC, sCD,
     sCRa, sCRb, sBC, sEVC2, sWa_, sWb_, sMMb, sEVg, sUE, sscB, sHa, sHb,
     fV, fS, fG) = (sms[n] for n in names)
    sX2 = [sXa, sXb]
    sW2 = [sWa_, sWb_]
    sH2 = [sHa, sHb]
    sCR2 = [sCRa, sCRb]

    with es, nc.Block() as blk:
        @blk.gpsimd
        def _(g):
            g.memset(t_ones[:], 1.0)
            make_identities(nc, g, [t_id[:]], fG)
            g.nop().then_inc(sID, 1)
            g.dma_start(out=t_rc[:],
                        in_=routerT.rearrange("(c p) n -> p c n", p=128)).then_inc(sRC, 16)
            for src in range(9):
                for io in range(NIO):
                    wi = src * NIO + io
                    if wi >= 2:
                        g.wait_ge(sMMb, 2 * NTCH * (wi - 1))
                    if src == 0:
                        gsrc = sgT.rearrange("(c p) n -> p c n", p=128)[
                            :, :, io * 128:(io + 1) * 128]
                        usrc = suT.rearrange("(c p) n -> p c n", p=128)[
                            :, :, io * 128:(io + 1) * 128]
                    else:
                        gsrc = wgT[src - 1].rearrange("(c p) n -> p c n", p=128)[
                            :, :, io * 128:(io + 1) * 128]
                        usrc = wuT[src - 1].rearrange("(c p) n -> p c n", p=128)[
                            :, :, io * 128:(io + 1) * 128]
                    g.dma_start(out=t_Wg[wi % 2][:], in_=gsrc).then_inc(sW2[wi % 2], 16)
                    g.dma_start(out=t_Wu[wi % 2][:], in_=usrc).then_inc(sW2[wi % 2], 16)

        @blk.sync
        def _(s):
            for i in range(NTT):
                if i >= 2:
                    s.wait_ge(sTIr, (i - 1) * CT)
                s.dma_start(out=t_x[i % 2][:],
                            in_=x1[ts(i, 128), :]).then_inc(sX2[i % 2], 16)
            s.wait_ge(sEVC, NTT)
            s.dma_start(out=comb_d, in_=t_ctT[0:8, :]).then_inc(sCD, 16)
            for src in range(9):
                if src >= 1:
                    for tch in range(NTCH):
                        ri = (src - 1) * NTCH + tch
                        if ri == 0:
                            s.wait_ge(sCD, 16)
                        if ri >= 2:
                            s.wait_ge(sBC, ri - 1)
                        s.dma_start(out=t_crow[ri % 2][:],
                                    in_=comb_d[src - 1:src, tch * TC:(tch + 1) * TC]
                                    ).then_inc(sCR2[ri % 2], 16)
                for io in range(NIO):
                    for tch in range(NTCH):
                        q = (src * NIO + io) * NTCH + tch
                        s.wait_ge(sscB, q + 1)
                        s.dma_start(
                            out=hsT_all[ts(src * NIO + io, 128),
                                        tch * TC:(tch + 1) * TC],
                            in_=t_hs[q % 2][:]).then_inc(sH2[q % 2], 16)
            s.wait_ge(sHa, 9 * NIO * NTCH // 2 * 16)
            s.wait_ge(sHb, 9 * NIO * NTCH // 2 * 16)

        @blk.tensor
        def _(t):
            t.wait_ge(sID, 1)
            for i in range(NTT):
                t.wait_ge(sX2[i % 2], 16 * (i // 2 + 1))
                for c in range(CT):
                    j = i * CT + c
                    if j >= 2:
                        t.wait_ge(sTIe, j - 1)
                    t.transpose(p4[0][:, (j % 2) * 512:(j % 2) * 512 + 128],
                                t_x[i % 2][:, c * 128:(c + 1) * 128],
                                t_id[:]).then_inc(sTIr, 1)
            t.wait_ge(sTIe, NTT * CT)
            t.wait_ge(sRC, 16)
            for i in range(NTT):
                if i >= 1:
                    t.wait_ge(svR, 2 * i)
                for c in range(CT):
                    ins = t.matmul(pslot(0)[:, 0:8],
                                   t_xT[:, c * T + i * 128: c * T + (i + 1) * 128],
                                   t_rc[:, c * 8:(c + 1) * 8],
                                   start=(c == 0), stop=(c == CT - 1))
                    if c == CT - 1:
                        ins.then_inc(sRM, 1)
                t.wait_ge(svR, 2 * i + 2)
                if i >= 1:
                    t.wait_ge(sEVC, i)
                t.transpose(pslot(2)[0:8, 0:128], t_comb[:],
                            t_id[:]).then_inc(sCT, 1)
            t.wait_ge(sEVC, NTT)
            for src in range(9):
                if src >= 1:
                    for tch in range(NTCH):
                        ri = (src - 1) * NTCH + tch
                        t.wait_ge(sCR2[ri % 2], 16 * (ri // 2 + 1))
                        if ri >= 2:
                            t.wait_ge(sEVC2, ri - 1)
                        t.matmul(pslot(4 + ri % 2), t_ones[:],
                                 t_crow[ri % 2][0:1, :],
                                 start=True, stop=True).then_inc(sBC, 1)
                for io in range(NIO):
                    wi = src * NIO + io
                    t.wait_ge(sW2[wi % 2], 2 * 16 * (wi // 2 + 1))
                    for tch in range(NTCH):
                        gi = wi * NTCH + tch
                        pp = gi % 2
                        if gi >= 2:
                            t.wait_ge(sEVg, gi - 1)
                            t.wait_ge(sUE, gi - 1)
                        for c in range(CT):
                            insg = t.matmul(
                                pslot(2 * pp), t_Wg[wi % 2][:, c * 128:(c + 1) * 128],
                                t_xT[:, c * T + tch * TC: c * T + (tch + 1) * TC],
                                start=(c == 0), stop=(c == CT - 1))
                            insu = t.matmul(
                                pslot(2 * pp + 1), t_Wu[wi % 2][:, c * 128:(c + 1) * 128],
                                t_xT[:, c * T + tch * TC: c * T + (tch + 1) * TC],
                                start=(c == 0), stop=(c == CT - 1))
                            if c == CT - 1:
                                insg.then_inc(sMMb, 1)
                                insu.then_inc(sMMb, 1)

        @blk.scalar
        def _(s):
            s.wait_ge(sID, 1)
            for i in range(NTT):
                s.wait_ge(svR, 2 * i + 1)
                s.activation(t_exp[:], t_w8[:], AF.Exp, bias=t_nm[:],
                             accum_out=t_den[:]).then_inc(sscR, 1)
                s.wait_ge(sscR, i + 1)
            for src in range(9):
                for io in range(NIO):
                    for tch in range(NTCH):
                        gi = (src * NIO + io) * NTCH + tch
                        s.wait_ge(sMMb, 2 * gi + 2)
                        if gi >= 1:
                            s.wait_ge(sscB, gi)
                        s.activation(t_sg[:], pslot(2 * (gi % 2)),
                                     AF.Sigmoid).then_inc(sEVg, 1)

        @blk.vector
        def _(v):
            cV = Chain(v, fV)
            for i in range(NTT):
                for c in range(CT):
                    j = i * CT + c
                    v.wait_ge(sTIr, j + 1)
                    v.tensor_copy(t_xT[:, c * T + i * 128: c * T + (i + 1) * 128],
                                  p4[0][:, (j % 2) * 512:(j % 2) * 512 + 128]
                                  ).then_inc(sTIe, 1)
            for i in range(NTT):
                v.wait_ge(sRM, i + 1)
                cV.wait()
                cV.op(v.tensor_copy(t_w8[:], pslot(0)[:, 0:8]))
                cV.wait()
                ins = v.reduce_max(t_nm[:], t_w8[:], axis=AX.X, negate=True)
                ins.then_inc(svR, 1)
                v.wait_ge(sscR, i + 1)
                cV.wait()
                cV.op(v.reciprocal(t_rden[:], t_den[:]))
                cV.wait()
                cV.op(v.tensor_scalar(t_wsm[:], t_exp[:], t_rden[:], None, ALU.mult))
                cV.wait()
                cV.op(v.reduce_max(t_m1[:], t_wsm[:], axis=AX.X))
                cV.wait()
                cV.op(v.tensor_scalar(t_mask[:], t_wsm[:], t_m1[:], None, ALU.is_ge))
                cV.wait()
                cV.op(v.scalar_tensor_tensor(t_w2[:], t_mask[:], -1e9, t_wsm[:],
                                             ALU.mult, ALU.add))
                cV.wait()
                cV.op(v.reduce_max(t_m2[:], t_w2[:], axis=AX.X))
                cV.wait()
                cV.op(v.tensor_scalar(t_keep[:], t_wsm[:], t_m2[:], None, ALU.is_ge))
                cV.wait()
                ins = v.tensor_tensor(t_comb[:], t_wsm[:], t_keep[:], ALU.mult)
                ins.then_inc(svR, 1)
                v.wait_ge(sCT, i + 1)
                v.tensor_copy(t_ctT[0:8, i * 128:(i + 1) * 128],
                              pslot(2)[0:8, 0:128]).then_inc(sEVC, 1)
            for src in range(9):
                if src >= 1:
                    for tch in range(NTCH):
                        ri = (src - 1) * NTCH + tch
                        v.wait_ge(sBC, ri + 1)
                        v.tensor_copy(t_cbc[:, tch * TC:(tch + 1) * TC],
                                      pslot(4 + ri % 2)).then_inc(sEVC2, 1)
                for io in range(NIO):
                    for tch in range(NTCH):
                        gi = (src * NIO + io) * NTCH + tch
                        q = gi
                        v.wait_ge(sMMb, 2 * gi + 2)
                        cV.wait()
                        if src == 0:
                            ins = v.tensor_copy(t_ue[:], pslot(2 * (gi % 2) + 1))
                        else:
                            ins = v.scalar_tensor_tensor(
                                t_ue[:], pslot(2 * (gi % 2) + 1), 1.0,
                                t_cbc[:, tch * TC:(tch + 1) * TC],
                                ALU.mult, ALU.mult)
                        ins.then_inc(sUE, 1)
                        cV.n += 0
                        v.wait_ge(sUE, gi + 1)
                        v.wait_ge(sEVg, gi + 1)
                        if q >= 2:
                            v.wait_ge(sH2[q % 2], 16 * (q // 2))
                        cV.wait()
                        ins = v.tensor_tensor(t_hs[q % 2][:], t_sg[:], t_ue[:],
                                              ALU.mult)
                        ins.then_inc(sscB, 1)
                        v.wait_ge(sscB, gi + 1)

    return nc


def build_l0ffnB():
    nc = bass.Bass("TRN2", target_bir_lowering=False)
    hsT_all = nc.dram_tensor("hsT_all", [9 * 1408, T], BF16, kind="ExternalInput").ap()
    wdT_all = nc.dram_tensor("wdT_all", [9 * 1408, D], F32, kind="ExternalInput").ap()
    x1 = nc.dram_tensor("x1", [T, D], F32, kind="ExternalInput").ap()
    x2 = nc.dram_tensor("x2", [T, D], F32, kind="ExternalOutput").ap()

    NTCH = 8
    TC = T // NTCH
    NSUB = TC // 128
    ET = D // 128
    NC_ = 99

    es = contextlib.ExitStack()
    sb = lambda *a: es.enter_context(nc.sbuf_tensor(*a))
    ps = lambda *a: es.enter_context(nc.psum_tensor(*a))
    t_id = sb("ident", [128, 128], F32)
    t_hs = sb("hsc", [128, NC_ * TC], BF16)
    t_W = [sb("W0", [128, NC_ * 128], BF16)]
    t_mixT = sb("mixT", [128, ET * TC], BF16)
    t_x1t = sb("x1t", [128, NSUB * D], F32)
    t_s = sb("s_res", [128, NSUB * D], F32)
    t_stats = sb("stats", [128, 4 * 24], F32)
    t_mv = sb("mv", [128, 4 * 2], F32)
    t_lnv = sb("lnv", [128, 4], F32)
    t_rstd = sb("rstd", [128, 4], F32)
    t_eps = sb("eps", [128, 1], F32)
    t_idb = sb("identb", [128, 128], BF16)
    p4 = [ps(f"pb{i}", [128, 1024], F32) for i in range(2)]
    p_tb = [ps(f"ptb{i}", [128, 1024], BF16) for i in range(2)]

    def pslot(i):
        return p4[i // 2][:, (i % 2) * 512:(i % 2 + 1) * 512]

    names = ["sYL", "sWa", "sWb", "sMM", "sEV", "sX2", "sTOr", "sTOc",
             "sv", "ssc", "sLN", "sOUT", "sID", "fG", "fV", "fS"]
    sms = {n: es.enter_context(nc.semaphore(n)) for n in names}
    (sYL, sWa, sWb, sMM, sEV, sX2, sTOr, sTOc, sv, ssc, sLN, sOUT, sID,
     fG, fV, fS) = (sms[n] for n in names)
    sW2 = [sWa, sWb]

    with es, nc.Block() as blk:
        @blk.gpsimd
        def _(g):
            g.memset(t_eps[:], EPS)
            make_identities(nc, g, [t_id[:], t_idb[:]], fG)
            g.nop().then_inc(sID, 1)
            for tch in range(NTCH):
                for cout in range(ET):
                    wi = tch * ET + cout
                    if wi >= 1:
                        g.wait_ge(sMM, wi)
                    g.dma_start(
                        out=t_W[0][:],
                        in_=wdT_all.rearrange("(i p) n -> p i n", p=128)[
                            :, :, cout * 128:(cout + 1) * 128]
                    ).then_inc(sWa, 16)

        @blk.sync
        def _(s):
            for tch in range(NTCH):
                if tch >= 1:
                    s.wait_ge(sMM, ET * tch)
                for ct in range(NC_):
                    s.dma_start(out=t_hs[:, ct * TC:(ct + 1) * TC],
                                in_=hsT_all[ts(ct, 128), tch * TC:(tch + 1) * TC]
                                ).then_inc(sYL, 16)
                if tch >= 1:
                    s.wait_ge(sLN, NSUB * tch)
                    s.wait_ge(sOUT, 16 * NSUB * tch)
                for tsub in range(NSUB):
                    s.dma_start(out=t_x1t[:, tsub * D:(tsub + 1) * D],
                                in_=x1[ts(tch * NSUB + tsub, 128), :]
                                ).then_inc(sX2, 16)
                for tsub in range(NSUB):
                    k = tch * NSUB + tsub
                    s.wait_ge(sLN, k + 1)
                    s.dma_start(out=x2[ts(k, 128), :],
                                in_=t_x1t[:, tsub * D:(tsub + 1) * D]
                                ).then_inc(sOUT, 16)
            s.wait_ge(sOUT, 16 * NTCH * NSUB)

        @blk.tensor
        def _(t):
            t.wait_ge(sID, 1)
            for tch in range(NTCH):
                t.wait_ge(sYL, 16 * NC_ * (tch + 1))
                for cout in range(ET):
                    wi = tch * ET + cout
                    t.wait_ge(sWa, 16 * (wi + 1))
                    if wi >= 2:
                        t.wait_ge(sEV, wi - 1)
                    for ct in range(NC_):
                        ins = t.matmul(pslot(wi % 2),
                                       t_W[0][:, ct * 128:(ct + 1) * 128],
                                       t_hs[:, ct * TC:(ct + 1) * TC],
                                       start=(ct == 0), stop=(ct == NC_ - 1))
                    ins.then_inc(sMM, 1)
                for j in range(ET * NSUB):
                    et, tsub = divmod(j, NSUB)
                    jj = tch * ET * NSUB + j
                    t.wait_ge(sEV, tch * ET + et + 1)
                    if jj >= 2:
                        t.wait_ge(sTOc, jj - 1)
                    t.transpose(p_tb[jj % 2][:, 0:128],
                                t_mixT[:, et * TC + tsub * 128: et * TC + (tsub + 1) * 128],
                                t_idb[:]).then_inc(sTOr, 1)

        @blk.scalar
        def _(s):
            s.wait_ge(sID, 1)
            cS = Chain(s, fS)
            for tch in range(NTCH):
                s.wait_ge(sv, tch + 1)
                for tsub in range(NSUB):
                    cS.wait()
                    cS.op(s.activation(t_lnv[:, tsub:tsub + 1],
                                       t_mv[:, 2 * tsub + 1:2 * tsub + 2],
                                       AF.Ln, bias=t_eps[:]))
                cS.wait()
                for tsub in range(NSUB):
                    ins = s.activation(t_rstd[:, tsub:tsub + 1],
                                       t_lnv[:, tsub:tsub + 1], AF.Exp, scale=-0.5)
                    if tsub == NSUB - 1:
                        ins.then_inc(ssc, 1)
                s.wait_ge(ssc, tch + 1)

        @blk.vector
        def _(v):
            cV = Chain(v, fV)
            for tch in range(NTCH):
                for cout in range(ET):
                    wi = tch * ET + cout
                    v.wait_ge(sMM, wi + 1)
                    v.tensor_copy(t_mixT[:, cout * TC:(cout + 1) * TC],
                                  pslot(wi % 2)).then_inc(sEV, 1)
                v.wait_ge(sX2, 16 * NSUB * (tch + 1))
                for j in range(ET * NSUB):
                    et, tsub = divmod(j, NSUB)
                    jj = tch * ET * NSUB + j
                    v.wait_ge(sTOr, jj + 1)
                    v.scalar_tensor_tensor(
                        t_s[:, tsub * D + et * 128: tsub * D + (et + 1) * 128],
                        t_x1t[:, tsub * D + et * 128: tsub * D + (et + 1) * 128],
                        2.0, p_tb[jj % 2][:, 0:128],
                        ALU.mult, ALU.add).then_inc(sTOc, 1)
                v.wait_ge(sTOc, (tch + 1) * ET * NSUB)
                for tsub in range(NSUB):
                    for gix in range(4):
                        ins = v.bn_stats(
                            t_stats[:, tsub * 24 + gix * 6: tsub * 24 + gix * 6 + 6],
                            t_s[:, tsub * D + gix * 512: tsub * D + (gix + 1) * 512])
                        cV.op(ins)
                cV.wait()
                for tsub in range(NSUB):
                    ins_ = v.bn_aggr(t_mv[:, 2 * tsub:2 * tsub + 2],
                                     t_stats[:, tsub * 24:(tsub + 1) * 24])
                    if tsub == NSUB - 1:
                        ins_.then_inc(sv, 1)
                v.wait_ge(ssc, tch + 1)
                for tsub in range(NSUB):
                    v.tensor_scalar(t_x1t[:, tsub * D:(tsub + 1) * D],
                                    t_s[:, tsub * D:(tsub + 1) * D],
                                    t_mv[:, 2 * tsub:2 * tsub + 1],
                                    t_rstd[:, tsub:tsub + 1],
                                    ALU.subtract, ALU.mult).then_inc(sLN, 1)

    return nc


def build_l0attnA():
    nc = bass.Bass("TRN2", target_bir_lowering=False)
    x0 = nc.dram_tensor("x0", [T, D], F32, kind="ExternalInput").ap()
    Wq = nc.dram_tensor("Wq", [D, D], F32, kind="ExternalInput").ap()
    Wk = nc.dram_tensor("Wk", [D, D], F32, kind="ExternalInput").ap()
    Wv = nc.dram_tensor("Wv", [D, D], F32, kind="ExternalInput").ap()
    qrw = nc.dram_tensor("qrw", [128], F32, kind="ExternalInput").ap()
    krw = nc.dram_tensor("krw", [128], F32, kind="ExternalInput").ap()
    qT = nc.dram_tensor("qT", [D, T], BF16, kind="ExternalOutput").ap()
    kT = nc.dram_tensor("kT", [D, T], BF16, kind="ExternalOutput").ap()
    vT = nc.dram_tensor("vT", [D, T], BF16, kind="ExternalOutput").ap()

    CT = D // 128
    NTT = T // 128
    NTCH = 8
    TC = T // NTCH

    es = contextlib.ExitStack()
    sb = lambda *a: es.enter_context(nc.sbuf_tensor(*a))
    ps = lambda *a: es.enter_context(nc.psum_tensor(*a))
    t_id = sb("ident", [128, 128], F32)
    t_x = [sb(f"xb{i}", [128, D], F32) for i in range(2)]
    t_xT = sb("xT", [128, CT * T], BF16)
    t_W = [sb(f"W{i}", [128, 16 * 128], BF16) for i in range(2)]
    t_onesc = sb("onesc", [128, 1], F32)
    t_ones1 = sb("ones1", [1, 128], F32)
    t_qf = sb("qf", [128, TC], F32)
    t_q2 = sb("q2", [128, TC], F32)
    t_ssq = sb("ssq", [1, TC], F32)
    t_rr = sb("rr", [1, TC], F32)
    t_rw = sb("rwcol", [128, 2], F32)
    t_eps1 = sb("eps1", [1, 1], F32)
    t_eps128 = sb("eps128", [1, 1], F32)
    t_out = [sb(f"ob{i}", [128, TC], BF16) for i in range(2)]
    p4 = [ps(f"pb{i}", [128, 1024], F32) for i in range(2)]

    def pslot(i):
        return p4[i // 2][:, (i % 2) * 512:(i % 2 + 1) * 512]

    names = ["sXa", "sXb", "sTIr", "sTIe", "sID", "sRW", "sWa_", "sWb_",
             "sMM", "sQF", "svQ", "sMM2", "sscQ", "svR2", "sBC", "sU2",
             "sSTT", "sOa", "sOb", "fV", "fS", "fG"]
    sms = {n: es.enter_context(nc.semaphore(n)) for n in names}
    (sXa, sXb, sTIr, sTIe, sID, sRW, sWa_, sWb_, sMM, sQF, svQ, sMM2, sscQ,
     svR2, sBC, sU2, sSTT, sOa, sOb, fV, fS, fG) = (sms[n] for n in names)
    sX2 = [sXa, sXb]
    sW2 = [sWa_, sWb_]
    sO2 = [sOa, sOb]

    WMATS = [(Wq, qT, 0), (Wk, kT, 1), (Wv, vT, 2)]

    def qof(mi, ho, tch):
        return (mi * 16 + ho) * NTCH + tch

    def qiof(mi, ho, tch):
        return (mi * 16 + ho) * NTCH + tch

    with es, nc.Block() as blk:
        @blk.gpsimd
        def _(g):
            g.memset(t_onesc[:], 1.0)
            g.memset(t_ones1[:], 1.0)
            g.memset(t_eps1[:], EPS)
            g.memset(t_eps128[:], 128.0 * EPS)
            make_identities(nc, g, [t_id[:]], fG)
            g.nop().then_inc(sID, 1)
            wi = 0
            for (Wm, _o, _k) in WMATS:
                for ho in range(16):
                    if wi >= 2:
                        g.wait_ge(sMM, NTCH * (wi - 1))
                    g.dma_start(
                        out=t_W[wi % 2][:],
                        in_=Wm.rearrange("(c p) n -> p c n", p=128)[
                            :, :, ho * 128:(ho + 1) * 128]
                    ).then_inc(sW2[wi % 2], 16)
                    wi += 1

        @blk.sync
        def _(s):
            s.dma_start(out=t_rw[:, 0:1],
                        in_=qrw.rearrange("(p o) -> p o", o=1)).then_inc(sRW, 16)
            s.dma_start(out=t_rw[:, 1:2],
                        in_=krw.rearrange("(p o) -> p o", o=1)).then_inc(sRW, 16)
            for i in range(NTT):
                if i >= 2:
                    s.wait_ge(sTIr, (i - 1) * CT)
                s.dma_start(out=t_x[i % 2][:],
                            in_=x0[ts(i, 128), :]).then_inc(sX2[i % 2], 16)
            for mi, (_w, Om, kind) in enumerate(WMATS):
                for ho in range(16):
                    for tch in range(NTCH):
                        q = qof(mi, ho, tch)
                        s.wait_ge(sSTT, q + 1)
                        s.dma_start(out=Om[ts(ho, 128), tch * TC:(tch + 1) * TC],
                                    in_=t_out[q % 2][:]).then_inc(sO2[q % 2], 16)
            s.wait_ge(sOa, 3 * 16 * NTCH // 2 * 16)
            s.wait_ge(sOb, 3 * 16 * NTCH // 2 * 16)

        @blk.tensor
        def _(t):
            t.wait_ge(sID, 1)
            for i in range(NTT):
                t.wait_ge(sX2[i % 2], 16 * (i // 2 + 1))
                for c in range(CT):
                    j = i * CT + c
                    if j >= 2:
                        t.wait_ge(sTIe, j - 1)
                    t.transpose(p4[0][:, (j % 2) * 512:(j % 2) * 512 + 128],
                                t_x[i % 2][:, c * 128:(c + 1) * 128],
                                t_id[:]).then_inc(sTIr, 1)
            t.wait_ge(sTIe, NTT * CT)
            wi = 0
            for mi, (_w, _o, kind) in enumerate(WMATS):
                for ho in range(16):
                    t.wait_ge(sW2[wi % 2], 16 * (wi // 2 + 1))
                    for tch in range(NTCH):
                        q = wi * NTCH + tch
                        if q >= 2:
                            t.wait_ge(sQF, q - 1)
                        for c in range(CT):
                            ins = t.matmul(
                                pslot(q % 2), t_W[wi % 2][:, c * 128:(c + 1) * 128],
                                t_xT[:, c * T + tch * TC: c * T + (tch + 1) * TC],
                                start=(c == 0), stop=(c == CT - 1))
                        ins.then_inc(sMM, 1)
                        if kind < 2:
                            qi = qiof(mi, ho, tch)
                            t.wait_ge(svQ, qi + 1)
                            if qi >= 1:
                                t.wait_ge(sscQ, qi)
                            t.matmul(pslot(2)[0:1, :], t_onesc[:], t_q2[:],
                                     start=True, stop=True).then_inc(sMM2, 1)
                            t.wait_ge(svR2, qi + 1)
                            if qi >= 1:
                                t.wait_ge(sU2, qi)
                            t.matmul(pslot(3), t_ones1[:], t_rr[0:1, :],
                                     start=True, stop=True).then_inc(sBC, 1)
                    wi += 1

        @blk.scalar
        def _(s):
            s.wait_ge(sID, 1)
            for mi in range(2):
                for ho in range(16):
                    for tch in range(NTCH):
                        qi = qiof(mi, ho, tch)
                        s.wait_ge(sMM2, qi + 1)
                        if qi >= 1:
                            s.wait_ge(svR2, qi)
                        if mi == 0:
                            s.activation(t_ssq[:], pslot(2)[0:1, :], AF.Sqrt,
                                         bias=t_eps128[:]).then_inc(sscQ, 1)
                        else:
                            s.activation(t_ssq[:], pslot(2)[0:1, :], AF.Sqrt,
                                         scale=1.0 / 128.0,
                                         bias=t_eps1[:]).then_inc(sscQ, 1)
                        s.wait_ge(sscQ, qi + 1)

        @blk.vector
        def _(v):
            v.wait_ge(sRW, 32)
            cV = Chain(v, fV)
            for i in range(NTT):
                for c in range(CT):
                    j = i * CT + c
                    v.wait_ge(sTIr, j + 1)
                    v.tensor_copy(t_xT[:, c * T + i * 128: c * T + (i + 1) * 128],
                                  p4[0][:, (j % 2) * 512:(j % 2) * 512 + 128]
                                  ).then_inc(sTIe, 1)
            for mi, (_w, _o, kind) in enumerate(WMATS):
                for ho in range(16):
                    for tch in range(NTCH):
                        q = qof(mi, ho, tch)
                        v.wait_ge(sMM, q + 1)
                        if kind == 2:
                            if q >= 2:
                                v.wait_ge(sO2[q % 2], 16 * (q // 2))
                            v.tensor_copy(t_out[q % 2][:],
                                          pslot(q % 2)).then_inc(sSTT, 1)
                            v.wait_ge(sSTT, q + 1)
                            v.engine_nop().then_inc(sQF, 1)
                            continue
                        qi = qiof(mi, ho, tch)
                        cV.wait()
                        ins = v.tensor_copy(t_qf[:], pslot(q % 2))
                        ins.then_inc(sQF, 1)
                        v.wait_ge(sQF, q + 1)
                        if qi >= 1:
                            v.wait_ge(sMM2, qi)
                        cV.wait()
                        ins = v.tensor_tensor(t_q2[:], t_qf[:], t_qf[:], ALU.mult)
                        ins.then_inc(svQ, 1)
                        v.wait_ge(svQ, qi + 1)
                        v.wait_ge(sscQ, qi + 1)
                        cV.wait()
                        ins = v.reciprocal(t_rr[:], t_ssq[:])
                        ins.then_inc(svR2, 1)
                        v.wait_ge(svR2, qi + 1)
                        v.wait_ge(sBC, qi + 1)
                        if q >= 2:
                            v.wait_ge(sO2[q % 2], 16 * (q // 2))
                        cV.wait()
                        ins = v.scalar_tensor_tensor(
                            t_out[q % 2][:], t_qf[:], t_rw[:, mi:mi + 1],
                            pslot(3), ALU.mult, ALU.mult)
                        ins.then_inc(sU2, 1)
                        v.wait_ge(sU2, qi + 1)
                        v.engine_nop().then_inc(sSTT, 1)

    return nc


def build_l0attnB():
    nc = bass.Bass("TRN2", target_bir_lowering=False)
    qTd = nc.dram_tensor("qT", [D, T], BF16, kind="ExternalInput").ap()
    kTd = nc.dram_tensor("kT", [D, T], BF16, kind="ExternalInput").ap()
    vTd = nc.dram_tensor("vT", [D, T], BF16, kind="ExternalInput").ap()
    Wproj = nc.dram_tensor("Wproj", [D, D], F32, kind="ExternalInput").ap()
    subw = nc.dram_tensor("subw", [128], F32, kind="ExternalInput").ap()
    Rt = nc.dram_tensor("Rt", [16, T], F32, kind="ExternalInput").ap()
    x0 = nc.dram_tensor("x0", [T, D], F32, kind="ExternalInput").ap()
    x1 = nc.dram_tensor("x1", [T, D], F32, kind="ExternalOutput").ap()

    NQB = 8
    QB = 512
    NSUB = 4
    NH = 16
    ET = D // 128

    es = contextlib.ExitStack()
    sb = lambda *a: es.enter_context(nc.sbuf_tensor(*a))
    ps = lambda *a: es.enter_context(nc.psum_tensor(*a))
    t_id = sb("ident", [128, 128], F32)
    t_idb = sb("identb", [128, 128], BF16)
    t_mask = sb("cmask", [128, 128], F32)
    t_kh = sb("kh", [128, T], BF16)
    t_vtok = sb("vtok", [128, T], BF16)
    t_qh = sb("qh", [128, QB], BF16)
    t_R = sb("Rrow", [1, T], F32)
    t_sc = sb("scores", [128, T], F32)
    t_P = sb("P", [128, T], BF16)
    t_PT = sb("PT", [128, 32 * QB], BF16)
    t_cm = sb("cm", [128, 33], F32)
    t_nm = sb("nm", [128, 1], F32)
    t_den = sb("den", [128, 1], F32)
    t_rd = sb("rd", [128, 1], F32)
    t_yf = sb("yf", [128, QB], F32)
    t_q2 = sb("q2", [128, QB], F32)
    t_ssq = sb("ssq", [1, QB], F32)
    t_rr = sb("rr", [1, QB], F32)
    t_sw = sb("swcol", [128, 1], F32)
    t_eps1 = sb("eps1", [1, 1], F32)
    t_onesc = sb("onesc", [128, 1], F32)
    t_ones1 = sb("ones1", [1, 128], F32)
    t_ynT = sb("ynT", [128, NH * QB], BF16)
    t_aT = sb("aT", [128, ET * QB], BF16)
    t_Wp = [sb("Wp0", [128, 16 * 128], BF16)]
    t_x0t = sb("x0t", [128, NSUB * D], F32)
    t_s = sb("s_res", [128, NSUB * D], F32)
    t_stats = sb("stats", [128, 4 * 24], F32)
    t_mv = sb("mv", [128, 4 * 2], F32)
    t_lnv = sb("lnv", [128, 4], F32)
    t_rstd = sb("rstd", [128, 4], F32)
    t_eps = sb("eps", [128, 1], F32)
    p4 = [ps(f"pb{i}", [128, 1024], F32) for i in range(3)]
    p_tb = [ps(f"ptb{i}", [128, 1024], BF16) for i in range(2)]

    def pslot(i):
        return p4[i // 2][:, (i % 2) * 512:(i % 2 + 1) * 512]

    names = ["sID", "sKV", "sHD", "sSM", "sSE", "svM", "sscE", "svP",
             "sTPr", "sTPe", "sAV", "sYE", "sMS", "svQ", "sMM2", "sscQ", "svR2",
             "sBC", "sYN", "sWa_", "sWb_", "sVT", "sPM", "sPE", "sX0", "sTOr",
             "sTOc", "sv", "ssc", "sLN", "sOUT", "fV", "fS", "fG"]
    sms = {n: es.enter_context(nc.semaphore(n)) for n in names}
    (sID, sKV, sHD, sSM, sSE, svM, sscE, svP, sTPr, sTPe, sAV, sYE, sMS, svQ,
     sMM2, sscQ, svR2, sBC, sYN, sWa_, sWb_, sVT, sPM, sPE, sX0, sTOr, sTOc,
     sv, ssc, sLN, sOUT, fV, fS, fG) = (sms[n] for n in names)
    sW2 = [sWa_, sWb_]

    def chunks(qa):
        klen = (qa + 1) * 128
        out = []
        off = 0
        while off < klen:
            ln = min(512, klen - off)
            out.append((off, ln))
            off += ln
        return out

    with es, nc.Block() as blk:
        @blk.gpsimd
        def _(g):
            g.memset(t_eps1[:], EPS)
            g.memset(t_eps[:], EPS)
            g.memset(t_onesc[:], 1.0)
            g.memset(t_ones1[:], 1.0)
            g.memset(t_mask[:], 0.0)
            make_identities(nc, g, [t_id[:], t_idb[:]], fG)
            g.affine_select(out=t_mask[:], in_=t_mask[:], compare_op=ALU.is_ge,
                            fill=-1e30, base=0, pattern=[[-1, 128]],
                            channel_multiplier=1).then_inc(fG, 1)
            g.wait_ge(fG, 3)
            g.nop().then_inc(sID, 1)
            for qb in range(NQB):
                for h in range(NH):
                    hi = qb * NH + h
                    if hi >= 1:
                        g.wait_ge(sHD, hi)
                    g.dma_start(out=t_kh[:], in_=kTd[ts(h, 128), :]).then_inc(sKV, 16)
                    g.dma_start(out=t_qh[:],
                                in_=qTd[ts(h, 128), qb * QB:(qb + 1) * QB]
                                ).then_inc(sKV, 16)
                    g.dma_start(out=t_R[:], in_=Rt[h:h + 1, :]).then_inc(sKV, 16)
                for cout in range(ET):
                    wi = qb * ET + cout
                    if wi >= 1:
                        g.wait_ge(sPM, wi)
                    g.dma_start(
                        out=t_Wp[0][:],
                        in_=Wproj.rearrange("(c p) n -> p c n", p=128)[
                            :, :, cout * 128:(cout + 1) * 128]
                    ).then_inc(sWa_, 16)

        @blk.sync
        def _(s):
            s.dma_start(out=t_sw[:],
                        in_=subw.rearrange("(p o) -> p o", o=1)).then_inc(sX0, 16)
            for qb in range(NQB):
                for h in range(NH):
                    hi = qb * NH + h
                    if hi >= 1:
                        s.wait_ge(sHD, hi)
                    for st in range(32):
                        s.dma_start_transpose(
                            t_vtok[:, st * 128:(st + 1) * 128],
                            vTd[ts(h, 128), st * 128:(st + 1) * 128]
                        ).then_inc(sVT, 16)

                if qb >= 1:
                    s.wait_ge(sLN, NSUB * qb)
                    s.wait_ge(sOUT, 16 * NSUB * qb)
                for tsub in range(NSUB):
                    s.dma_start(out=t_x0t[:, tsub * D:(tsub + 1) * D],
                                in_=x0[ts(qb * NSUB + tsub, 128), :]
                                ).then_inc(sX0, 16)
                for tsub in range(NSUB):
                    k = qb * NSUB + tsub
                    s.wait_ge(sLN, k + 1)
                    s.dma_start(out=x1[ts(k, 128), :],
                                in_=t_x0t[:, tsub * D:(tsub + 1) * D]
                                ).then_inc(sOUT, 16)
            s.wait_ge(sOUT, 16 * NQB * NSUB)

        @blk.tensor
        def _(t):
            t.wait_ge(sID, 1)
            nTP = 0
            nSM = 0
            nAV = 0
            nQI = 0
            for qb in range(NQB):
                for h in range(NH):
                    hi = qb * NH + h
                    t.wait_ge(sKV, 48 * (hi + 1))
                    t.wait_ge(sVT, 16 * 32 * (hi + 1))
                    for qs in range(NSUB):
                        qa = qb * NSUB + qs
                        chs = chunks(qa)
                        for ji, (off, ln) in enumerate(chs):
                            nSM += 1
                            if nSM >= 3:
                                t.wait_ge(sSE, nSM - 2)
                            t.matmul(pslot(nSM % 2)[:, 0:ln],
                                     t_qh[:, qs * 128:(qs + 1) * 128],
                                     t_kh[:, off:off + ln],
                                     start=True, stop=False)
                            t.matmul(pslot(nSM % 2)[:, 0:ln], t_ones1[:],
                                     t_R[0:1, off:off + ln],
                                     start=False, stop=True).then_inc(sSM, 1)
                        si = (qb * NH + h) * NSUB + qs
                        t.wait_ge(svP, si + 1)
                        for st in range(qa + 1):
                            nTP += 1
                            if nTP >= 3:
                                t.wait_ge(sTPe, nTP - 2)
                            t.transpose(p_tb[nTP % 2][:, 0:128],
                                        t_P[:, st * 128:(st + 1) * 128],
                                        t_idb[:]).then_inc(sTPr, 1)
                    t.wait_ge(sTPe, nTP)
                    t.wait_ge(sMS, 3 * (hi + 1))
                    t.wait_ge(sYE, hi)
                    nst = qb * NSUB + NSUB
                    for st in range(nst):
                        ins = t.matmul(pslot(4), t_vtok[:, st * 128:(st + 1) * 128],
                                       t_PT[:, st * QB:(st + 1) * QB],
                                       start=(st == 0), stop=(st == nst - 1))
                    ins.then_inc(sAV, 1)
                    qi = hi
                    t.wait_ge(svQ, qi + 1)
                    if qi >= 1:
                        t.wait_ge(sscQ, qi)
                    t.matmul(pslot(3)[0:1, :], t_onesc[:], t_q2[:],
                             start=True, stop=True).then_inc(sMM2, 1)
                    t.wait_ge(svR2, qi + 1)
                    if qi >= 1:
                        t.wait_ge(sYN, qi)
                    t.matmul(pslot(2), t_ones1[:], t_rr[0:1, :],
                             start=True, stop=True).then_inc(sBC, 1)
                    t.nop().then_inc(sHD, 1)
                t.wait_ge(sYN, (qb + 1) * NH)
                for cout in range(ET):
                    wi = qb * ET + cout
                    t.wait_ge(sWa_, 16 * (wi + 1))
                    if wi >= 2:
                        t.wait_ge(sPE, wi - 1)
                    for h in range(NH):
                        ins = t.matmul(pslot(5 if wi % 2 == 0 else 0),
                                       t_Wp[0][:, h * 128:(h + 1) * 128],
                                       t_ynT[:, h * QB:(h + 1) * QB],
                                       start=(h == 0), stop=(h == NH - 1))
                    ins.then_inc(sPM, 1)
                for j in range(ET * NSUB):
                    et, tsub = divmod(j, NSUB)
                    jj = qb * ET * NSUB + j
                    t.wait_ge(sPE, qb * ET + et + 1)
                    nTP += 1
                    if nTP >= 3:
                        t.wait_ge(sTPe, nTP - 2)
                    t.transpose(p_tb[nTP % 2][:, 0:128],
                                t_aT[:, et * QB + tsub * 128: et * QB + (tsub + 1) * 128],
                                t_idb[:]).then_inc(sTPr, 1)

        @blk.scalar
        def _(s):
            s.wait_ge(sID, 1)
            cS = Chain(s, fS)
            for qb in range(NQB):
                for h in range(NH):
                    hi = qb * NH + h
                    for qs in range(NSUB):
                        qa = qb * NSUB + qs
                        si = (qb * NH + h) * NSUB + qs
                        s.wait_ge(svM, si + 1)
                        s.activation(t_P[:, 0:(qa + 1) * 128],
                                     t_sc[:, 0:(qa + 1) * 128], AF.Exp,
                                     bias=t_nm[:], accum_out=t_den[:]
                                     ).then_inc(sscE, 1)
                        s.wait_ge(sscE, si + 1)
                    qi = hi
                    s.wait_ge(sMM2, qi + 1)
                    if qi >= 1:
                        s.wait_ge(svR2, qi)
                    s.activation(t_ssq[:], pslot(3)[0:1, :], AF.Sqrt,
                                 scale=1.0 / 128.0, bias=t_eps1[:]).then_inc(sscQ, 1)
                    s.wait_ge(sscQ, qi + 1)
                s.wait_ge(sv, qb + 1)
                for tsub in range(NSUB):
                    cS.wait()
                    cS.op(s.activation(t_lnv[:, tsub:tsub + 1],
                                       t_mv[:, 2 * tsub + 1:2 * tsub + 2],
                                       AF.Ln, bias=t_eps[:]))
                cS.wait()
                for tsub in range(NSUB):
                    ins = s.activation(t_rstd[:, tsub:tsub + 1],
                                       t_lnv[:, tsub:tsub + 1], AF.Exp, scale=-0.5)
                    if tsub == NSUB - 1:
                        ins.then_inc(ssc, 1)
                s.wait_ge(ssc, qb + 1)

        @blk.vector
        def _(v):
            v.wait_ge(sX0, 16)
            cV = Chain(v, fV)
            nTP = 0
            nSM = 0
            for qb in range(NQB):
                for h in range(NH):
                    hi = qb * NH + h
                    if hi >= 1:
                        v.wait_ge(sAV, hi)
                    for ib in range(1, NSUB):
                        st = qb * NSUB + ib
                        v.memset(t_PT[:, st * QB: st * QB + ib * 128],
                                 0.0).then_inc(sMS, 1)
                    for qs in range(NSUB):
                        qa = qb * NSUB + qs
                        si = (qb * NH + h) * NSUB + qs
                        chs = chunks(qa)
                        for ji, (off, ln) in enumerate(chs):
                            nSM += 1
                            v.wait_ge(sSM, nSM)
                            npc = ln // 128
                            for u in range(npc):
                                kidx = off // 128 + u
                                pp = pslot(nSM % 2)[:, u * 128:(u + 1) * 128]
                                dst = t_sc[:, kidx * 128:(kidx + 1) * 128]
                                if kidx == qa:
                                    cV.wait()
                                    cV.op(v.scalar_tensor_tensor(
                                        dst, pp, 1.0, t_mask[:], ALU.mult, ALU.add))
                                    cV.wait()
                                    cV.op(v.reduce_max(t_cm[:, kidx:kidx + 1],
                                                       dst, axis=AX.X))
                                else:
                                    cV.wait()
                                    cV.op(v.tensor_scalar(
                                        dst, pp, 1.0, -1e30, ALU.mult, ALU.max,
                                        accum_out=t_cm[:, kidx:kidx + 1]))
                            v.wait_ge(fV, cV.n)
                            v.engine_nop().then_inc(sSE, 1)
                        cV.wait()
                        ins = v.reduce_max(t_nm[:], t_cm[:, 0:qa + 1], axis=AX.X,
                                           negate=True)
                        ins.then_inc(svM, 1)
                        v.wait_ge(sscE, si + 1)
                        cV.wait()
                        cV.op(v.reciprocal(t_rd[:], t_den[:]))
                        cV.wait()
                        ins = v.tensor_scalar(t_P[:, 0:(qa + 1) * 128],
                                              t_P[:, 0:(qa + 1) * 128],
                                              t_rd[:], None, ALU.mult)
                        ins.then_inc(svP, 1)
                        v.wait_ge(svP, si + 1)
                        for st in range(qa + 1):
                            nTP += 1
                            v.wait_ge(sTPr, nTP)
                            v.tensor_copy(
                                t_PT[:, st * QB + qs * 128: st * QB + (qs + 1) * 128],
                                p_tb[nTP % 2][:, 0:128]).then_inc(sTPe, 1)
                    v.wait_ge(sAV, hi + 1)
                    cV.wait()
                    ins = v.tensor_copy(t_yf[:], pslot(4))
                    ins.then_inc(sYE, 1)
                    v.wait_ge(sYE, hi + 1)
                    qi = hi
                    if qi >= 1:
                        v.wait_ge(sMM2, qi)
                    cV.wait()
                    ins = v.tensor_tensor(t_q2[:], t_yf[:], t_yf[:], ALU.mult)
                    ins.then_inc(svQ, 1)
                    v.wait_ge(svQ, qi + 1)
                    v.wait_ge(sscQ, qi + 1)
                    cV.wait()
                    ins = v.reciprocal(t_rr[:], t_ssq[:])
                    ins.then_inc(svR2, 1)
                    v.wait_ge(svR2, qi + 1)
                    v.wait_ge(sBC, qi + 1)
                    cV.wait()
                    ins = v.scalar_tensor_tensor(
                        t_ynT[:, h * QB:(h + 1) * QB], t_yf[:], t_sw[:],
                        pslot(2), ALU.mult, ALU.mult)
                    ins.then_inc(sYN, 1)
                    v.wait_ge(sYN, qi + 1)
                for cout in range(ET):
                    wi = qb * ET + cout
                    v.wait_ge(sPM, wi + 1)
                    v.tensor_copy(t_aT[:, cout * QB:(cout + 1) * QB],
                                  pslot(5 if wi % 2 == 0 else 0)).then_inc(sPE, 1)
                v.wait_ge(sX0, 16 + 16 * NSUB * (qb + 1))
                for j in range(ET * NSUB):
                    et, tsub = divmod(j, NSUB)
                    jj = qb * ET * NSUB + j
                    nTP += 1
                    v.wait_ge(sTPr, nTP)
                    v.scalar_tensor_tensor(
                        t_s[:, tsub * D + et * 128: tsub * D + (et + 1) * 128],
                        t_x0t[:, tsub * D + et * 128: tsub * D + (et + 1) * 128],
                        1.0, p_tb[nTP % 2][:, 0:128],
                        ALU.mult, ALU.add).then_inc(sTPe, 1)
                v.wait_ge(sTPe, nTP)
                for tsub in range(NSUB):
                    for gix in range(4):
                        ins = v.bn_stats(
                            t_stats[:, tsub * 24 + gix * 6: tsub * 24 + gix * 6 + 6],
                            t_s[:, tsub * D + gix * 512: tsub * D + (gix + 1) * 512])
                        cV.op(ins)
                cV.wait()
                for tsub in range(NSUB):
                    ins_ = v.bn_aggr(t_mv[:, 2 * tsub:2 * tsub + 2],
                                     t_stats[:, tsub * 24:(tsub + 1) * 24])
                    if tsub == NSUB - 1:
                        ins_.then_inc(sv, 1)
                v.wait_ge(ssc, qb + 1)
                for tsub in range(NSUB):
                    v.tensor_scalar(t_x0t[:, tsub * D:(tsub + 1) * D],
                                    t_s[:, tsub * D:(tsub + 1) * D],
                                    t_mv[:, 2 * tsub:2 * tsub + 1],
                                    t_rstd[:, tsub:tsub + 1],
                                    ALU.subtract, ALU.mult).then_inc(sLN, 1)

    return nc


_NCS = {}


def _get_nc(name):
    if name not in _NCS:
        _NCS[name] = globals()["build_" + name]()
    return _NCS[name]


def _alibi_table():
    def slopes(n):
        start = 2 ** (-2 ** (-(math.log2(n) - 3)))
        return [start * start ** i for i in range(n)]
    sl = np.array(slopes(16), dtype=np.float32)
    return np.ascontiguousarray(sl[:, None] * np.arange(T, dtype=np.float32)[None, :])


def _launch(name, per_core, shared):
    nc = _get_nc(name)
    in_maps = [dict(shared, **pc) for pc in per_core]
    res = run_bass_kernel_spmd(nc, in_maps, core_ids=list(range(NCORES)))
    return res.results


def kernel(**inp):
    cT = np.ascontiguousarray
    x0s = [cT(inp["L0_attn_x"][b]) for b in range(NCORES)]

    r = _launch("l0attnA", [dict(x0=x) for x in x0s],
                dict(Wq=cT(inp["L0_attn_Wq"]), Wk=cT(inp["L0_attn_Wk"]),
                     Wv=cT(inp["L0_attn_Wv"]),
                     qrw=cT(inp["L0_attn_q_rms_w"]), krw=cT(inp["L0_attn_k_rms_w"])))
    r = _launch("l0attnB",
                [dict(qT=r[b]["qT"], kT=r[b]["kT"], vT=r[b]["vT"], x0=x0s[b])
                 for b in range(NCORES)],
                dict(Wproj=cT(inp["L0_attn_Wproj"]), subw=cT(inp["L0_attn_subln_w"]),
                     Rt=_alibi_table()))
    x1s = [r[b]["x1"] for b in range(NCORES)]

    r = _launch("l0ffnA", [dict(x1=x1s[b]) for b in range(NCORES)],
                dict(routerT=cT(inp["L0_ffn_router_w"].T),
                     sgT=cT(inp["L0_ffn_shared_gate"].T),
                     suT=cT(inp["L0_ffn_shared_up"].T),
                     wgT=cT(inp["L0_ffn_w_gate"].transpose(0, 2, 1)),
                     wuT=cT(inp["L0_ffn_w_up"].transpose(0, 2, 1))))
    wdT_all = np.concatenate(
        [inp["L0_ffn_shared_down"].T]
        + [inp["L0_ffn_w_down"][e].T for e in range(8)], axis=0)
    r = _launch("l0ffnB",
                [dict(hsT_all=r[b]["hsT_all"], x1=x1s[b]) for b in range(NCORES)],
                dict(wdT_all=cT(wdT_all)))
    x2s = [r[b]["x2"] for b in range(NCORES)]
    del x1s

    r = _launch("l1preA", [dict(x2=x2s[b]) for b in range(NCORES)],
                dict(inprojT=cT(inp["L1_attn_in_proj_w"].T),
                     convw=cT(inp["L1_attn_conv_w"].reshape(4096, 4)),
                     convb=cT(inp["L1_attn_conv_b"]),
                     ln1w=cT(inp["L1_attn_ln1_w"]), ln1b=cT(inp["L1_attn_ln1_b"])))
    xcs = [r[b]["xcT"] for b in range(NCORES)]
    zs = [r[b]["zT"] for b in range(NCORES)]
    r = _launch("l1preB", [dict(xcT=xcs[b]) for b in range(NCORES)],
                dict(xprojT=cT(inp["L1_attn_x_proj_w"].T),
                     dtprojT=cT(inp["L1_attn_dt_proj_w"].T),
                     dtprojb=cT(inp["L1_attn_dt_proj_b"])))
    r = _launch("l1scan",
                [dict(xcT=xcs[b], dtT=r[b]["dtT"], BT=r[b]["BT"], CTo=r[b]["CTo"],
                      zT=zs[b]) for b in range(NCORES)],
                dict(A_log=cT(inp["L1_attn_A_log"]), D_skip=cT(inp["L1_attn_D_skip"])))
    del xcs, zs
    r = _launch("l1out",
                [dict(yfT=r[b]["yfT"], x2=x2s[b]) for b in range(NCORES)],
                dict(outprojT=cT(inp["L1_attn_out_proj_w"].T)))
    xmids = [r[b]["xmid"] for b in range(NCORES)]
    r = _launch("l1ffA", [dict(xmid=xmids[b]) for b in range(NCORES)],
                dict(ffprojT=cT(inp["L1_attn_ff_proj_w"].T),
                     ln2w=cT(inp["L1_attn_ln2_w"]), ln2b=cT(inp["L1_attn_ln2_b"])))
    r = _launch("l1ffB",
                [dict(hsT=r[b]["hsT"], x2=x2s[b], xmid=xmids[b])
                 for b in range(NCORES)],
                dict(ffoutT=cT(inp["L1_attn_ff_out_w"].T)))
    del x2s, xmids
    r = _launch("l1ffn", [dict(x3=r[b]["x3"]) for b in range(NCORES)],
                dict(factors=cT(inp["L1_ffn_factors"]), bias=cT(inp["L1_ffn_bias"])))
    return np.stack([r[b]["out"] for b in range(NCORES)], axis=0)


# version 04
# speedup vs baseline: 1.0059x; 1.0059x over previous
import contextlib
import math
import numpy as np
import ml_dtypes
import concourse.bass as bass
import concourse.mybir as mybir
from concourse.bass_utils import run_bass_kernel_spmd

F32 = mybir.dt.float32
BF16 = mybir.dt.bfloat16
AF = mybir.ActivationFunctionType
ALU = mybir.AluOpType
AX = mybir.AxisListType
ts = bass.ts

T = 4096
D = 2048
NCORES = 8
EPS = 1e-5


def make_identities(nc, g, aps, fence_sem, fence_base=0):
    for i, ap in enumerate(aps):
        ins = g.memset(ap, 0.0)
        if i == len(aps) - 1:
            ins.then_inc(fence_sem, 1)
    g.wait_ge(fence_sem, fence_base + 1)
    for i, ap in enumerate(aps):
        sq = ap.shape[0]
        ins = g.affine_select(out=ap, in_=ap, compare_op=ALU.not_equal,
                              fill=1.0, base=0, pattern=[[-1, sq]],
                              channel_multiplier=1)
        if i == len(aps) - 1:
            ins.then_inc(fence_sem, 1)
    g.wait_ge(fence_sem, fence_base + 2)


class Chain:

    def __init__(self, eng, sem):
        self.eng = eng
        self.sem = sem
        self.n = 0

    def op(self, ins):
        ins.then_inc(self.sem, 1)
        self.n += 1
        return ins

    def wait(self):
        self.eng.wait_ge(self.sem, self.n)




PHASE_IO = {
    "l1ffn": [
        ("x3", [T, D], F32, "ExternalInput"),
        ("factors", [8, D, D], F32, "ExternalInput"),
        ("bias", [D], F32, "ExternalInput"),
        ("out", [T, D], F32, "ExternalOutput"),
    ],
    "l1preA": [
        ("x2", [T, D], F32, "ExternalInput"),
        ("inprojT", [D, 8192], F32, "ExternalInput"),
        ("convw", [4096, 4], F32, "ExternalInput"),
        ("convb", [4096], F32, "ExternalInput"),
        ("ln1w", [D], F32, "ExternalInput"),
        ("ln1b", [D], F32, "ExternalInput"),
        ("xcT", [4096, T], BF16, "ExternalOutput"),
        ("zT", [4096, T], BF16, "ExternalOutput"),
    ],
    "l1preB": [
        ("xcT", [4096, T], BF16, "ExternalInput"),
        ("xprojT", [4096, 160], F32, "ExternalInput"),
        ("dtprojT", [128, 4096], F32, "ExternalInput"),
        ("dtprojb", [4096], F32, "ExternalInput"),
        ("dtT", [4096, T], F32, "ExternalOutput"),
        ("BT", [16, T], F32, "ExternalOutput"),
        ("CTo", [16, T], F32, "ExternalOutput"),
    ],
    "l1scan": [
        ("xcT", [4096, T], BF16, "ExternalInput"),
        ("dtT", [4096, T], F32, "ExternalInput"),
        ("BT", [16, T], F32, "ExternalInput"),
        ("CTo", [16, T], F32, "ExternalInput"),
        ("zT", [4096, T], BF16, "ExternalInput"),
        ("A_log", [4096, 16], F32, "ExternalInput"),
        ("D_skip", [4096], F32, "ExternalInput"),
        ("yfT", [4096, T], BF16, "ExternalOutput"),
    ],
    "l1out": [
        ("yfT", [4096, T], BF16, "ExternalInput"),
        ("outprojT", [4096, D], F32, "ExternalInput"),
        ("x2", [T, D], F32, "ExternalInput"),
        ("xmid", [T, D], F32, "ExternalOutput"),
    ],
    "l1ffA": [
        ("xmid", [T, D], F32, "ExternalInput"),
        ("ffprojT", [D, 8192], F32, "ExternalInput"),
        ("ln2w", [D], F32, "ExternalInput"),
        ("ln2b", [D], F32, "ExternalInput"),
        ("hsT", [4096, T], BF16, "ExternalOutput"),
    ],
    "l1ffB": [
        ("hsT", [4096, T], BF16, "ExternalInput"),
        ("ffoutT", [4096, D], F32, "ExternalInput"),
        ("x2", [T, D], F32, "ExternalInput"),
        ("xmid", [T, D], F32, "ExternalInput"),
        ("x3", [T, D], F32, "ExternalOutput"),
    ],
    "l0ffnA": [
        ("x1", [T, D], F32, "ExternalInput"),
        ("routerT", [D, 8], F32, "ExternalInput"),
        ("sgT", [D, 1408], F32, "ExternalInput"),
        ("suT", [D, 1408], F32, "ExternalInput"),
        ("wgT", [8, D, 1408], F32, "ExternalInput"),
        ("wuT", [8, D, 1408], F32, "ExternalInput"),
        ("hsT_all", [9 * 1408, T], BF16, "ExternalOutput"),
        ("comb_d", [8, T], F32, "Internal"),
    ],
    "l0ffnB": [
        ("hsT_all", [9 * 1408, T], BF16, "ExternalInput"),
        ("wdT_all", [9 * 1408, D], F32, "ExternalInput"),
        ("x1", [T, D], F32, "ExternalInput"),
        ("x2", [T, D], F32, "ExternalOutput"),
    ],
    "l0attnA": [
        ("x0", [T, D], F32, "ExternalInput"),
        ("Wq", [D, D], F32, "ExternalInput"),
        ("Wk", [D, D], F32, "ExternalInput"),
        ("Wv", [D, D], F32, "ExternalInput"),
        ("qrw", [128], F32, "ExternalInput"),
        ("krw", [128], F32, "ExternalInput"),
        ("qT", [D, T], BF16, "ExternalOutput"),
        ("kT", [D, T], BF16, "ExternalOutput"),
        ("vT", [D, T], BF16, "ExternalOutput"),
    ],
    "l0attnB": [
        ("qT", [D, T], BF16, "ExternalInput"),
        ("kT", [D, T], BF16, "ExternalInput"),
        ("vT", [D, T], BF16, "ExternalInput"),
        ("Wproj", [D, D], F32, "ExternalInput"),
        ("subw", [128], F32, "ExternalInput"),
        ("Rt", [16, T], F32, "ExternalInput"),
        ("x0", [T, D], F32, "ExternalInput"),
        ("x1", [T, D], F32, "ExternalOutput"),
    ],
}

def phase_l1ffn(nc, io, px="", stage=5):
    x3 = io["x3"]
    factors = io["factors"]
    bias = io["bias"]
    out = io["out"]

    CT = D // 128
    ET = D // 128
    TCH = 8
    TC = T // TCH
    NSUB = TC // 128
    NTI = CT * NSUB
    NTO = ET * NSUB

    es = contextlib.ExitStack()
    sb = lambda n, *a: es.enter_context(nc.sbuf_tensor(px + n, *a))
    ps = lambda n, *a: es.enter_context(nc.psum_tensor(px + n, *a))
    t_ws32 = [sb(f"ws32_{i}", [128, D], F32) for i in range(2)]
    t_wsbf = sb("wsbf", [128, CT * D], BF16)
    t_bias = sb("biascol", [128, ET], F32)
    t_id = sb("ident", [128, 128], F32)
    t_idb = sb("identb", [128, 128], BF16)
    t_x3tok = sb("x3tok", [128, NSUB * D], F32)
    t_x3T = sb("x3T", [128, CT * TC], BF16)
    t_mixT = sb("mixT", [128, ET * TC], BF16)
    t_s = sb("s_res", [128, NSUB * D], F32)
    t_stats = sb("stats", [128, 4 * 24], F32)
    t_mv = sb("mv", [128, 4 * 2], F32)
    st = {k: sb("st_" + k, [128, 4], F32) for k in ["rstd", "lnv"]}
    st["eps"] = sb("st_eps", [128, 1], F32)
    p_ti = [ps(f"p_ti{i}", [128, 512], F32) for i in range(2)]
    p_mm = [ps(f"p_mm{i}", [128, 512], F32) for i in range(2)]
    p_to = [ps(f"p_to{i}", [128, 1024], BF16) for i in range(2)]

    names = ["sWD", "sWC", "sB", "sX", "sTIr", "sTIe", "sMM", "sEV",
             "sTOr", "sTOc", "sv", "ssc", "sLN", "sOUT", "sID", "fG", "fV", "fS"]
    sms = {n: es.enter_context(nc.semaphore(px + n)) for n in names}
    (sWD, sWC, sB, sX, sTIr, sTIe, sMM, sEV, sTOr, sTOc, sv, ssc, sLN, sOUT,
     sID, fG, fV, fS) = (sms[n] for n in names)

    with es, nc.Block() as blk:
        @blk.gpsimd
        def _(g):
            g.memset(st["eps"][:], EPS)
            make_identities(nc, g, [t_id[:], t_idb[:]], fG)
            g.nop().then_inc(sID, 1)
            for c in range(CT):
                if c >= 2:
                    g.wait_ge(sWC, c - 1)
                for r in range(8):
                    if r > 0 or c > 0:
                        g.wait_ge(sWD, 16 * (c * 8 + r))
                    kw = {} if r == 0 else dict(accum_op=ALU.add)
                    g.dma_start(out=t_ws32[c % 2][:], in_=factors[r, ts(c, 128), :],
                                **kw).then_inc(sWD, 16)

        @blk.sync
        def _(s):
            for e in range(ET):
                s.dma_start(out=t_bias[:, e:e + 1],
                            in_=bias[ts(e, 128)].rearrange("(p o) -> p o", o=1)).then_inc(sB, 16)
            for ch in range(TCH):
                if ch > 0:
                    s.wait_ge(sLN, ch * NSUB)
                    s.wait_ge(sOUT, 16 * ch * NSUB)
                for tsub in range(NSUB):
                    s.dma_start(out=t_x3tok[:, tsub * D:(tsub + 1) * D],
                                in_=x3[ts(ch * NSUB + tsub, 128), :]).then_inc(sX, 16)
                for tsub in range(NSUB):
                    k = ch * NSUB + tsub
                    s.wait_ge(sLN, k + 1)
                    s.dma_start(out=out[ts(k, 128), :],
                                in_=t_x3tok[:, tsub * D:(tsub + 1) * D]).then_inc(sOUT, 16)
            s.wait_ge(sOUT, 16 * TCH * NSUB)

        @blk.tensor
        def _(t):
            t.wait_ge(sID, 1)
            for ch in range(TCH):
                t.wait_ge(sX, 16 * (ch + 1) * NSUB)
                for j in range(NTI):
                    tsub, ct = divmod(j, CT)
                    if ch * NTI + j >= 2:
                        t.wait_ge(sTIe, ch * NTI + j - 1)
                    t.transpose(p_ti[j % 2][:, 0:128],
                                t_x3tok[:, tsub * D + ct * 128: tsub * D + (ct + 1) * 128],
                                t_id[:]).then_inc(sTIr, 1)
                if stage < 2:
                    continue
                if ch == 0:
                    t.wait_ge(sWC, CT)
                t.wait_ge(sTIe, (ch + 1) * NTI)
                for et in range(ET):
                    if ch * ET + et >= 2:
                        t.wait_ge(sEV, ch * ET + et - 1)
                    for c in range(CT):
                        ins = t.matmul(p_mm[et % 2][:],
                                       t_wsbf[:, c * D + et * 128: c * D + (et + 1) * 128],
                                       t_x3T[:, c * TC:(c + 1) * TC],
                                       start=(c == 0), stop=(c == CT - 1))
                    ins.then_inc(sMM, 1)
                if stage < 3:
                    continue
                for j in range(NTO):
                    et, tsub = divmod(j, NSUB)
                    t.wait_ge(sEV, ch * ET + et + 1)
                    if ch * NTO + j >= 2:
                        t.wait_ge(sTOc, ch * NTO + j - 1)
                    t.transpose(p_to[j % 2][:, 0:128],
                                t_mixT[:, et * TC + tsub * 128: et * TC + (tsub + 1) * 128],
                                t_idb[:]).then_inc(sTOr, 1)

        @blk.scalar
        def _(s):
            s.wait_ge(sID, 1)
            for ch in range(TCH):
                for j in range(NTI):
                    tsub, ct = divmod(j, CT)
                    s.wait_ge(sTIr, ch * NTI + j + 1)
                    s.activation(
                        t_x3T[:, ct * TC + tsub * 128: ct * TC + (tsub + 1) * 128],
                        p_ti[j % 2][:, 0:128], AF.Copy).then_inc(sTIe, 1)
                if stage < 5:
                    continue
                s.wait_ge(sv, ch + 1)
                for tsub in range(NSUB):
                    ins = s.activation(st["lnv"][:, tsub:tsub + 1],
                                       t_mv[:, 2 * tsub + 1:2 * tsub + 2],
                                       AF.Ln, bias=st["eps"][:])
                    if tsub == NSUB - 1:
                        ins.then_inc(fS, 1)
                s.wait_ge(fS, ch + 1)
                for tsub in range(NSUB):
                    ins = s.activation(st["rstd"][:, tsub:tsub + 1],
                                       st["lnv"][:, tsub:tsub + 1], AF.Exp, scale=-0.5)
                    if tsub == NSUB - 1:
                        ins.then_inc(ssc, 1)

        @blk.vector
        def _(v):
            v.wait_ge(sB, 16 * ET)
            for c in range(CT):
                v.wait_ge(sWD, (c + 1) * 8 * 16)
                v.tensor_copy(t_wsbf[:, c * D:(c + 1) * D],
                              t_ws32[c % 2][:]).then_inc(sWC, 1)
            for ch in range(TCH):
                if stage < 3:
                    v.wait_ge(sTIe, (ch + 1) * NTI)
                    for tsub in range(NSUB):
                        v.engine_nop().then_inc(sLN, 1)
                if stage < 2:
                    continue
                for et in range(ET):
                    v.wait_ge(sMM, ch * ET + et + 1)
                    v.tensor_scalar(t_mixT[:, et * TC:(et + 1) * TC], p_mm[et % 2][:],
                                    1.0 / math.sqrt(8.0), t_bias[:, et:et + 1],
                                    ALU.mult, ALU.add).then_inc(sEV, 1)
                if stage < 3:
                    continue
                for j in range(NTO):
                    et, tsub = divmod(j, NSUB)
                    if j == 0 and stage >= 5:
                        v.wait_ge(ssc, ch)
                    v.wait_ge(sTOr, ch * NTO + j + 1)
                    ins = v.scalar_tensor_tensor(
                        t_s[:, tsub * D + et * 128: tsub * D + (et + 1) * 128],
                        t_x3tok[:, tsub * D + et * 128: tsub * D + (et + 1) * 128],
                        1.0, p_to[j % 2][:, 0:128],
                        ALU.mult, ALU.add)
                    ins.then_inc(sTOc, 1)
                    if j == NTO - 1:
                        v.wait_ge(sTOc, (ch + 1) * NTO)
                if stage < 4:
                    for tsub in range(NSUB):
                        v.engine_nop().then_inc(sLN, 1)
                    continue
                for tsub in range(NSUB):
                    for gix in range(4):
                        ins = v.bn_stats(
                            t_stats[:, tsub * 24 + gix * 6: tsub * 24 + gix * 6 + 6],
                            t_s[:, tsub * D + gix * 512: tsub * D + (gix + 1) * 512])
                        if tsub == NSUB - 1 and gix == 3:
                            ins.then_inc(fV, 1)
                v.wait_ge(fV, ch + 1)
                for tsub in range(NSUB):
                    ins_ = v.bn_aggr(t_mv[:, 2 * tsub:2 * tsub + 2],
                                     t_stats[:, tsub * 24:(tsub + 1) * 24])
                    if tsub == NSUB - 1:
                        ins_.then_inc(sv, 1)
                if stage < 5:
                    for tsub in range(NSUB):
                        v.engine_nop().then_inc(sLN, 1)
                    continue
                v.wait_ge(ssc, ch + 1)
                for tsub in range(NSUB):
                    s_ap = t_s[:, tsub * D:(tsub + 1) * D]
                    o_ap = t_x3tok[:, tsub * D:(tsub + 1) * D]
                    v.tensor_scalar(o_ap, s_ap, t_mv[:, 2 * tsub:2 * tsub + 1],
                                    st["rstd"][:, tsub:tsub + 1],
                                    ALU.subtract, ALU.mult).then_inc(sLN, 1)

    return nc


def build_l1ffn(stage=5):
    nc = bass.Bass("TRN2", target_bir_lowering=False)
    io = {}
    for name, shape, dt, kind in PHASE_IO["l1ffn"]:
        io[name] = nc.dram_tensor(name, shape, dt, kind=kind).ap()
    phase_l1ffn(nc, io, stage=stage)
    return nc

def kernel(**inputs):
    raise NotImplementedError("phases under construction")


def phase_l1preA(nc, io, px=""):
    x2 = io["x2"]
    inprojT = io["inprojT"]
    convw = io["convw"]
    convb = io["convb"]
    ln1w = io["ln1w"]
    ln1b = io["ln1b"]
    xcT = io["xcT"]
    zT = io["zT"]

    CT = D // 128
    NTT = T // 128
    NDG = 32
    NTCH = 8
    NCONST = 16 * 2 + 1 + 32

    es = contextlib.ExitStack()
    sb = lambda n, *a: es.enter_context(nc.sbuf_tensor(px + n, *a))
    ps = lambda n, *a: es.enter_context(nc.psum_tensor(px + n, *a))
    t_id = sb("ident", [128, 128], F32)
    t_x2 = [sb(f"x2b{i}", [128, D], F32) for i in range(2)]
    t_u0 = sb("u0b", [128, D], F32)
    t_u1T = sb("u1T", [128, CT * T], BF16)
    t_stats = sb("stats", [128, 24], F32)
    t_mv = sb("mv", [128, 2], F32)
    t_lnv = sb("lnv", [128, 1], F32)
    t_rstd = sb("rstd", [128, 1], F32)
    t_eps = sb("eps", [128, 1], F32)
    t_lnw = sb("lnw", [128, CT], F32)
    t_lnb = sb("lnb", [128, CT], F32)
    t_W = [sb(f"Wc{i}", [128, 16 * 256], BF16) for i in range(2)]
    t_slab = [sb(f"slab{i}", [128, T + 4], BF16) for i in range(2)]
    t_acc = [sb(f"acc{i}", [128, 1024], F32) for i in range(2)]
    t_xc = [sb(f"xcsb{i}", [128, 1024], BF16) for i in range(2)]
    t_z = [sb(f"zb{i}", [128, 512], BF16) for i in range(2)]
    t_cw = sb("cwcol", [128, 4 * 32], F32)
    t_cb = sb("cbcol", [128, 32], F32)
    p4 = [ps(f"pb{i}", [128, 1024], F32) for i in range(4)]

    names = ["sX", "svA", "sscA", "sAP", "sTIr", "sTIe", "sID", "sMMb",
             "sEVb", "svB", "sscB", "fV", "fS", "fG",
             "sXa", "sXb", "sWa", "sWb", "sZa", "sZb", "sXCa", "sXCb"]
    sms = {n: es.enter_context(nc.semaphore(px + n)) for n in names}
    (sX, svA, sscA, sAP, sTIr, sTIe, sID, sMMb, sEVb, svB, sscB,
     fV, fS, fG, sXa, sXb, sWa, sWb, sZa, sZb, sXCa, sXCb) = (
        sms[n] for n in names)
    sX2 = [sXa, sXb]
    sW2 = [sWa, sWb]
    sZ2 = [sZa, sZb]
    sXC2 = [sXCa, sXCb]

    z_list = []
    for dg in range(16, NDG):
        for tch in range(NTCH):
            for ab in range(2):
                gi = dg * NTCH + tch
                z_list.append((dg, tch, ab, 2 * gi + ab + 1))

    def pslot(i):
        return p4[i // 2][:, (i % 2) * 512:(i % 2 + 1) * 512]

    with es, nc.Block() as blk:
        @blk.gpsimd
        def _(g):
            g.memset(t_eps[:], EPS)
            g.memset(t_slab[0][:, 0:3], 0.0)
            g.memset(t_slab[1][:, 0:3], 0.0)
            make_identities(nc, g, [t_id[:]], fG)
            g.nop().then_inc(sID, 1)
            for dg in range(NDG):
                if dg >= 2:
                    g.wait_ge(sMMb, 2 * NTCH * (dg - 1))
                g.dma_start(
                    out=t_W[dg % 2][:],
                    in_=inprojT.rearrange("(c p) n -> p c n", p=128)[
                        :, :, dg * 256:(dg + 1) * 256]
                ).then_inc(sW2[dg % 2], 16)


        @blk.sync
        def _(s):
            for c in range(CT):
                s.dma_start(out=t_lnw[:, c:c + 1],
                            in_=ln1w[ts(c, 128)].rearrange("(p o) -> p o", o=1)).then_inc(sX, 16)
                s.dma_start(out=t_lnb[:, c:c + 1],
                            in_=ln1b[ts(c, 128)].rearrange("(p o) -> p o", o=1)).then_inc(sX, 16)
            s.dma_start(out=t_cw[:],
                        in_=convw.rearrange("(i p) j -> p i j", p=128)).then_inc(sX, 16)
            for i_ in range(32):
                s.dma_start(out=t_cb[:, i_:i_ + 1],
                            in_=convb[ts(i_, 128)].rearrange("(p o) -> p o", o=1)
                            ).then_inc(sX, 16)
            for i in range(NTT):
                if i >= 2:
                    s.wait_ge(sAP, i - 1)
                s.dma_start(out=t_x2[i % 2][:],
                            in_=x2[ts(i, 128), :]).then_inc(sX2[i % 2], 16)
            for dt_ in range(32):
                for h in range(4):
                    q = 4 * dt_ + h
                    s.wait_ge(sscB, q + 1)
                    s.dma_start(out=xcT[ts(dt_, 128), h * 1024:(h + 1) * 1024],
                                in_=t_xc[q % 2][:]).then_inc(sXC2[q % 2], 16)
            for zi, (dg, tch, ab, ev) in enumerate(z_list):
                s.wait_ge(sEVb, ev)
                ztile = 2 * dg + ab - 32
                s.dma_start(out=zT[ts(ztile, 128), tch * 512:(tch + 1) * 512],
                            in_=t_z[ab][:]).then_inc(sZ2[ab], 16)
            s.wait_ge(sXCa, 64 * 16)
            s.wait_ge(sXCb, 64 * 16)
            s.wait_ge(sZa, len(z_list) // 2 * 16)
            s.wait_ge(sZb, len(z_list) // 2 * 16)

        @blk.tensor
        def _(t):
            t.wait_ge(sID, 1)
            for i in range(NTT):
                t.wait_ge(sAP, i + 1)
                for c in range(CT):
                    j = i * CT + c
                    if j >= 2:
                        t.wait_ge(sTIe, j - 1)
                    t.transpose(p4[j % 2][:, 0:128],
                                t_u0[:, c * 128:(c + 1) * 128],
                                t_id[:]).then_inc(sTIr, 1)
            t.wait_ge(sTIe, NTT * CT)
            for dg in range(NDG):
                t.wait_ge(sW2[dg % 2], 16 * (dg // 2 + 1))
                for tch in range(NTCH):
                    gi = dg * NTCH + tch
                    pp = gi % 2
                    if gi >= 2:
                        t.wait_ge(sEVb, 2 * (gi - 1))
                    for c in range(CT):
                        for ab in range(2):
                            ins = t.matmul(
                                pslot(pp * 2 + ab),
                                t_W[dg % 2][:, c * 256 + ab * 128: c * 256 + (ab + 1) * 128],
                                t_u1T[:, c * T + tch * 512: c * T + (tch + 1) * 512],
                                start=(c == 0), stop=(c == CT - 1))
                            if c == CT - 1:
                                ins.then_inc(sMMb, 1)

        @blk.scalar
        def _(s):
            s.wait_ge(sID, 1)
            s.wait_ge(sX, NCONST * 16)
            cS = Chain(s, fS)
            for i in range(NTT):
                s.wait_ge(svA, i + 1)
                cS.wait(); cS.op(s.activation(t_lnv[:], t_mv[:, 1:2], AF.Ln, bias=t_eps[:]))
                cS.wait()
                s.activation(t_rstd[:], t_lnv[:], AF.Exp, scale=-0.5).then_inc(sscA, 1)
                s.wait_ge(sscA, i + 1)
            for dt_ in range(32):
                for h in range(4):
                    q = 4 * dt_ + h
                    if q >= 2:
                        s.wait_ge(sXC2[q % 2], 16 * (q // 2))
                    s.wait_ge(svB, q + 1)
                    s.activation(t_xc[q % 2][:], t_acc[1][:], AF.Silu,
                                 bias=t_cb[:, dt_:dt_ + 1]).then_inc(sscB, 1)

        @blk.vector
        def _(v):
            v.wait_ge(sX, NCONST * 16)
            cV = Chain(v, fV)
            for i in range(NTT):
                v.wait_ge(sX2[i % 2], (i // 2 + 1) * 16)
                v.wait_ge(svA, i)
                for gix in range(4):
                    cV.op(v.bn_stats(t_stats[:, gix * 6:(gix + 1) * 6],
                                     t_x2[i % 2][:, gix * 512:(gix + 1) * 512]))
                cV.wait()
                v.bn_aggr(t_mv[:], t_stats[:]).then_inc(svA, 1)
                v.wait_ge(sscA, i + 1)
                if i >= 1:
                    v.wait_ge(sTIr, i * CT)
                v.wait_ge(fV, cV.n)
                v.tensor_scalar(t_u0[:], t_x2[i % 2][:], t_mv[:, 0:1],
                                t_rstd[:], ALU.subtract, ALU.mult).then_inc(sAP, 1)
                for c in range(CT):
                    j = i * CT + c
                    v.wait_ge(sTIr, j + 1)
                    v.tensor_scalar(t_u1T[:, c * T + i * 128: c * T + (i + 1) * 128],
                                    p4[j % 2][:, 0:128], t_lnw[:, c:c + 1],
                                    t_lnb[:, c:c + 1], ALU.mult,
                                    ALU.add).then_inc(sTIe, 1)
            nzb = [0, 0]
            for dg in range(NDG):
                for tch in range(NTCH):
                    gi = dg * NTCH + tch
                    pp = gi % 2
                    for ab in range(2):
                        dout = 2 * dg + ab
                        v.wait_ge(sMMb, 2 * gi + ab + 1)
                        if dout < 32:
                            v.tensor_copy(
                                t_slab[ab][:, 3 + tch * 512: 3 + (tch + 1) * 512],
                                pslot(pp * 2 + ab)).then_inc(sEVb, 1)
                        else:
                            nzb[ab] += 1
                            if nzb[ab] > 1:
                                v.wait_ge(sZ2[ab], 16 * (nzb[ab] - 1))
                            v.tensor_copy(t_z[ab][:],
                                          pslot(pp * 2 + ab)).then_inc(sEVb, 1)
                if 2 * dg < 32:
                    v.wait_ge(sEVb, 2 * (dg * NTCH + NTCH - 1) + 2)
                    for ab in range(2):
                        dt_ = 2 * dg + ab
                        sl = t_slab[ab]
                        for h in range(4):
                            base = h * 1024
                            cV.wait()
                            cV.op(v.tensor_scalar(
                                t_acc[0][:], sl[:, base:base + 1024],
                                t_cw[:, 4 * dt_:4 * dt_ + 1], None, ALU.mult))
                            cV.wait()
                            cV.op(v.scalar_tensor_tensor(
                                t_acc[1][:], sl[:, base + 1:base + 1 + 1024],
                                t_cw[:, 4 * dt_ + 1:4 * dt_ + 2], t_acc[0][:],
                                ALU.mult, ALU.add))
                            cV.wait()
                            cV.op(v.scalar_tensor_tensor(
                                t_acc[0][:], sl[:, base + 2:base + 2 + 1024],
                                t_cw[:, 4 * dt_ + 2:4 * dt_ + 3], t_acc[1][:],
                                ALU.mult, ALU.add))
                            cV.wait()
                            ins = v.scalar_tensor_tensor(
                                t_acc[1][:], sl[:, base + 3:base + 3 + 1024],
                                t_cw[:, 4 * dt_ + 3:4 * dt_ + 4], t_acc[0][:],
                                ALU.mult, ALU.add)
                            ins.then_inc(svB, 1)
                            v.wait_ge(sscB, 4 * dt_ + h + 1)

    return nc




def build_l1preA():
    nc = bass.Bass("TRN2", target_bir_lowering=False)
    io = {}
    for name, shape, dt, kind in PHASE_IO["l1preA"]:
        io[name] = nc.dram_tensor(name, shape, dt, kind=kind).ap()
    phase_l1preA(nc, io)
    return nc

def phase_l1preB(nc, io, px=""):
    xcT = io["xcT"]
    xprojT = io["xprojT"]
    dtprojT = io["dtprojT"]
    dtprojb = io["dtprojb"]
    dtT = io["dtT"]
    BT = io["BT"]
    CTo = io["CTo"]

    NTCH = 8
    TC = T // NTCH

    es = contextlib.ExitStack()
    sb = lambda n, *a: es.enter_context(nc.sbuf_tensor(px + n, *a))
    ps = lambda n, *a: es.enter_context(nc.psum_tensor(px + n, *a))
    t_xp = sb("xpc", [128, 32 * 160], BF16)
    t_dp = sb("dpc", [128, 4096], BF16)
    t_db = sb("dbcol", [128, 32], F32)
    t_one = sb("onecol", [128, 1], F32)
    t_xcb = [sb(f"xcb{i}", [128, TC], BF16) for i in range(2)]
    t_draw = sb("draw", [128, TC], BF16)
    t_bc = sb("bcsb", [128, TC], F32)
    t_y = sb("y", [128, TC], F32)
    t_ny = sb("ny", [128, TC], F32)
    t_ay = sb("ay", [128, TC], F32)
    t_e = sb("e", [128, TC], F32)
    t_l = sb("l", [128, TC], F32)
    t_r = sb("r", [128, TC], F32)
    t_do = [sb(f"dto{i}", [128, TC], F32) for i in range(2)]
    p4 = [ps(f"pb{i}", [128, 1024], F32) for i in range(4)]

    def pslot(i):
        return p4[i // 2][:, (i % 2) * 512:(i % 2 + 1) * 512]

    names = ["sC", "sDB", "sMMx", "sEVx", "sBC", "sMMd", "sPD", "svP", "sscP",
             "fV", "fS", "sDTa", "sDTb", "sXca", "sXcb"]
    sms = {n: es.enter_context(nc.semaphore(px + n)) for n in names}
    (sC, sDB, sMMx, sEVx, sBC, sMMd, sPD, svP, sscP, fV, fS, sDTa, sDTb,
     sXca, sXcb) = (sms[n] for n in names)
    sDT2 = [sDTa, sDTb]
    sXc2 = [sXca, sXcb]

    with es, nc.Block() as blk:
        @blk.gpsimd
        def _(g):
            g.memset(t_one[:], 1.0)
            g.dma_start(out=t_xp[:],
                        in_=xprojT.rearrange("(i p) n -> p i n", p=128)).then_inc(sC, 16)
            g.dma_start(out=t_dp[:], in_=dtprojT).then_inc(sC, 16)

        @blk.sync
        def _(s):
            for i_ in range(32):
                s.dma_start(out=t_db[:, i_:i_ + 1],
                            in_=dtprojb[ts(i_, 128)].rearrange("(p o) -> p o", o=1)
                            ).then_inc(sDB, 16)
            for tch in range(NTCH):
                for dt_ in range(32):
                    li = tch * 32 + dt_
                    if li >= 2:
                        s.wait_ge(sMMx, li - 1)
                    s.dma_start(out=t_xcb[li % 2][:],
                                in_=xcT[ts(dt_, 128), tch * TC:(tch + 1) * TC]
                                ).then_inc(sXc2[li % 2], 16)
                s.wait_ge(sEVx, 2 * tch + 2)
                s.dma_start(out=BT[:, tch * TC:(tch + 1) * TC],
                            in_=t_bc[0:16, :]).then_inc(sBC, 16)
                s.dma_start(out=CTo[:, tch * TC:(tch + 1) * TC],
                            in_=t_bc[16:32, :]).then_inc(sBC, 16)
                for do in range(32):
                    q = tch * 32 + do
                    s.wait_ge(svP, 2 * q + 2)
                    s.dma_start(out=dtT[ts(do, 128), tch * TC:(tch + 1) * TC],
                                in_=t_do[q % 2][:]).then_inc(sDT2[q % 2], 16)
            s.wait_ge(sDTa, 128 * 16)
            s.wait_ge(sDTb, 128 * 16)
            s.wait_ge(sBC, NTCH * 2 * 16)

        @blk.tensor
        def _(t):
            t.wait_ge(sC, 32)
            for tch in range(NTCH):
                if tch >= 1:
                    t.wait_ge(sEVx, 2 * tch)
                for dt_ in range(32):
                    li = tch * 32 + dt_
                    t.wait_ge(sXc2[li % 2], 16 * (li // 2 + 1))
                    t.matmul(pslot(0), t_xp[:, dt_ * 160: dt_ * 160 + 128],
                             t_xcb[li % 2][:], start=(dt_ == 0), stop=(dt_ == 31))
                    ins = t.matmul(pslot(1)[0:32, :],
                                   t_xp[:, dt_ * 160 + 128: dt_ * 160 + 160],
                                   t_xcb[li % 2][:], start=(dt_ == 0), stop=(dt_ == 31))
                    ins.then_inc(sMMx, 1)
                t.wait_ge(sEVx, 2 * tch + 1)
                for do in range(32):
                    q = tch * 32 + do
                    if q >= 2:
                        t.wait_ge(sPD, q - 1)
                    t.matmul(pslot(2 + q % 2), t_dp[:, do * 128:(do + 1) * 128],
                             t_draw[:], start=True, stop=True).then_inc(sMMd, 1)

        @blk.scalar
        def _(s):
            cS = Chain(s, fS)
            for tch in range(NTCH):
                for do in range(32):
                    q = tch * 32 + do
                    s.wait_ge(svP, 2 * q + 1)
                    cS.wait()
                    cS.op(s.activation(t_e[:], t_ay[:], AF.Exp, scale=-1.0))
                    cS.wait()
                    s.activation(t_l[:], t_e[:], AF.Ln,
                                 bias=t_one[:]).then_inc(sscP, 1)
                    s.wait_ge(sscP, q + 1)

        @blk.vector
        def _(v):
            v.wait_ge(sDB, 32 * 16)
            cV = Chain(v, fV)
            for tch in range(NTCH):
                v.wait_ge(sMMx, 32 * (tch + 1))
                if tch >= 1:
                    v.wait_ge(sMMd, 32 * tch)
                v.tensor_copy(t_draw[:], pslot(0)).then_inc(sEVx, 1)
                if tch >= 1:
                    v.wait_ge(sBC, 2 * 16 * tch)
                v.tensor_copy(t_bc[0:32, :], pslot(1)[0:32, :]).then_inc(sEVx, 1)
                for do in range(32):
                    q = tch * 32 + do
                    v.wait_ge(sMMd, q + 1)
                    if q >= 2:
                        v.wait_ge(sDT2[q % 2], 16 * (q // 2))
                    cV.wait()
                    cV.op(v.tensor_scalar(t_y[:], pslot(2 + q % 2),
                                          t_db[:, do:do + 1], None, ALU.add))
                    v.wait_ge(fV, cV.n)
                    v.engine_nop().then_inc(sPD, 1)
                    cV.wait()
                    cV.op(v.tensor_scalar(t_ny[:], t_y[:], -1.0, None, ALU.mult))
                    cV.wait()
                    ins = v.tensor_tensor(t_ay[:], t_y[:], t_ny[:], ALU.max)
                    ins.then_inc(svP, 1)
                    cV.n += 0
                    cV.wait()
                    cV.op(v.tensor_scalar(t_r[:], t_y[:], 0.0, None, ALU.max))
                    v.wait_ge(sscP, q + 1)
                    cV.wait()
                    ins = v.tensor_tensor(t_do[q % 2][:], t_r[:], t_l[:], ALU.add)
                    ins.then_inc(svP, 1)
                    v.wait_ge(svP, 2 * q + 2)

    return nc




def build_l1preB():
    nc = bass.Bass("TRN2", target_bir_lowering=False)
    io = {}
    for name, shape, dt, kind in PHASE_IO["l1preB"]:
        io[name] = nc.dram_tensor(name, shape, dt, kind=kind).ap()
    phase_l1preB(nc, io)
    return nc

def phase_l1scan(nc, io, px=""):
    xcT = io["xcT"]
    dtT = io["dtT"]
    BT = io["BT"]
    CTo = io["CTo"]
    zT = io["zT"]
    A_log = io["A_log"]
    D_skip = io["D_skip"]
    yfT = io["yfT"]

    NTC = 4
    TC = 1024
    ND = 32
    NS = 16

    es = contextlib.ExitStack()
    sb = lambda n, *a: es.enter_context(nc.sbuf_tensor(px + n, *a))
    ps = lambda n, *a: es.enter_context(nc.psum_tensor(px + n, *a))
    t_Araw = sb("Araw", [128, 512], F32)
    t_Ae = sb("Ae", [128, 512], F32)
    t_A = sb("Acol", [128, 512], F32)
    t_D = sb("Dcol", [128, 32], F32)
    t_ones = sb("ones1", [1, 128], F32)
    t_state = sb("state", [128, 512], F32)
    t_row = [sb(f"row{i}", [1, TC], F32) for i in range(2)]
    t_Bbc = sb("Bbc", [128, NS * TC], BF16)
    t_Cbc = sb("Cbc", [128, NS * TC], BF16)
    t_h = sb("hall", [128, NS * TC], BF16)
    t_dt = [sb(f"dtb{i}", [128, TC], F32) for i in range(2)]
    t_xc = [sb(f"xcb{i}", [128, TC], BF16) for i in range(2)]
    t_z = [sb(f"zb{i}", [128, TC], BF16) for i in range(2)]
    t_u = sb("u", [128, TC], BF16)
    t_dA = [sb(f"dA{i}", [128, TC], F32) for i in range(2)]
    t_uB = sb("uB", [128, TC], BF16)
    t_g = sb("g", [128, TC], BF16)
    t_y = [sb(f"y{i}", [128, TC], F32) for i in range(2)]
    t_sz = sb("sz", [128, TC], BF16)
    t_yf = [sb(f"yf{i}", [128, TC], BF16) for i in range(2)]
    p4 = [ps(f"pb{i}", [128, 1024], F32) for i in range(2)]

    def pslot(i):
        return p4[i // 2][:, (i % 2) * 512:(i % 2 + 1) * 512]

    names = ["sAC", "sDC", "sST", "sBRa", "sBRb", "sMMB", "sEVB", "sLa", "sLb",
             "sZa", "sZb", "svS", "sscS", "szS", "svY", "sYFa", "sYFb",
             "fV", "fS", "sAr"]
    sms = {n: es.enter_context(nc.semaphore(px + n)) for n in names}
    (sAC, sDC, sST, sBRa, sBRb, sMMB, sEVB, sLa, sLb, sZa, sZb, svS, sscS, szS,
     svY, sYFa, sYFb, fV, fS, sAr) = (sms[n] for n in names)
    sBR2 = [sBRa, sBRb]
    sL2 = [sLa, sLb]
    sZ2 = [sZa, sZb]
    sYF2 = [sYFa, sYFb]

    with es, nc.Block() as blk:
        @blk.gpsimd
        def _(g):
            g.memset(t_ones[:], 1.0)
            g.memset(t_state[:], 0.0)
            g.nop().then_inc(sST, 1)
            for i_ in range(32):
                g.dma_start(out=t_Araw[:, i_ * 16:(i_ + 1) * 16],
                            in_=A_log[ts(i_, 128), :]).then_inc(sAC, 16)
            for i_ in range(32):
                g.dma_start(out=t_D[:, i_:i_ + 1],
                            in_=D_skip[ts(i_, 128)].rearrange("(p o) -> p o", o=1)
                            ).then_inc(sDC, 16)
            for tc in range(NTC):
                for s_ in range(NS):
                    for bc in range(2):
                        j = (tc * NS + s_) * 2 + bc
                        if j >= 2:
                            g.wait_ge(sMMB, 2 * (j - 1))
                        srcrow = BT if bc == 0 else CTo
                        g.dma_start(out=t_row[j % 2][:],
                                    in_=srcrow[s_:s_ + 1, tc * TC:(tc + 1) * TC]
                                    ).then_inc(sBR2[j % 2], 16)

        @blk.sync
        def _(s):
            def store_yf(bj):
                tcj, dtj = divmod(bj, ND)
                s.wait_ge(svY, bj + 1)
                s.dma_start(out=yfT[ts(dtj, 128), tcj * TC:(tcj + 1) * TC],
                            in_=t_yf[bj % 2][:]).then_inc(sYF2[bj % 2], 16)

            for tc in range(NTC):
                for dt_ in range(ND):
                    bi = tc * ND + dt_
                    if bi >= 2:
                        s.wait_ge(svY, bi - 1)
                        s.wait_ge(sscS, 16 * (bi - 1))
                        s.wait_ge(szS, bi - 1)
                    s.dma_start(out=t_dt[bi % 2][:],
                                in_=dtT[ts(dt_, 128), tc * TC:(tc + 1) * TC]
                                ).then_inc(sL2[bi % 2], 16)
                    s.dma_start(out=t_xc[bi % 2][:],
                                in_=xcT[ts(dt_, 128), tc * TC:(tc + 1) * TC]
                                ).then_inc(sL2[bi % 2], 16)
                    s.dma_start(out=t_z[bi % 2][:],
                                in_=zT[ts(dt_, 128), tc * TC:(tc + 1) * TC]
                                ).then_inc(sZ2[bi % 2], 16)
                    if bi >= 1:
                        store_yf(bi - 1)
            store_yf(NTC * ND - 1)
            s.wait_ge(sYFa, 64 * 16)
            s.wait_ge(sYFb, 64 * 16)

        @blk.tensor
        def _(t):
            for tc in range(NTC):
                for s_ in range(NS):
                    for bc in range(2):
                        j = (tc * NS + s_) * 2 + bc
                        t.wait_ge(sBR2[j % 2], 16 * (j // 2 + 1))
                        for hh in range(2):
                            k = 2 * j + hh
                            if k >= 2:
                                t.wait_ge(sEVB, k - 1)
                            t.matmul(pslot(k % 2), t_ones[:],
                                     t_row[j % 2][0:1, hh * 512:(hh + 1) * 512],
                                     start=True, stop=True).then_inc(sMMB, 1)

        @blk.scalar
        def _(s):
            cS = Chain(s, fS)
            s.wait_ge(sAC, 32 * 16)
            cS.wait(); cS.op(s.activation(t_Ae[:], t_Araw[:], AF.Exp))
            cS.wait()
            s.mul(t_A[:], t_Ae[:], -1.0).then_inc(sAr, 1)
            s.wait_ge(sAr, 1)
            for tc in range(NTC):
                for dt_ in range(ND):
                    bi = tc * ND + dt_
                    s.wait_ge(sL2[bi % 2], 32 * (bi // 2 + 1))
                    for s_ in range(NS):
                        k = bi * 16 + s_
                        if s_ >= 2 or bi >= 1:
                            s.wait_ge(svS, k - 1)
                        s.activation(t_dA[s_ % 2][:], t_dt[bi % 2][:], AF.Exp,
                                     scale=t_A[:, dt_ * 16 + s_: dt_ * 16 + s_ + 1]
                                     ).then_inc(sscS, 1)
                    s.wait_ge(sZ2[bi % 2], 16 * (bi // 2 + 1))
                    if bi >= 1:
                        s.wait_ge(svY, bi)
                    s.activation(t_sz[:], t_z[bi % 2][:], AF.Silu).then_inc(szS, 1)

        @blk.vector
        def _(v):
            cV = Chain(v, fV)
            v.wait_ge(sDC, 32 * 16)
            v.wait_ge(sST, 1)
            for tc in range(NTC):
                for s_ in range(NS):
                    for bc in range(2):
                        dst = t_Bbc if bc == 0 else t_Cbc
                        for hh in range(2):
                            k = 2 * ((tc * NS + s_) * 2 + bc) + hh
                            v.wait_ge(sMMB, k + 1)
                            v.tensor_copy(
                                dst[:, s_ * TC + hh * 512: s_ * TC + (hh + 1) * 512],
                                pslot(k % 2)).then_inc(sEVB, 1)
                v.wait_ge(sEVB, 64 * (tc + 1))
                for dt_ in range(ND):
                    bi = tc * ND + dt_
                    v.wait_ge(sL2[bi % 2], 32 * (bi // 2 + 1))
                    cV.wait()
                    cV.op(v.tensor_tensor(t_u[:], t_dt[bi % 2][:], t_xc[bi % 2][:],
                                          ALU.mult))
                    for s_ in range(NS):
                        k = bi * 16 + s_
                        v.wait_ge(sscS, k + 1)
                        cV.wait()
                        cV.op(v.tensor_tensor(t_uB[:], t_u[:],
                                              t_Bbc[:, s_ * TC:(s_ + 1) * TC],
                                              ALU.mult))
                        cV.wait()
                        ins = v.tensor_tensor_scan(
                            t_h[:, s_ * TC:(s_ + 1) * TC], t_dA[s_ % 2][:],
                            t_uB[:], t_state[:, dt_ * 16 + s_: dt_ * 16 + s_ + 1],
                            ALU.mult, ALU.add)
                        ins.then_inc(svS, 1)
                        cV.n += 0
                        v.wait_ge(svS, k + 1)
                        if s_ == 0:
                            cV.op(v.tensor_tensor(t_y[0][:],
                                                  t_h[:, 0:TC],
                                                  t_Cbc[:, 0:TC], ALU.mult))
                        else:
                            cV.wait()
                            cV.op(v.tensor_tensor(t_g[:],
                                                  t_h[:, s_ * TC:(s_ + 1) * TC],
                                                  t_Cbc[:, s_ * TC:(s_ + 1) * TC],
                                                  ALU.mult))
                            cV.wait()
                            cV.op(v.tensor_tensor(t_y[s_ % 2][:], t_y[(s_ - 1) % 2][:],
                                                  t_g[:], ALU.mult if False else ALU.add))
                    cV.wait()
                    cV.op(v.tensor_copy(t_state[:, dt_ * 16:(dt_ + 1) * 16],
                                        t_h[:, TC - 1::TC]))
                    cV.wait()
                    cV.op(v.scalar_tensor_tensor(t_y[(NS - 1) % 2 ^ 1][:],
                                                 t_xc[bi % 2][:],
                                                 t_D[:, dt_:dt_ + 1],
                                                 t_y[(NS - 1) % 2][:],
                                                 ALU.mult, ALU.add))
                    v.wait_ge(szS, bi + 1)
                    if bi >= 2:
                        v.wait_ge(sYF2[bi % 2], 16 * (bi // 2))
                    cV.wait()
                    ins = v.tensor_tensor(t_yf[bi % 2][:], t_y[(NS - 1) % 2 ^ 1][:],
                                          t_sz[:], ALU.mult)
                    ins.then_inc(svY, 1)
                    v.wait_ge(svY, bi + 1)

    return nc




def build_l1scan():
    nc = bass.Bass("TRN2", target_bir_lowering=False)
    io = {}
    for name, shape, dt, kind in PHASE_IO["l1scan"]:
        io[name] = nc.dram_tensor(name, shape, dt, kind=kind).ap()
    phase_l1scan(nc, io)
    return nc

def phase_l1out(nc, io, px=""):
    yfT = io["yfT"]
    outprojT = io["outprojT"]
    x2 = io["x2"]
    xmid = io["xmid"]

    NTCH = 8
    TC = T // NTCH
    NSUB = TC // 128
    ET = D // 128
    NC_ = 32

    es = contextlib.ExitStack()
    sb = lambda n, *a: es.enter_context(nc.sbuf_tensor(px + n, *a))
    ps = lambda n, *a: es.enter_context(nc.psum_tensor(px + n, *a))
    t_id = sb("ident", [128, 128], F32)
    t_yf = sb("yfc", [128, NC_ * TC], BF16)
    t_W = [sb(f"W{i}", [128, NC_ * 128], BF16) for i in range(2)]
    t_mixT = sb("mixT", [128, ET * TC], F32)
    t_x2t = sb("x2t", [128, NSUB * D], F32)
    p4 = [ps(f"pb{i}", [128, 1024], F32) for i in range(2)]

    def pslot(i):
        return p4[i // 2][:, (i % 2) * 512:(i % 2 + 1) * 512]

    names = ["sYL", "sWa", "sWb", "sMM", "sEV", "sX2", "sTOr", "sTOc",
             "sRES", "sOUT", "sID", "fG"]
    sms = {n: es.enter_context(nc.semaphore(px + n)) for n in names}
    (sYL, sWa, sWb, sMM, sEV, sX2, sTOr, sTOc, sRES, sOUT, sID, fG) = (
        sms[n] for n in names)
    sW2 = [sWa, sWb]

    with es, nc.Block() as blk:
        @blk.gpsimd
        def _(g):
            make_identities(nc, g, [t_id[:]], fG)
            g.nop().then_inc(sID, 1)
            for tch in range(NTCH):
                for cout in range(ET):
                    wi = tch * ET + cout
                    if wi >= 2:
                        g.wait_ge(sMM, wi - 1)
                    g.dma_start(
                        out=t_W[wi % 2][:],
                        in_=outprojT.rearrange("(i p) n -> p i n", p=128)[
                            :, :, cout * 128:(cout + 1) * 128]
                    ).then_inc(sW2[wi % 2], 16)

        @blk.sync
        def _(s):
            for tch in range(NTCH):
                if tch >= 1:
                    s.wait_ge(sMM, ET * tch)
                for ct in range(NC_):
                    s.dma_start(out=t_yf[:, ct * TC:(ct + 1) * TC],
                                in_=yfT[ts(ct, 128), tch * TC:(tch + 1) * TC]
                                ).then_inc(sYL, 16)
                if tch >= 1:
                    s.wait_ge(sOUT, 16 * NSUB * tch)
                for tsub in range(NSUB):
                    s.dma_start(out=t_x2t[:, tsub * D:(tsub + 1) * D],
                                in_=x2[ts(tch * NSUB + tsub, 128), :]
                                ).then_inc(sX2, 16)
                for tsub in range(NSUB):
                    k = tch * NSUB + tsub
                    s.wait_ge(sTOc, 64 * (tch + 1))
                    s.dma_start(out=xmid[ts(k, 128), :],
                                in_=t_x2t[:, tsub * D:(tsub + 1) * D]
                                ).then_inc(sOUT, 16)
            s.wait_ge(sOUT, 16 * NTCH * NSUB)

        @blk.tensor
        def _(t):
            t.wait_ge(sID, 1)
            for tch in range(NTCH):
                t.wait_ge(sYL, 16 * NC_ * (tch + 1))
                for cout in range(ET):
                    wi = tch * ET + cout
                    t.wait_ge(sW2[wi % 2], 16 * (wi // 2 + 1))
                    if wi >= 2:
                        t.wait_ge(sEV, wi - 1)
                    for ct in range(NC_):
                        ins = t.matmul(pslot(wi % 2),
                                       t_W[wi % 2][:, ct * 128:(ct + 1) * 128],
                                       t_yf[:, ct * TC:(ct + 1) * TC],
                                       start=(ct == 0), stop=(ct == NC_ - 1))
                    ins.then_inc(sMM, 1)
                for j in range(ET * NSUB):
                    et, tsub = divmod(j, NSUB)
                    jj = tch * ET * NSUB + j
                    t.wait_ge(sEV, tch * ET + et + 1)
                    if jj >= 2:
                        t.wait_ge(sTOc, jj - 1)
                    t.transpose(pslot(2 + jj % 2)[:, 0:128],
                                t_mixT[:, et * TC + tsub * 128: et * TC + (tsub + 1) * 128],
                                t_id[:]).then_inc(sTOr, 1)

        @blk.vector
        def _(v):
            for tch in range(NTCH):
                for cout in range(ET):
                    wi = tch * ET + cout
                    v.wait_ge(sMM, wi + 1)
                    v.tensor_copy(t_mixT[:, cout * TC:(cout + 1) * TC],
                                  pslot(wi % 2)).then_inc(sEV, 1)
                v.wait_ge(sX2, 16 * NSUB * (tch + 1))
                for j in range(ET * NSUB):
                    et, tsub = divmod(j, NSUB)
                    jj = tch * ET * NSUB + j
                    v.wait_ge(sTOr, jj + 1)
                    v.scalar_tensor_tensor(
                        t_x2t[:, tsub * D + et * 128: tsub * D + (et + 1) * 128],
                        t_x2t[:, tsub * D + et * 128: tsub * D + (et + 1) * 128],
                        1.0, pslot(2 + jj % 2)[:, 0:128],
                        ALU.mult, ALU.add).then_inc(sTOc, 1)

    return nc




def build_l1out():
    nc = bass.Bass("TRN2", target_bir_lowering=False)
    io = {}
    for name, shape, dt, kind in PHASE_IO["l1out"]:
        io[name] = nc.dram_tensor(name, shape, dt, kind=kind).ap()
    phase_l1out(nc, io)
    return nc

def phase_l1ffA(nc, io, px=""):
    xmid = io["xmid"]
    ffprojT = io["ffprojT"]
    ln2w = io["ln2w"]
    ln2b = io["ln2b"]
    hsT = io["hsT"]

    CT = D // 128
    NTT = T // 128
    NPG = 32
    NTCH = 8
    TC = T // NTCH

    es = contextlib.ExitStack()
    sb = lambda n, *a: es.enter_context(nc.sbuf_tensor(px + n, *a))
    ps = lambda n, *a: es.enter_context(nc.psum_tensor(px + n, *a))
    t_id = sb("ident", [128, 128], F32)
    t_x = [sb(f"xb{i}", [128, D], F32) for i in range(2)]
    t_u0 = sb("u0b", [128, D], F32)
    t_uT = sb("uT", [128, CT * T], BF16)
    t_stats = sb("stats", [128, 24], F32)
    t_mv = sb("mv", [128, 2], F32)
    t_lnv = sb("lnv", [128, 1], F32)
    t_rstd = sb("rstd", [128, 1], F32)
    t_eps = sb("eps", [128, 1], F32)
    t_lnw = sb("lnw", [128, CT], F32)
    t_lnb = sb("lnb", [128, CT], F32)
    t_Wa = [sb(f"Wa{i}", [128, 16 * 128], BF16) for i in range(2)]
    t_Wg = [sb(f"Wg{i}", [128, 16 * 128], BF16) for i in range(2)]
    t_sg = sb("sg", [128, TC], BF16)
    t_hs = [sb(f"hs{i}", [128, TC], BF16) for i in range(2)]
    p4 = [ps(f"pb{i}", [128, 1024], F32) for i in range(2)]

    def pslot(i):
        return p4[i // 2][:, (i % 2) * 512:(i % 2 + 1) * 512]

    names = ["sX", "svA", "sscA", "sAP", "sTIr", "sTIe", "sID", "sMMb", "sEVb",
             "sscB", "fV", "fS", "fG", "sXa", "sXb", "sWa_", "sWb_", "sHa", "sHb"]
    sms = {n: es.enter_context(nc.semaphore(px + n)) for n in names}
    (sX, svA, sscA, sAP, sTIr, sTIe, sID, sMMb, sEVb, sscB, fV, fS, fG,
     sXa, sXb, sWa_, sWb_, sHa, sHb) = (sms[n] for n in names)
    sX2 = [sXa, sXb]
    sW2 = [sWa_, sWb_]
    sH2 = [sHa, sHb]

    with es, nc.Block() as blk:
        @blk.gpsimd
        def _(g):
            g.memset(t_eps[:], EPS)
            make_identities(nc, g, [t_id[:]], fG)
            g.nop().then_inc(sID, 1)
            for pg in range(NPG):
                if pg >= 2:
                    g.wait_ge(sMMb, 2 * NTCH * (pg - 1))
                g.dma_start(
                    out=t_Wa[pg % 2][:],
                    in_=ffprojT.rearrange("(c p) n -> p c n", p=128)[
                        :, :, pg * 128:(pg + 1) * 128]).then_inc(sW2[pg % 2], 16)
                g.dma_start(
                    out=t_Wg[pg % 2][:],
                    in_=ffprojT.rearrange("(c p) n -> p c n", p=128)[
                        :, :, (pg + 32) * 128:(pg + 33) * 128]
                    ).then_inc(sW2[pg % 2], 16)

        @blk.sync
        def _(s):
            for c in range(CT):
                s.dma_start(out=t_lnw[:, c:c + 1],
                            in_=ln2w[ts(c, 128)].rearrange("(p o) -> p o", o=1)).then_inc(sX, 16)
                s.dma_start(out=t_lnb[:, c:c + 1],
                            in_=ln2b[ts(c, 128)].rearrange("(p o) -> p o", o=1)).then_inc(sX, 16)
            for i in range(NTT):
                if i >= 2:
                    s.wait_ge(sAP, i - 1)
                s.dma_start(out=t_x[i % 2][:],
                            in_=xmid[ts(i, 128), :]).then_inc(sX2[i % 2], 16)
            for pg in range(NPG):
                for tch in range(NTCH):
                    q = pg * NTCH + tch
                    s.wait_ge(sscB, q + 1)
                    s.dma_start(out=hsT[ts(pg, 128), tch * TC:(tch + 1) * TC],
                                in_=t_hs[q % 2][:]).then_inc(sH2[q % 2], 16)
            s.wait_ge(sHa, NPG * NTCH // 2 * 16)
            s.wait_ge(sHb, NPG * NTCH // 2 * 16)

        @blk.tensor
        def _(t):
            t.wait_ge(sID, 1)
            for i in range(NTT):
                t.wait_ge(sAP, i + 1)
                for c in range(CT):
                    j = i * CT + c
                    if j >= 2:
                        t.wait_ge(sTIe, j - 1)
                    t.transpose(p4[0][:, (j % 2) * 512:(j % 2) * 512 + 128],
                                t_u0[:, c * 128:(c + 1) * 128],
                                t_id[:]).then_inc(sTIr, 1)
            t.wait_ge(sTIe, NTT * CT)
            for pg in range(NPG):
                t.wait_ge(sW2[pg % 2], 16 * 2 * (pg // 2 + 1))
                for tch in range(NTCH):
                    gi = pg * NTCH + tch
                    pp = gi % 2
                    if gi >= 2:
                        t.wait_ge(sEVb, gi - 1)
                        t.wait_ge(sscB, gi - 1)
                    for c in range(CT):
                        insa = t.matmul(pslot(2 * pp),
                                        t_Wa[pg % 2][:, c * 128:(c + 1) * 128],
                                        t_uT[:, c * T + tch * TC: c * T + (tch + 1) * TC],
                                        start=(c == 0), stop=(c == CT - 1))
                        ins = t.matmul(pslot(2 * pp + 1),
                                       t_Wg[pg % 2][:, c * 128:(c + 1) * 128],
                                       t_uT[:, c * T + tch * TC: c * T + (tch + 1) * TC],
                                       start=(c == 0), stop=(c == CT - 1))
                        if c == CT - 1:
                            insa.then_inc(sMMb, 1)
                            ins.then_inc(sMMb, 1)

        @blk.scalar
        def _(s):
            s.wait_ge(sID, 1)
            cS = Chain(s, fS)
            for i in range(NTT):
                s.wait_ge(svA, i + 1)
                cS.wait(); cS.op(s.activation(t_lnv[:], t_mv[:, 1:2], AF.Ln, bias=t_eps[:]))
                cS.wait()
                s.activation(t_rstd[:], t_lnv[:], AF.Exp, scale=-0.5).then_inc(sscA, 1)
                s.wait_ge(sscA, i + 1)
            for pg in range(NPG):
                for tch in range(NTCH):
                    gi = pg * NTCH + tch
                    s.wait_ge(sMMb, 2 * gi + 2)
                    if gi >= 1:
                        s.wait_ge(sscB, gi)
                    s.activation(t_sg[:], pslot(2 * (gi % 2) + 1),
                                 AF.Silu).then_inc(sEVb, 1)

        @blk.vector
        def _(v):
            v.wait_ge(sX, 32 * 16)
            cV = Chain(v, fV)
            for i in range(NTT):
                v.wait_ge(sX2[i % 2], (i // 2 + 1) * 16)
                v.wait_ge(svA, i)
                for gix in range(4):
                    cV.op(v.bn_stats(t_stats[:, gix * 6:(gix + 1) * 6],
                                     t_x[i % 2][:, gix * 512:(gix + 1) * 512]))
                cV.wait()
                v.bn_aggr(t_mv[:], t_stats[:]).then_inc(svA, 1)
                v.wait_ge(sscA, i + 1)
                if i >= 1:
                    v.wait_ge(sTIr, i * CT)
                v.wait_ge(fV, cV.n)
                v.tensor_scalar(t_u0[:], t_x[i % 2][:], t_mv[:, 0:1],
                                t_rstd[:], ALU.subtract, ALU.mult).then_inc(sAP, 1)
                for c in range(CT):
                    j = i * CT + c
                    v.wait_ge(sTIr, j + 1)
                    v.tensor_scalar(t_uT[:, c * T + i * 128: c * T + (i + 1) * 128],
                                    p4[0][:, (j % 2) * 512:(j % 2) * 512 + 128],
                                    t_lnw[:, c:c + 1], t_lnb[:, c:c + 1],
                                    ALU.mult, ALU.add).then_inc(sTIe, 1)
            for pg in range(NPG):
                for tch in range(NTCH):
                    gi = pg * NTCH + tch
                    q = gi
                    v.wait_ge(sEVb, gi + 1)
                    if q >= 2:
                        v.wait_ge(sH2[q % 2], 16 * (q // 2))
                    v.scalar_tensor_tensor(t_hs[q % 2][:], pslot(2 * (gi % 2)),
                                           1.0, t_sg[:], ALU.mult,
                                           ALU.mult).then_inc(sscB, 1)

    return nc




def build_l1ffA():
    nc = bass.Bass("TRN2", target_bir_lowering=False)
    io = {}
    for name, shape, dt, kind in PHASE_IO["l1ffA"]:
        io[name] = nc.dram_tensor(name, shape, dt, kind=kind).ap()
    phase_l1ffA(nc, io)
    return nc

def phase_l1ffB(nc, io, px="", stage=3):
    hsT = io["hsT"]
    ffoutT = io["ffoutT"]
    x2 = io["x2"]
    xmid = io["xmid"]
    x3 = io["x3"]

    NTCH = 8
    TC = T // NTCH
    NSUB = TC // 128
    ET = D // 128
    NC_ = 32

    es = contextlib.ExitStack()
    sb = lambda n, *a: es.enter_context(nc.sbuf_tensor(px + n, *a))
    ps = lambda n, *a: es.enter_context(nc.psum_tensor(px + n, *a))
    t_id = sb("ident", [128, 128], F32)
    t_hs = sb("hsc", [128, NC_ * TC], BF16)
    t_W = [sb(f"W{i}", [128, NC_ * 128], BF16) for i in range(2)]
    t_mixT = sb("mixT", [128, ET * TC], F32)
    t_x2t = sb("x2t", [128, NSUB * D], F32)
    t_xmt = sb("xmt", [128, NSUB * D], F32)
    t_s = sb("s_res", [128, NSUB * D], F32)
    t_stats = sb("stats", [128, 4 * 24], F32)
    t_mv = sb("mv", [128, 4 * 2], F32)
    t_lnv = sb("lnv", [128, 4], F32)
    t_rstd = sb("rstd", [128, 4], F32)
    t_eps = sb("eps", [128, 1], F32)
    p4 = [ps(f"pb{i}", [128, 1024], F32) for i in range(2)]

    def pslot(i):
        return p4[i // 2][:, (i % 2) * 512:(i % 2 + 1) * 512]

    names = ["sYL", "sWa", "sWb", "sMM", "sEV", "sX2", "sTOr", "sTOc",
             "sv", "ssc", "sLN", "sOUT", "sID", "fG", "fV", "fS"]
    sms = {n: es.enter_context(nc.semaphore(px + n)) for n in names}
    (sYL, sWa, sWb, sMM, sEV, sX2, sTOr, sTOc, sv, ssc, sLN, sOUT, sID,
     fG, fV, fS) = (sms[n] for n in names)
    sW2 = [sWa, sWb]

    with es, nc.Block() as blk:
        @blk.gpsimd
        def _(g):
            g.memset(t_eps[:], EPS)
            make_identities(nc, g, [t_id[:]], fG)
            g.nop().then_inc(sID, 1)
            for tch in range(NTCH):
                for cout in range(ET):
                    wi = tch * ET + cout
                    if wi >= 2:
                        g.wait_ge(sMM, wi - 1)
                    g.dma_start(
                        out=t_W[wi % 2][:],
                        in_=ffoutT.rearrange("(i p) n -> p i n", p=128)[
                            :, :, cout * 128:(cout + 1) * 128]
                    ).then_inc(sW2[wi % 2], 16)

        @blk.sync
        def _(s):
            for tch in range(NTCH):
                if tch >= 1:
                    s.wait_ge(sMM, ET * tch)
                for ct in range(NC_):
                    s.dma_start(out=t_hs[:, ct * TC:(ct + 1) * TC],
                                in_=hsT[ts(ct, 128), tch * TC:(tch + 1) * TC]
                                ).then_inc(sYL, 16)
                if tch >= 1:
                    s.wait_ge(sLN, NSUB * tch)
                    s.wait_ge(sOUT, 16 * NSUB * tch)
                for tsub in range(NSUB):
                    s.dma_start(out=t_x2t[:, tsub * D:(tsub + 1) * D],
                                in_=x2[ts(tch * NSUB + tsub, 128), :]
                                ).then_inc(sX2, 16)
                    if stage >= 2:
                        s.dma_start(out=t_xmt[:, tsub * D:(tsub + 1) * D],
                                    in_=xmid[ts(tch * NSUB + tsub, 128), :]
                                    ).then_inc(sX2, 16)
                for tsub in range(NSUB):
                    k = tch * NSUB + tsub
                    s.wait_ge(sLN, k + 1)
                    s.dma_start(out=x3[ts(k, 128), :],
                                in_=t_x2t[:, tsub * D:(tsub + 1) * D]
                                ).then_inc(sOUT, 16)
            s.wait_ge(sOUT, 16 * NTCH * NSUB)

        @blk.tensor
        def _(t):
            t.wait_ge(sID, 1)
            for tch in range(NTCH):
                t.wait_ge(sYL, 16 * NC_ * (tch + 1))
                for cout in range(ET):
                    wi = tch * ET + cout
                    t.wait_ge(sW2[wi % 2], 16 * (wi // 2 + 1))
                    if wi >= 2:
                        t.wait_ge(sEV, wi - 1)
                    for ct in range(NC_):
                        ins = t.matmul(pslot(wi % 2),
                                       t_W[wi % 2][:, ct * 128:(ct + 1) * 128],
                                       t_hs[:, ct * TC:(ct + 1) * TC],
                                       start=(ct == 0), stop=(ct == NC_ - 1))
                    ins.then_inc(sMM, 1)
                if stage >= 1:
                    for j in range(ET * NSUB):
                        et, tsub = divmod(j, NSUB)
                        jj = tch * ET * NSUB + j
                        t.wait_ge(sEV, tch * ET + et + 1)
                        if jj >= 2:
                            t.wait_ge(sTOc, jj - 1)
                        t.transpose(pslot(2 + jj % 2)[:, 0:128],
                                    t_mixT[:, et * TC + tsub * 128: et * TC + (tsub + 1) * 128],
                                    t_id[:]).then_inc(sTOr, 1)

        @blk.scalar
        def _(s):
            s.wait_ge(sID, 1)
            cS = Chain(s, fS)
            for tch in range(NTCH):
                if stage < 3:
                    continue
                s.wait_ge(sv, tch + 1)
                for tsub in range(NSUB):
                    cS.wait()
                    cS.op(s.activation(t_lnv[:, tsub:tsub + 1],
                                       t_mv[:, 2 * tsub + 1:2 * tsub + 2],
                                       AF.Ln, bias=t_eps[:]))
                cS.wait()
                for tsub in range(NSUB):
                    ins = s.activation(t_rstd[:, tsub:tsub + 1],
                                       t_lnv[:, tsub:tsub + 1], AF.Exp, scale=-0.5)
                    if tsub == NSUB - 1:
                        ins.then_inc(ssc, 1)
                s.wait_ge(ssc, tch + 1)

        @blk.vector
        def _(v):
            cV = Chain(v, fV)
            for tch in range(NTCH):
                for cout in range(ET):
                    wi = tch * ET + cout
                    v.wait_ge(sMM, wi + 1)
                    v.tensor_copy(t_mixT[:, cout * TC:(cout + 1) * TC],
                                  pslot(wi % 2)).then_inc(sEV, 1)
                v.wait_ge(sX2, (2 if stage >= 2 else 1) * 16 * NSUB * (tch + 1))
                if stage < 2:
                    v.wait_ge(sEV, (tch + 1) * ET)
                    if stage >= 1:
                        for j in range(ET * NSUB):
                            v.wait_ge(sTOr, tch * ET * NSUB + j + 1)
                            v.engine_nop().then_inc(sTOc, 1)
                    for tsub in range(NSUB):
                        v.engine_nop().then_inc(sLN, 1)
                    continue
                for tsub in range(NSUB):
                    cV.op(v.tensor_tensor(t_xmt[:, tsub * D:(tsub + 1) * D],
                                          t_x2t[:, tsub * D:(tsub + 1) * D],
                                          t_xmt[:, tsub * D:(tsub + 1) * D],
                                          ALU.add))
                cV.wait()
                for j in range(ET * NSUB):
                    et, tsub = divmod(j, NSUB)
                    jj = tch * ET * NSUB + j
                    v.wait_ge(sTOr, jj + 1)
                    v.scalar_tensor_tensor(
                        t_s[:, tsub * D + et * 128: tsub * D + (et + 1) * 128],
                        t_xmt[:, tsub * D + et * 128: tsub * D + (et + 1) * 128],
                        1.0, pslot(2 + jj % 2)[:, 0:128],
                        ALU.mult, ALU.add).then_inc(sTOc, 1)
                v.wait_ge(sTOc, (tch + 1) * ET * NSUB)
                if stage < 3:
                    for tsub in range(NSUB):
                        v.engine_nop().then_inc(sLN, 1)
                    continue
                for tsub in range(NSUB):
                    for gix in range(4):
                        ins = v.bn_stats(
                            t_stats[:, tsub * 24 + gix * 6: tsub * 24 + gix * 6 + 6],
                            t_s[:, tsub * D + gix * 512: tsub * D + (gix + 1) * 512])
                        cV.op(ins)
                cV.wait()
                for tsub in range(NSUB):
                    ins_ = v.bn_aggr(t_mv[:, 2 * tsub:2 * tsub + 2],
                                     t_stats[:, tsub * 24:(tsub + 1) * 24])
                    if tsub == NSUB - 1:
                        ins_.then_inc(sv, 1)
                v.wait_ge(ssc, tch + 1)
                for tsub in range(NSUB):
                    v.tensor_scalar(t_x2t[:, tsub * D:(tsub + 1) * D],
                                    t_s[:, tsub * D:(tsub + 1) * D],
                                    t_mv[:, 2 * tsub:2 * tsub + 1],
                                    t_rstd[:, tsub:tsub + 1],
                                    ALU.subtract, ALU.mult).then_inc(sLN, 1)

    return nc




def build_l1ffB(stage=3):
    nc = bass.Bass("TRN2", target_bir_lowering=False)
    io = {}
    for name, shape, dt, kind in PHASE_IO["l1ffB"]:
        io[name] = nc.dram_tensor(name, shape, dt, kind=kind).ap()
    phase_l1ffB(nc, io, stage=stage)
    return nc

def phase_l0ffnA(nc, io, px=""):
    x1 = io["x1"]
    routerT = io["routerT"]
    sgT = io["sgT"]
    suT = io["suT"]
    wgT = io["wgT"]
    wuT = io["wuT"]
    hsT_all = io["hsT_all"]
    comb_d = io["comb_d"]

    CT = D // 128
    NTT = T // 128
    NIO = 11
    NTCH = 8
    TC = T // NTCH

    es = contextlib.ExitStack()
    sb = lambda n, *a: es.enter_context(nc.sbuf_tensor(px + n, *a))
    ps = lambda n, *a: es.enter_context(nc.psum_tensor(px + n, *a))
    t_id = sb("ident", [128, 128], F32)
    t_x = [sb(f"xb{i}", [128, D], F32) for i in range(2)]
    t_xT = sb("xT", [128, CT * T], BF16)
    t_rc = sb("rc", [128, CT * 8], BF16)
    t_w8 = sb("w8", [128, 8], F32)
    t_exp = sb("exp8", [128, 8], F32)
    t_rden = sb("rden", [128, 1], F32)
    t_wsm = sb("wsm", [128, 8], F32)
    t_mask = sb("mask8", [128, 8], F32)
    t_w2 = sb("w28", [128, 8], F32)
    t_keep = sb("keep8", [128, 8], F32)
    t_comb = sb("comb8", [128, 8], F32)
    t_nm = sb("nm", [128, 1], F32)
    t_den = sb("den", [128, 1], F32)
    t_m1 = sb("m1", [128, 1], F32)
    t_m2 = sb("m2", [128, 1], F32)
    t_ctT = sb("ctT", [128, T], F32)
    t_crow = [sb(f"crow{i}", [1, TC], F32) for i in range(2)]
    t_cbc = sb("cbc", [128, T], BF16)
    t_ones = sb("ones1", [1, 128], F32)
    t_Wg = [sb(f"Wg{i}", [128, 16 * 128], BF16) for i in range(2)]
    t_Wu = [sb(f"Wu{i}", [128, 16 * 128], BF16) for i in range(2)]
    t_sg = sb("sg", [128, TC], BF16)
    t_ue = sb("ue", [128, TC], BF16)
    t_hs = [sb(f"hs{i}", [128, TC], BF16) for i in range(2)]
    p4 = [ps(f"pb{i}", [128, 1024], F32) for i in range(3)]

    def pslot(i):
        return p4[i // 2][:, (i % 2) * 512:(i % 2 + 1) * 512]

    names = ["sXa", "sXb", "sTIr", "sTIe", "sID", "sRC", "sRM", "svR", "sscR",
             "sCT", "sEVC", "sCD", "sCRa", "sCRb", "sBC", "sEVC2",
             "sWa_", "sWb_", "sMMb", "sEVg", "sUE", "sscB", "sHa", "sHb",
             "fV", "fS", "fG"]
    sms = {n: es.enter_context(nc.semaphore(px + n)) for n in names}
    (sXa, sXb, sTIr, sTIe, sID, sRC, sRM, svR, sscR, sCT, sEVC, sCD,
     sCRa, sCRb, sBC, sEVC2, sWa_, sWb_, sMMb, sEVg, sUE, sscB, sHa, sHb,
     fV, fS, fG) = (sms[n] for n in names)
    sX2 = [sXa, sXb]
    sW2 = [sWa_, sWb_]
    sH2 = [sHa, sHb]
    sCR2 = [sCRa, sCRb]

    with es, nc.Block() as blk:
        @blk.gpsimd
        def _(g):
            g.memset(t_ones[:], 1.0)
            make_identities(nc, g, [t_id[:]], fG)
            g.nop().then_inc(sID, 1)
            g.dma_start(out=t_rc[:],
                        in_=routerT.rearrange("(c p) n -> p c n", p=128)).then_inc(sRC, 16)
            for src in range(9):
                for io in range(NIO):
                    wi = src * NIO + io
                    if wi >= 2:
                        g.wait_ge(sMMb, 2 * NTCH * (wi - 1))
                    if src == 0:
                        gsrc = sgT.rearrange("(c p) n -> p c n", p=128)[
                            :, :, io * 128:(io + 1) * 128]
                        usrc = suT.rearrange("(c p) n -> p c n", p=128)[
                            :, :, io * 128:(io + 1) * 128]
                    else:
                        gsrc = wgT[src - 1].rearrange("(c p) n -> p c n", p=128)[
                            :, :, io * 128:(io + 1) * 128]
                        usrc = wuT[src - 1].rearrange("(c p) n -> p c n", p=128)[
                            :, :, io * 128:(io + 1) * 128]
                    g.dma_start(out=t_Wg[wi % 2][:], in_=gsrc).then_inc(sW2[wi % 2], 16)
                    g.dma_start(out=t_Wu[wi % 2][:], in_=usrc).then_inc(sW2[wi % 2], 16)

        @blk.sync
        def _(s):
            for i in range(NTT):
                if i >= 2:
                    s.wait_ge(sTIr, (i - 1) * CT)
                s.dma_start(out=t_x[i % 2][:],
                            in_=x1[ts(i, 128), :]).then_inc(sX2[i % 2], 16)
            s.wait_ge(sEVC, NTT)
            s.dma_start(out=comb_d, in_=t_ctT[0:8, :]).then_inc(sCD, 16)
            for src in range(9):
                if src >= 1:
                    for tch in range(NTCH):
                        ri = (src - 1) * NTCH + tch
                        if ri == 0:
                            s.wait_ge(sCD, 16)
                        if ri >= 2:
                            s.wait_ge(sBC, ri - 1)
                        s.dma_start(out=t_crow[ri % 2][:],
                                    in_=comb_d[src - 1:src, tch * TC:(tch + 1) * TC]
                                    ).then_inc(sCR2[ri % 2], 16)
                for io in range(NIO):
                    for tch in range(NTCH):
                        q = (src * NIO + io) * NTCH + tch
                        s.wait_ge(sscB, q + 1)
                        s.dma_start(
                            out=hsT_all[ts(src * NIO + io, 128),
                                        tch * TC:(tch + 1) * TC],
                            in_=t_hs[q % 2][:]).then_inc(sH2[q % 2], 16)
            s.wait_ge(sHa, 9 * NIO * NTCH // 2 * 16)
            s.wait_ge(sHb, 9 * NIO * NTCH // 2 * 16)

        @blk.tensor
        def _(t):
            t.wait_ge(sID, 1)
            for i in range(NTT):
                t.wait_ge(sX2[i % 2], 16 * (i // 2 + 1))
                for c in range(CT):
                    j = i * CT + c
                    if j >= 2:
                        t.wait_ge(sTIe, j - 1)
                    t.transpose(p4[0][:, (j % 2) * 512:(j % 2) * 512 + 128],
                                t_x[i % 2][:, c * 128:(c + 1) * 128],
                                t_id[:]).then_inc(sTIr, 1)
            t.wait_ge(sTIe, NTT * CT)
            t.wait_ge(sRC, 16)
            for i in range(NTT):
                if i >= 1:
                    t.wait_ge(svR, 2 * i)
                for c in range(CT):
                    ins = t.matmul(pslot(0)[:, 0:8],
                                   t_xT[:, c * T + i * 128: c * T + (i + 1) * 128],
                                   t_rc[:, c * 8:(c + 1) * 8],
                                   start=(c == 0), stop=(c == CT - 1))
                    if c == CT - 1:
                        ins.then_inc(sRM, 1)
                t.wait_ge(svR, 2 * i + 2)
                if i >= 1:
                    t.wait_ge(sEVC, i)
                t.transpose(pslot(2)[0:8, 0:128], t_comb[:],
                            t_id[:]).then_inc(sCT, 1)
            t.wait_ge(sEVC, NTT)
            for src in range(9):
                if src >= 1:
                    for tch in range(NTCH):
                        ri = (src - 1) * NTCH + tch
                        t.wait_ge(sCR2[ri % 2], 16 * (ri // 2 + 1))
                        if ri >= 2:
                            t.wait_ge(sEVC2, ri - 1)
                        t.matmul(pslot(4 + ri % 2), t_ones[:],
                                 t_crow[ri % 2][0:1, :],
                                 start=True, stop=True).then_inc(sBC, 1)
                for io in range(NIO):
                    wi = src * NIO + io
                    t.wait_ge(sW2[wi % 2], 2 * 16 * (wi // 2 + 1))
                    for tch in range(NTCH):
                        gi = wi * NTCH + tch
                        pp = gi % 2
                        if gi >= 2:
                            t.wait_ge(sEVg, gi - 1)
                            t.wait_ge(sUE, gi - 1)
                        for c in range(CT):
                            insg = t.matmul(
                                pslot(2 * pp), t_Wg[wi % 2][:, c * 128:(c + 1) * 128],
                                t_xT[:, c * T + tch * TC: c * T + (tch + 1) * TC],
                                start=(c == 0), stop=(c == CT - 1))
                            insu = t.matmul(
                                pslot(2 * pp + 1), t_Wu[wi % 2][:, c * 128:(c + 1) * 128],
                                t_xT[:, c * T + tch * TC: c * T + (tch + 1) * TC],
                                start=(c == 0), stop=(c == CT - 1))
                            if c == CT - 1:
                                insg.then_inc(sMMb, 1)
                                insu.then_inc(sMMb, 1)

        @blk.scalar
        def _(s):
            s.wait_ge(sID, 1)
            for i in range(NTT):
                s.wait_ge(svR, 2 * i + 1)
                s.activation(t_exp[:], t_w8[:], AF.Exp, bias=t_nm[:],
                             accum_out=t_den[:]).then_inc(sscR, 1)
                s.wait_ge(sscR, i + 1)
            for src in range(9):
                for io in range(NIO):
                    for tch in range(NTCH):
                        gi = (src * NIO + io) * NTCH + tch
                        s.wait_ge(sMMb, 2 * gi + 2)
                        if gi >= 1:
                            s.wait_ge(sscB, gi)
                        s.activation(t_sg[:], pslot(2 * (gi % 2)),
                                     AF.Sigmoid).then_inc(sEVg, 1)

        @blk.vector
        def _(v):
            cV = Chain(v, fV)
            for i in range(NTT):
                for c in range(CT):
                    j = i * CT + c
                    v.wait_ge(sTIr, j + 1)
                    v.tensor_copy(t_xT[:, c * T + i * 128: c * T + (i + 1) * 128],
                                  p4[0][:, (j % 2) * 512:(j % 2) * 512 + 128]
                                  ).then_inc(sTIe, 1)
            for i in range(NTT):
                v.wait_ge(sRM, i + 1)
                cV.wait()
                cV.op(v.tensor_copy(t_w8[:], pslot(0)[:, 0:8]))
                cV.wait()
                ins = v.reduce_max(t_nm[:], t_w8[:], axis=AX.X, negate=True)
                ins.then_inc(svR, 1)
                v.wait_ge(sscR, i + 1)
                cV.wait()
                cV.op(v.reciprocal(t_rden[:], t_den[:]))
                cV.wait()
                cV.op(v.tensor_scalar(t_wsm[:], t_exp[:], t_rden[:], None, ALU.mult))
                cV.wait()
                cV.op(v.reduce_max(t_m1[:], t_wsm[:], axis=AX.X))
                cV.wait()
                cV.op(v.tensor_scalar(t_mask[:], t_wsm[:], t_m1[:], None, ALU.is_ge))
                cV.wait()
                cV.op(v.scalar_tensor_tensor(t_w2[:], t_mask[:], -1e9, t_wsm[:],
                                             ALU.mult, ALU.add))
                cV.wait()
                cV.op(v.reduce_max(t_m2[:], t_w2[:], axis=AX.X))
                cV.wait()
                cV.op(v.tensor_scalar(t_keep[:], t_wsm[:], t_m2[:], None, ALU.is_ge))
                cV.wait()
                ins = v.tensor_tensor(t_comb[:], t_wsm[:], t_keep[:], ALU.mult)
                ins.then_inc(svR, 1)
                v.wait_ge(sCT, i + 1)
                v.tensor_copy(t_ctT[0:8, i * 128:(i + 1) * 128],
                              pslot(2)[0:8, 0:128]).then_inc(sEVC, 1)
            for src in range(9):
                if src >= 1:
                    for tch in range(NTCH):
                        ri = (src - 1) * NTCH + tch
                        v.wait_ge(sBC, ri + 1)
                        v.tensor_copy(t_cbc[:, tch * TC:(tch + 1) * TC],
                                      pslot(4 + ri % 2)).then_inc(sEVC2, 1)
                for io in range(NIO):
                    for tch in range(NTCH):
                        gi = (src * NIO + io) * NTCH + tch
                        q = gi
                        v.wait_ge(sMMb, 2 * gi + 2)
                        cV.wait()
                        if src == 0:
                            ins = v.tensor_copy(t_ue[:], pslot(2 * (gi % 2) + 1))
                        else:
                            ins = v.scalar_tensor_tensor(
                                t_ue[:], pslot(2 * (gi % 2) + 1), 1.0,
                                t_cbc[:, tch * TC:(tch + 1) * TC],
                                ALU.mult, ALU.mult)
                        ins.then_inc(sUE, 1)
                        cV.n += 0
                        v.wait_ge(sUE, gi + 1)
                        v.wait_ge(sEVg, gi + 1)
                        if q >= 2:
                            v.wait_ge(sH2[q % 2], 16 * (q // 2))
                        cV.wait()
                        ins = v.tensor_tensor(t_hs[q % 2][:], t_sg[:], t_ue[:],
                                              ALU.mult)
                        ins.then_inc(sscB, 1)
                        v.wait_ge(sscB, gi + 1)

    return nc




def build_l0ffnA():
    nc = bass.Bass("TRN2", target_bir_lowering=False)
    io = {}
    for name, shape, dt, kind in PHASE_IO["l0ffnA"]:
        io[name] = nc.dram_tensor(name, shape, dt, kind=kind).ap()
    phase_l0ffnA(nc, io)
    return nc

def phase_l0ffnB(nc, io, px=""):
    hsT_all = io["hsT_all"]
    wdT_all = io["wdT_all"]
    x1 = io["x1"]
    x2 = io["x2"]

    NTCH = 8
    TC = T // NTCH
    NSUB = TC // 128
    ET = D // 128
    NC_ = 99

    es = contextlib.ExitStack()
    sb = lambda n, *a: es.enter_context(nc.sbuf_tensor(px + n, *a))
    ps = lambda n, *a: es.enter_context(nc.psum_tensor(px + n, *a))
    t_id = sb("ident", [128, 128], F32)
    t_hs = sb("hsc", [128, NC_ * TC], BF16)
    t_W = [sb("W0", [128, NC_ * 128], BF16)]
    t_mixT = sb("mixT", [128, ET * TC], BF16)
    t_x1t = sb("x1t", [128, NSUB * D], F32)
    t_s = sb("s_res", [128, NSUB * D], F32)
    t_stats = sb("stats", [128, 4 * 24], F32)
    t_mv = sb("mv", [128, 4 * 2], F32)
    t_lnv = sb("lnv", [128, 4], F32)
    t_rstd = sb("rstd", [128, 4], F32)
    t_eps = sb("eps", [128, 1], F32)
    t_idb = sb("identb", [128, 128], BF16)
    p4 = [ps(f"pb{i}", [128, 1024], F32) for i in range(2)]
    p_tb = [ps(f"ptb{i}", [128, 1024], BF16) for i in range(2)]

    def pslot(i):
        return p4[i // 2][:, (i % 2) * 512:(i % 2 + 1) * 512]

    names = ["sYL", "sWa", "sWb", "sMM", "sEV", "sX2", "sTOr", "sTOc",
             "sv", "ssc", "sLN", "sOUT", "sID", "fG", "fV", "fS"]
    sms = {n: es.enter_context(nc.semaphore(px + n)) for n in names}
    (sYL, sWa, sWb, sMM, sEV, sX2, sTOr, sTOc, sv, ssc, sLN, sOUT, sID,
     fG, fV, fS) = (sms[n] for n in names)
    sW2 = [sWa, sWb]

    with es, nc.Block() as blk:
        @blk.gpsimd
        def _(g):
            g.memset(t_eps[:], EPS)
            make_identities(nc, g, [t_id[:], t_idb[:]], fG)
            g.nop().then_inc(sID, 1)
            for tch in range(NTCH):
                for cout in range(ET):
                    wi = tch * ET + cout
                    if wi >= 1:
                        g.wait_ge(sMM, wi)
                    g.dma_start(
                        out=t_W[0][:],
                        in_=wdT_all.rearrange("(i p) n -> p i n", p=128)[
                            :, :, cout * 128:(cout + 1) * 128]
                    ).then_inc(sWa, 16)

        @blk.sync
        def _(s):
            for tch in range(NTCH):
                if tch >= 1:
                    s.wait_ge(sMM, ET * tch)
                for ct in range(NC_):
                    s.dma_start(out=t_hs[:, ct * TC:(ct + 1) * TC],
                                in_=hsT_all[ts(ct, 128), tch * TC:(tch + 1) * TC]
                                ).then_inc(sYL, 16)
                if tch >= 1:
                    s.wait_ge(sLN, NSUB * tch)
                    s.wait_ge(sOUT, 16 * NSUB * tch)
                for tsub in range(NSUB):
                    s.dma_start(out=t_x1t[:, tsub * D:(tsub + 1) * D],
                                in_=x1[ts(tch * NSUB + tsub, 128), :]
                                ).then_inc(sX2, 16)
                for tsub in range(NSUB):
                    k = tch * NSUB + tsub
                    s.wait_ge(sLN, k + 1)
                    s.dma_start(out=x2[ts(k, 128), :],
                                in_=t_x1t[:, tsub * D:(tsub + 1) * D]
                                ).then_inc(sOUT, 16)
            s.wait_ge(sOUT, 16 * NTCH * NSUB)

        @blk.tensor
        def _(t):
            t.wait_ge(sID, 1)
            for tch in range(NTCH):
                t.wait_ge(sYL, 16 * NC_ * (tch + 1))
                for cout in range(ET):
                    wi = tch * ET + cout
                    t.wait_ge(sWa, 16 * (wi + 1))
                    if wi >= 2:
                        t.wait_ge(sEV, wi - 1)
                    for ct in range(NC_):
                        ins = t.matmul(pslot(wi % 2),
                                       t_W[0][:, ct * 128:(ct + 1) * 128],
                                       t_hs[:, ct * TC:(ct + 1) * TC],
                                       start=(ct == 0), stop=(ct == NC_ - 1))
                    ins.then_inc(sMM, 1)
                for j in range(ET * NSUB):
                    et, tsub = divmod(j, NSUB)
                    jj = tch * ET * NSUB + j
                    t.wait_ge(sEV, tch * ET + et + 1)
                    if jj >= 2:
                        t.wait_ge(sTOc, jj - 1)
                    t.transpose(p_tb[jj % 2][:, 0:128],
                                t_mixT[:, et * TC + tsub * 128: et * TC + (tsub + 1) * 128],
                                t_idb[:]).then_inc(sTOr, 1)

        @blk.scalar
        def _(s):
            s.wait_ge(sID, 1)
            cS = Chain(s, fS)
            for tch in range(NTCH):
                s.wait_ge(sv, tch + 1)
                for tsub in range(NSUB):
                    cS.wait()
                    cS.op(s.activation(t_lnv[:, tsub:tsub + 1],
                                       t_mv[:, 2 * tsub + 1:2 * tsub + 2],
                                       AF.Ln, bias=t_eps[:]))
                cS.wait()
                for tsub in range(NSUB):
                    ins = s.activation(t_rstd[:, tsub:tsub + 1],
                                       t_lnv[:, tsub:tsub + 1], AF.Exp, scale=-0.5)
                    if tsub == NSUB - 1:
                        ins.then_inc(ssc, 1)
                s.wait_ge(ssc, tch + 1)

        @blk.vector
        def _(v):
            cV = Chain(v, fV)
            for tch in range(NTCH):
                for cout in range(ET):
                    wi = tch * ET + cout
                    v.wait_ge(sMM, wi + 1)
                    v.tensor_copy(t_mixT[:, cout * TC:(cout + 1) * TC],
                                  pslot(wi % 2)).then_inc(sEV, 1)
                v.wait_ge(sX2, 16 * NSUB * (tch + 1))
                for j in range(ET * NSUB):
                    et, tsub = divmod(j, NSUB)
                    jj = tch * ET * NSUB + j
                    v.wait_ge(sTOr, jj + 1)
                    v.scalar_tensor_tensor(
                        t_s[:, tsub * D + et * 128: tsub * D + (et + 1) * 128],
                        t_x1t[:, tsub * D + et * 128: tsub * D + (et + 1) * 128],
                        2.0, p_tb[jj % 2][:, 0:128],
                        ALU.mult, ALU.add).then_inc(sTOc, 1)
                v.wait_ge(sTOc, (tch + 1) * ET * NSUB)
                for tsub in range(NSUB):
                    for gix in range(4):
                        ins = v.bn_stats(
                            t_stats[:, tsub * 24 + gix * 6: tsub * 24 + gix * 6 + 6],
                            t_s[:, tsub * D + gix * 512: tsub * D + (gix + 1) * 512])
                        cV.op(ins)
                cV.wait()
                for tsub in range(NSUB):
                    ins_ = v.bn_aggr(t_mv[:, 2 * tsub:2 * tsub + 2],
                                     t_stats[:, tsub * 24:(tsub + 1) * 24])
                    if tsub == NSUB - 1:
                        ins_.then_inc(sv, 1)
                v.wait_ge(ssc, tch + 1)
                for tsub in range(NSUB):
                    v.tensor_scalar(t_x1t[:, tsub * D:(tsub + 1) * D],
                                    t_s[:, tsub * D:(tsub + 1) * D],
                                    t_mv[:, 2 * tsub:2 * tsub + 1],
                                    t_rstd[:, tsub:tsub + 1],
                                    ALU.subtract, ALU.mult).then_inc(sLN, 1)

    return nc




def build_l0ffnB():
    nc = bass.Bass("TRN2", target_bir_lowering=False)
    io = {}
    for name, shape, dt, kind in PHASE_IO["l0ffnB"]:
        io[name] = nc.dram_tensor(name, shape, dt, kind=kind).ap()
    phase_l0ffnB(nc, io)
    return nc

def phase_l0attnA(nc, io, px=""):
    x0 = io["x0"]
    Wq = io["Wq"]
    Wk = io["Wk"]
    Wv = io["Wv"]
    qrw = io["qrw"]
    krw = io["krw"]
    qT = io["qT"]
    kT = io["kT"]
    vT = io["vT"]

    CT = D // 128
    NTT = T // 128
    NTCH = 8
    TC = T // NTCH

    es = contextlib.ExitStack()
    sb = lambda n, *a: es.enter_context(nc.sbuf_tensor(px + n, *a))
    ps = lambda n, *a: es.enter_context(nc.psum_tensor(px + n, *a))
    t_id = sb("ident", [128, 128], F32)
    t_x = [sb(f"xb{i}", [128, D], F32) for i in range(2)]
    t_xT = sb("xT", [128, CT * T], BF16)
    t_W = [sb(f"W{i}", [128, 16 * 128], BF16) for i in range(2)]
    t_onesc = sb("onesc", [128, 1], F32)
    t_ones1 = sb("ones1", [1, 128], F32)
    t_qf = sb("qf", [128, TC], F32)
    t_q2 = sb("q2", [128, TC], F32)
    t_ssq = sb("ssq", [1, TC], F32)
    t_rr = sb("rr", [1, TC], F32)
    t_rw = sb("rwcol", [128, 2], F32)
    t_eps1 = sb("eps1", [1, 1], F32)
    t_eps128 = sb("eps128", [1, 1], F32)
    t_out = [sb(f"ob{i}", [128, TC], BF16) for i in range(2)]
    p4 = [ps(f"pb{i}", [128, 1024], F32) for i in range(2)]

    def pslot(i):
        return p4[i // 2][:, (i % 2) * 512:(i % 2 + 1) * 512]

    names = ["sXa", "sXb", "sTIr", "sTIe", "sID", "sRW", "sWa_", "sWb_",
             "sMM", "sQF", "svQ", "sMM2", "sscQ", "svR2", "sBC", "sU2",
             "sSTT", "sOa", "sOb", "fV", "fS", "fG"]
    sms = {n: es.enter_context(nc.semaphore(px + n)) for n in names}
    (sXa, sXb, sTIr, sTIe, sID, sRW, sWa_, sWb_, sMM, sQF, svQ, sMM2, sscQ,
     svR2, sBC, sU2, sSTT, sOa, sOb, fV, fS, fG) = (sms[n] for n in names)
    sX2 = [sXa, sXb]
    sW2 = [sWa_, sWb_]
    sO2 = [sOa, sOb]

    WMATS = [(Wq, qT, 0), (Wk, kT, 1), (Wv, vT, 2)]

    def qof(mi, ho, tch):
        return (mi * 16 + ho) * NTCH + tch

    def qiof(mi, ho, tch):
        return (mi * 16 + ho) * NTCH + tch

    with es, nc.Block() as blk:
        @blk.gpsimd
        def _(g):
            g.memset(t_onesc[:], 1.0)
            g.memset(t_ones1[:], 1.0)
            g.memset(t_eps1[:], EPS)
            g.memset(t_eps128[:], 128.0 * EPS)
            make_identities(nc, g, [t_id[:]], fG)
            g.nop().then_inc(sID, 1)
            wi = 0
            for (Wm, _o, _k) in WMATS:
                for ho in range(16):
                    if wi >= 2:
                        g.wait_ge(sMM, NTCH * (wi - 1))
                    g.dma_start(
                        out=t_W[wi % 2][:],
                        in_=Wm.rearrange("(c p) n -> p c n", p=128)[
                            :, :, ho * 128:(ho + 1) * 128]
                    ).then_inc(sW2[wi % 2], 16)
                    wi += 1

        @blk.sync
        def _(s):
            s.dma_start(out=t_rw[:, 0:1],
                        in_=qrw.rearrange("(p o) -> p o", o=1)).then_inc(sRW, 16)
            s.dma_start(out=t_rw[:, 1:2],
                        in_=krw.rearrange("(p o) -> p o", o=1)).then_inc(sRW, 16)
            for i in range(NTT):
                if i >= 2:
                    s.wait_ge(sTIr, (i - 1) * CT)
                s.dma_start(out=t_x[i % 2][:],
                            in_=x0[ts(i, 128), :]).then_inc(sX2[i % 2], 16)
            for mi, (_w, Om, kind) in enumerate(WMATS):
                for ho in range(16):
                    for tch in range(NTCH):
                        q = qof(mi, ho, tch)
                        s.wait_ge(sSTT, q + 1)
                        s.dma_start(out=Om[ts(ho, 128), tch * TC:(tch + 1) * TC],
                                    in_=t_out[q % 2][:]).then_inc(sO2[q % 2], 16)
            s.wait_ge(sOa, 3 * 16 * NTCH // 2 * 16)
            s.wait_ge(sOb, 3 * 16 * NTCH // 2 * 16)

        @blk.tensor
        def _(t):
            t.wait_ge(sID, 1)
            for i in range(NTT):
                t.wait_ge(sX2[i % 2], 16 * (i // 2 + 1))
                for c in range(CT):
                    j = i * CT + c
                    if j >= 2:
                        t.wait_ge(sTIe, j - 1)
                    t.transpose(p4[0][:, (j % 2) * 512:(j % 2) * 512 + 128],
                                t_x[i % 2][:, c * 128:(c + 1) * 128],
                                t_id[:]).then_inc(sTIr, 1)
            t.wait_ge(sTIe, NTT * CT)
            wi = 0
            for mi, (_w, _o, kind) in enumerate(WMATS):
                for ho in range(16):
                    t.wait_ge(sW2[wi % 2], 16 * (wi // 2 + 1))
                    for tch in range(NTCH):
                        q = wi * NTCH + tch
                        if q >= 2:
                            t.wait_ge(sQF, q - 1)
                        for c in range(CT):
                            ins = t.matmul(
                                pslot(q % 2), t_W[wi % 2][:, c * 128:(c + 1) * 128],
                                t_xT[:, c * T + tch * TC: c * T + (tch + 1) * TC],
                                start=(c == 0), stop=(c == CT - 1))
                        ins.then_inc(sMM, 1)
                        if kind < 2:
                            qi = qiof(mi, ho, tch)
                            t.wait_ge(svQ, qi + 1)
                            if qi >= 1:
                                t.wait_ge(sscQ, qi)
                            t.matmul(pslot(2)[0:1, :], t_onesc[:], t_q2[:],
                                     start=True, stop=True).then_inc(sMM2, 1)
                            t.wait_ge(svR2, qi + 1)
                            if qi >= 1:
                                t.wait_ge(sU2, qi)
                            t.matmul(pslot(3), t_ones1[:], t_rr[0:1, :],
                                     start=True, stop=True).then_inc(sBC, 1)
                    wi += 1

        @blk.scalar
        def _(s):
            s.wait_ge(sID, 1)
            for mi in range(2):
                for ho in range(16):
                    for tch in range(NTCH):
                        qi = qiof(mi, ho, tch)
                        s.wait_ge(sMM2, qi + 1)
                        if qi >= 1:
                            s.wait_ge(svR2, qi)
                        if mi == 0:
                            s.activation(t_ssq[:], pslot(2)[0:1, :], AF.Sqrt,
                                         bias=t_eps128[:]).then_inc(sscQ, 1)
                        else:
                            s.activation(t_ssq[:], pslot(2)[0:1, :], AF.Sqrt,
                                         scale=1.0 / 128.0,
                                         bias=t_eps1[:]).then_inc(sscQ, 1)
                        s.wait_ge(sscQ, qi + 1)

        @blk.vector
        def _(v):
            v.wait_ge(sRW, 32)
            cV = Chain(v, fV)
            for i in range(NTT):
                for c in range(CT):
                    j = i * CT + c
                    v.wait_ge(sTIr, j + 1)
                    v.tensor_copy(t_xT[:, c * T + i * 128: c * T + (i + 1) * 128],
                                  p4[0][:, (j % 2) * 512:(j % 2) * 512 + 128]
                                  ).then_inc(sTIe, 1)
            for mi, (_w, _o, kind) in enumerate(WMATS):
                for ho in range(16):
                    for tch in range(NTCH):
                        q = qof(mi, ho, tch)
                        v.wait_ge(sMM, q + 1)
                        if kind == 2:
                            if q >= 2:
                                v.wait_ge(sO2[q % 2], 16 * (q // 2))
                            v.tensor_copy(t_out[q % 2][:],
                                          pslot(q % 2)).then_inc(sSTT, 1)
                            v.wait_ge(sSTT, q + 1)
                            v.engine_nop().then_inc(sQF, 1)
                            continue
                        qi = qiof(mi, ho, tch)
                        cV.wait()
                        ins = v.tensor_copy(t_qf[:], pslot(q % 2))
                        ins.then_inc(sQF, 1)
                        v.wait_ge(sQF, q + 1)
                        if qi >= 1:
                            v.wait_ge(sMM2, qi)
                        cV.wait()
                        ins = v.tensor_tensor(t_q2[:], t_qf[:], t_qf[:], ALU.mult)
                        ins.then_inc(svQ, 1)
                        v.wait_ge(svQ, qi + 1)
                        v.wait_ge(sscQ, qi + 1)
                        cV.wait()
                        ins = v.reciprocal(t_rr[:], t_ssq[:])
                        ins.then_inc(svR2, 1)
                        v.wait_ge(svR2, qi + 1)
                        v.wait_ge(sBC, qi + 1)
                        if q >= 2:
                            v.wait_ge(sO2[q % 2], 16 * (q // 2))
                        cV.wait()
                        ins = v.scalar_tensor_tensor(
                            t_out[q % 2][:], t_qf[:], t_rw[:, mi:mi + 1],
                            pslot(3), ALU.mult, ALU.mult)
                        ins.then_inc(sU2, 1)
                        v.wait_ge(sU2, qi + 1)
                        v.engine_nop().then_inc(sSTT, 1)

    return nc




def build_l0attnA():
    nc = bass.Bass("TRN2", target_bir_lowering=False)
    io = {}
    for name, shape, dt, kind in PHASE_IO["l0attnA"]:
        io[name] = nc.dram_tensor(name, shape, dt, kind=kind).ap()
    phase_l0attnA(nc, io)
    return nc

def phase_l0attnB(nc, io, px=""):
    qTd = io["qT"]
    kTd = io["kT"]
    vTd = io["vT"]
    Wproj = io["Wproj"]
    subw = io["subw"]
    Rt = io["Rt"]
    x0 = io["x0"]
    x1 = io["x1"]

    NQB = 8
    QB = 512
    NSUB = 4
    NH = 16
    ET = D // 128

    es = contextlib.ExitStack()
    sb = lambda n, *a: es.enter_context(nc.sbuf_tensor(px + n, *a))
    ps = lambda n, *a: es.enter_context(nc.psum_tensor(px + n, *a))
    t_id = sb("ident", [128, 128], F32)
    t_idb = sb("identb", [128, 128], BF16)
    t_mask = sb("cmask", [128, 128], F32)
    t_kh = sb("kh", [128, T], BF16)
    t_vtok = sb("vtok", [128, T], BF16)
    t_qh = sb("qh", [128, QB], BF16)
    t_R = sb("Rrow", [1, T], F32)
    t_sc = sb("scores", [128, T], F32)
    t_P = sb("P", [128, T], BF16)
    t_PT = sb("PT", [128, 32 * QB], BF16)
    t_cm = sb("cm", [128, 33], F32)
    t_nm = sb("nm", [128, 1], F32)
    t_den = sb("den", [128, 1], F32)
    t_rd = sb("rd", [128, 1], F32)
    t_yf = sb("yf", [128, QB], F32)
    t_q2 = sb("q2", [128, QB], F32)
    t_ssq = sb("ssq", [1, QB], F32)
    t_rr = sb("rr", [1, QB], F32)
    t_sw = sb("swcol", [128, 1], F32)
    t_eps1 = sb("eps1", [1, 1], F32)
    t_onesc = sb("onesc", [128, 1], F32)
    t_ones1 = sb("ones1", [1, 128], F32)
    t_ynT = sb("ynT", [128, NH * QB], BF16)
    t_aT = sb("aT", [128, ET * QB], BF16)
    t_Wp = [sb("Wp0", [128, 16 * 128], BF16)]
    t_x0t = sb("x0t", [128, NSUB * D], F32)
    t_s = sb("s_res", [128, NSUB * D], F32)
    t_stats = sb("stats", [128, 4 * 24], F32)
    t_mv = sb("mv", [128, 4 * 2], F32)
    t_lnv = sb("lnv", [128, 4], F32)
    t_rstd = sb("rstd", [128, 4], F32)
    t_eps = sb("eps", [128, 1], F32)
    p4 = [ps(f"pb{i}", [128, 1024], F32) for i in range(3)]
    p_tb = [ps(f"ptb{i}", [128, 1024], BF16) for i in range(2)]

    def pslot(i):
        return p4[i // 2][:, (i % 2) * 512:(i % 2 + 1) * 512]

    names = ["sID", "sKV", "sHD", "sSM", "sSE", "svM", "sscE", "svP",
             "sTPr", "sTPe", "sAV", "sYE", "sMS", "svQ", "sMM2", "sscQ", "svR2",
             "sBC", "sYN", "sWa_", "sWb_", "sVT", "sPM", "sPE", "sX0", "sTOr",
             "sTOc", "sv", "ssc", "sLN", "sOUT", "fV", "fS", "fG"]
    sms = {n: es.enter_context(nc.semaphore(px + n)) for n in names}
    (sID, sKV, sHD, sSM, sSE, svM, sscE, svP, sTPr, sTPe, sAV, sYE, sMS, svQ,
     sMM2, sscQ, svR2, sBC, sYN, sWa_, sWb_, sVT, sPM, sPE, sX0, sTOr, sTOc,
     sv, ssc, sLN, sOUT, fV, fS, fG) = (sms[n] for n in names)
    sW2 = [sWa_, sWb_]

    def chunks(qa):
        klen = (qa + 1) * 128
        out = []
        off = 0
        while off < klen:
            ln = min(512, klen - off)
            out.append((off, ln))
            off += ln
        return out

    with es, nc.Block() as blk:
        @blk.gpsimd
        def _(g):
            g.memset(t_eps1[:], EPS)
            g.memset(t_eps[:], EPS)
            g.memset(t_onesc[:], 1.0)
            g.memset(t_ones1[:], 1.0)
            g.memset(t_mask[:], 0.0)
            make_identities(nc, g, [t_id[:], t_idb[:]], fG)
            g.affine_select(out=t_mask[:], in_=t_mask[:], compare_op=ALU.is_ge,
                            fill=-1e30, base=0, pattern=[[-1, 128]],
                            channel_multiplier=1).then_inc(fG, 1)
            g.wait_ge(fG, 3)
            g.nop().then_inc(sID, 1)
            for qb in range(NQB):
                for h in range(NH):
                    hi = qb * NH + h
                    if hi >= 1:
                        g.wait_ge(sHD, hi)
                    g.dma_start(out=t_kh[:], in_=kTd[ts(h, 128), :]).then_inc(sKV, 16)
                    g.dma_start(out=t_qh[:],
                                in_=qTd[ts(h, 128), qb * QB:(qb + 1) * QB]
                                ).then_inc(sKV, 16)
                    g.dma_start(out=t_R[:], in_=Rt[h:h + 1, :]).then_inc(sKV, 16)
                for cout in range(ET):
                    wi = qb * ET + cout
                    if wi >= 1:
                        g.wait_ge(sPM, wi)
                    g.dma_start(
                        out=t_Wp[0][:],
                        in_=Wproj.rearrange("(c p) n -> p c n", p=128)[
                            :, :, cout * 128:(cout + 1) * 128]
                    ).then_inc(sWa_, 16)

        @blk.sync
        def _(s):
            s.dma_start(out=t_sw[:],
                        in_=subw.rearrange("(p o) -> p o", o=1)).then_inc(sX0, 16)
            for qb in range(NQB):
                for h in range(NH):
                    hi = qb * NH + h
                    if hi >= 1:
                        s.wait_ge(sHD, hi)
                    for st in range(32):
                        s.dma_start_transpose(
                            t_vtok[:, st * 128:(st + 1) * 128],
                            vTd[ts(h, 128), st * 128:(st + 1) * 128]
                        ).then_inc(sVT, 16)

                if qb >= 1:
                    s.wait_ge(sLN, NSUB * qb)
                    s.wait_ge(sOUT, 16 * NSUB * qb)
                for tsub in range(NSUB):
                    s.dma_start(out=t_x0t[:, tsub * D:(tsub + 1) * D],
                                in_=x0[ts(qb * NSUB + tsub, 128), :]
                                ).then_inc(sX0, 16)
                for tsub in range(NSUB):
                    k = qb * NSUB + tsub
                    s.wait_ge(sLN, k + 1)
                    s.dma_start(out=x1[ts(k, 128), :],
                                in_=t_x0t[:, tsub * D:(tsub + 1) * D]
                                ).then_inc(sOUT, 16)
            s.wait_ge(sOUT, 16 * NQB * NSUB)

        @blk.tensor
        def _(t):
            t.wait_ge(sID, 1)
            nTP = 0
            nSM = 0
            nAV = 0
            nQI = 0
            for qb in range(NQB):
                for h in range(NH):
                    hi = qb * NH + h
                    t.wait_ge(sKV, 48 * (hi + 1))
                    t.wait_ge(sVT, 16 * 32 * (hi + 1))
                    for qs in range(NSUB):
                        qa = qb * NSUB + qs
                        chs = chunks(qa)
                        for ji, (off, ln) in enumerate(chs):
                            nSM += 1
                            if nSM >= 3:
                                t.wait_ge(sSE, nSM - 2)
                            t.matmul(pslot(nSM % 2)[:, 0:ln],
                                     t_qh[:, qs * 128:(qs + 1) * 128],
                                     t_kh[:, off:off + ln],
                                     start=True, stop=False)
                            t.matmul(pslot(nSM % 2)[:, 0:ln], t_ones1[:],
                                     t_R[0:1, off:off + ln],
                                     start=False, stop=True).then_inc(sSM, 1)
                        si = (qb * NH + h) * NSUB + qs
                        t.wait_ge(svP, si + 1)
                        for st in range(qa + 1):
                            nTP += 1
                            if nTP >= 3:
                                t.wait_ge(sTPe, nTP - 2)
                            t.transpose(p_tb[nTP % 2][:, 0:128],
                                        t_P[:, st * 128:(st + 1) * 128],
                                        t_idb[:]).then_inc(sTPr, 1)
                    t.wait_ge(sTPe, nTP)
                    t.wait_ge(sMS, 3 * (hi + 1))
                    t.wait_ge(sYE, hi)
                    nst = qb * NSUB + NSUB
                    for st in range(nst):
                        ins = t.matmul(pslot(4), t_vtok[:, st * 128:(st + 1) * 128],
                                       t_PT[:, st * QB:(st + 1) * QB],
                                       start=(st == 0), stop=(st == nst - 1))
                    ins.then_inc(sAV, 1)
                    qi = hi
                    t.wait_ge(svQ, qi + 1)
                    if qi >= 1:
                        t.wait_ge(sscQ, qi)
                    t.matmul(pslot(3)[0:1, :], t_onesc[:], t_q2[:],
                             start=True, stop=True).then_inc(sMM2, 1)
                    t.wait_ge(svR2, qi + 1)
                    if qi >= 1:
                        t.wait_ge(sYN, qi)
                    t.matmul(pslot(2), t_ones1[:], t_rr[0:1, :],
                             start=True, stop=True).then_inc(sBC, 1)
                    t.nop().then_inc(sHD, 1)
                t.wait_ge(sYN, (qb + 1) * NH)
                for cout in range(ET):
                    wi = qb * ET + cout
                    t.wait_ge(sWa_, 16 * (wi + 1))
                    if wi >= 2:
                        t.wait_ge(sPE, wi - 1)
                    for h in range(NH):
                        ins = t.matmul(pslot(5 if wi % 2 == 0 else 0),
                                       t_Wp[0][:, h * 128:(h + 1) * 128],
                                       t_ynT[:, h * QB:(h + 1) * QB],
                                       start=(h == 0), stop=(h == NH - 1))
                    ins.then_inc(sPM, 1)
                for j in range(ET * NSUB):
                    et, tsub = divmod(j, NSUB)
                    jj = qb * ET * NSUB + j
                    t.wait_ge(sPE, qb * ET + et + 1)
                    nTP += 1
                    if nTP >= 3:
                        t.wait_ge(sTPe, nTP - 2)
                    t.transpose(p_tb[nTP % 2][:, 0:128],
                                t_aT[:, et * QB + tsub * 128: et * QB + (tsub + 1) * 128],
                                t_idb[:]).then_inc(sTPr, 1)

        @blk.scalar
        def _(s):
            s.wait_ge(sID, 1)
            cS = Chain(s, fS)
            for qb in range(NQB):
                for h in range(NH):
                    hi = qb * NH + h
                    for qs in range(NSUB):
                        qa = qb * NSUB + qs
                        si = (qb * NH + h) * NSUB + qs
                        s.wait_ge(svM, si + 1)
                        s.activation(t_P[:, 0:(qa + 1) * 128],
                                     t_sc[:, 0:(qa + 1) * 128], AF.Exp,
                                     bias=t_nm[:], accum_out=t_den[:]
                                     ).then_inc(sscE, 1)
                        s.wait_ge(sscE, si + 1)
                    qi = hi
                    s.wait_ge(sMM2, qi + 1)
                    if qi >= 1:
                        s.wait_ge(svR2, qi)
                    s.activation(t_ssq[:], pslot(3)[0:1, :], AF.Sqrt,
                                 scale=1.0 / 128.0, bias=t_eps1[:]).then_inc(sscQ, 1)
                    s.wait_ge(sscQ, qi + 1)
                s.wait_ge(sv, qb + 1)
                for tsub in range(NSUB):
                    cS.wait()
                    cS.op(s.activation(t_lnv[:, tsub:tsub + 1],
                                       t_mv[:, 2 * tsub + 1:2 * tsub + 2],
                                       AF.Ln, bias=t_eps[:]))
                cS.wait()
                for tsub in range(NSUB):
                    ins = s.activation(t_rstd[:, tsub:tsub + 1],
                                       t_lnv[:, tsub:tsub + 1], AF.Exp, scale=-0.5)
                    if tsub == NSUB - 1:
                        ins.then_inc(ssc, 1)
                s.wait_ge(ssc, qb + 1)

        @blk.vector
        def _(v):
            v.wait_ge(sX0, 16)
            cV = Chain(v, fV)
            nTP = 0
            nSM = 0
            for qb in range(NQB):
                for h in range(NH):
                    hi = qb * NH + h
                    if hi >= 1:
                        v.wait_ge(sAV, hi)
                    for ib in range(1, NSUB):
                        st = qb * NSUB + ib
                        v.memset(t_PT[:, st * QB: st * QB + ib * 128],
                                 0.0).then_inc(sMS, 1)
                    for qs in range(NSUB):
                        qa = qb * NSUB + qs
                        si = (qb * NH + h) * NSUB + qs
                        chs = chunks(qa)
                        for ji, (off, ln) in enumerate(chs):
                            nSM += 1
                            v.wait_ge(sSM, nSM)
                            npc = ln // 128
                            for u in range(npc):
                                kidx = off // 128 + u
                                pp = pslot(nSM % 2)[:, u * 128:(u + 1) * 128]
                                dst = t_sc[:, kidx * 128:(kidx + 1) * 128]
                                if kidx == qa:
                                    cV.wait()
                                    cV.op(v.scalar_tensor_tensor(
                                        dst, pp, 1.0, t_mask[:], ALU.mult, ALU.add))
                                    cV.wait()
                                    cV.op(v.reduce_max(t_cm[:, kidx:kidx + 1],
                                                       dst, axis=AX.X))
                                else:
                                    cV.wait()
                                    cV.op(v.tensor_scalar(
                                        dst, pp, 1.0, -1e30, ALU.mult, ALU.max,
                                        accum_out=t_cm[:, kidx:kidx + 1]))
                            v.wait_ge(fV, cV.n)
                            v.engine_nop().then_inc(sSE, 1)
                        cV.wait()
                        ins = v.reduce_max(t_nm[:], t_cm[:, 0:qa + 1], axis=AX.X,
                                           negate=True)
                        ins.then_inc(svM, 1)
                        v.wait_ge(sscE, si + 1)
                        cV.wait()
                        cV.op(v.reciprocal(t_rd[:], t_den[:]))
                        cV.wait()
                        ins = v.tensor_scalar(t_P[:, 0:(qa + 1) * 128],
                                              t_P[:, 0:(qa + 1) * 128],
                                              t_rd[:], None, ALU.mult)
                        ins.then_inc(svP, 1)
                        v.wait_ge(svP, si + 1)
                        for st in range(qa + 1):
                            nTP += 1
                            v.wait_ge(sTPr, nTP)
                            v.tensor_copy(
                                t_PT[:, st * QB + qs * 128: st * QB + (qs + 1) * 128],
                                p_tb[nTP % 2][:, 0:128]).then_inc(sTPe, 1)
                    v.wait_ge(sAV, hi + 1)
                    cV.wait()
                    ins = v.tensor_copy(t_yf[:], pslot(4))
                    ins.then_inc(sYE, 1)
                    v.wait_ge(sYE, hi + 1)
                    qi = hi
                    if qi >= 1:
                        v.wait_ge(sMM2, qi)
                    cV.wait()
                    ins = v.tensor_tensor(t_q2[:], t_yf[:], t_yf[:], ALU.mult)
                    ins.then_inc(svQ, 1)
                    v.wait_ge(svQ, qi + 1)
                    v.wait_ge(sscQ, qi + 1)
                    cV.wait()
                    ins = v.reciprocal(t_rr[:], t_ssq[:])
                    ins.then_inc(svR2, 1)
                    v.wait_ge(svR2, qi + 1)
                    v.wait_ge(sBC, qi + 1)
                    cV.wait()
                    ins = v.scalar_tensor_tensor(
                        t_ynT[:, h * QB:(h + 1) * QB], t_yf[:], t_sw[:],
                        pslot(2), ALU.mult, ALU.mult)
                    ins.then_inc(sYN, 1)
                    v.wait_ge(sYN, qi + 1)
                for cout in range(ET):
                    wi = qb * ET + cout
                    v.wait_ge(sPM, wi + 1)
                    v.tensor_copy(t_aT[:, cout * QB:(cout + 1) * QB],
                                  pslot(5 if wi % 2 == 0 else 0)).then_inc(sPE, 1)
                v.wait_ge(sX0, 16 + 16 * NSUB * (qb + 1))
                for j in range(ET * NSUB):
                    et, tsub = divmod(j, NSUB)
                    jj = qb * ET * NSUB + j
                    nTP += 1
                    v.wait_ge(sTPr, nTP)
                    v.scalar_tensor_tensor(
                        t_s[:, tsub * D + et * 128: tsub * D + (et + 1) * 128],
                        t_x0t[:, tsub * D + et * 128: tsub * D + (et + 1) * 128],
                        1.0, p_tb[nTP % 2][:, 0:128],
                        ALU.mult, ALU.add).then_inc(sTPe, 1)
                v.wait_ge(sTPe, nTP)
                for tsub in range(NSUB):
                    for gix in range(4):
                        ins = v.bn_stats(
                            t_stats[:, tsub * 24 + gix * 6: tsub * 24 + gix * 6 + 6],
                            t_s[:, tsub * D + gix * 512: tsub * D + (gix + 1) * 512])
                        cV.op(ins)
                cV.wait()
                for tsub in range(NSUB):
                    ins_ = v.bn_aggr(t_mv[:, 2 * tsub:2 * tsub + 2],
                                     t_stats[:, tsub * 24:(tsub + 1) * 24])
                    if tsub == NSUB - 1:
                        ins_.then_inc(sv, 1)
                v.wait_ge(ssc, qb + 1)
                for tsub in range(NSUB):
                    v.tensor_scalar(t_x0t[:, tsub * D:(tsub + 1) * D],
                                    t_s[:, tsub * D:(tsub + 1) * D],
                                    t_mv[:, 2 * tsub:2 * tsub + 1],
                                    t_rstd[:, tsub:tsub + 1],
                                    ALU.subtract, ALU.mult).then_inc(sLN, 1)

    return nc


_NCS = {}


def build_l0attnB():
    nc = bass.Bass("TRN2", target_bir_lowering=False)
    io = {}
    for name, shape, dt, kind in PHASE_IO["l0attnB"]:
        io[name] = nc.dram_tensor(name, shape, dt, kind=kind).ap()
    phase_l0attnB(nc, io)
    return nc

def _get_nc(name):
    if name not in _NCS:
        _NCS[name] = globals()["build_" + name]()
    return _NCS[name]


def _alibi_table():
    def slopes(n):
        start = 2 ** (-2 ** (-(math.log2(n) - 3)))
        return [start * start ** i for i in range(n)]
    sl = np.array(slopes(16), dtype=np.float32)
    return np.ascontiguousarray(sl[:, None] * np.arange(T, dtype=np.float32)[None, :])


def _launch(name, per_core, shared):
    nc = _get_nc(name)
    in_maps = [dict(shared, **pc) for pc in per_core]
    res = run_bass_kernel_spmd(nc, in_maps, core_ids=list(range(NCORES)))
    return res.results


def kernel_unfused(**inp):
    cT = np.ascontiguousarray
    x0s = [cT(inp["L0_attn_x"][b]) for b in range(NCORES)]

    r = _launch("l0attnA", [dict(x0=x) for x in x0s],
                dict(Wq=cT(inp["L0_attn_Wq"]), Wk=cT(inp["L0_attn_Wk"]),
                     Wv=cT(inp["L0_attn_Wv"]),
                     qrw=cT(inp["L0_attn_q_rms_w"]), krw=cT(inp["L0_attn_k_rms_w"])))
    r = _launch("l0attnB",
                [dict(qT=r[b]["qT"], kT=r[b]["kT"], vT=r[b]["vT"], x0=x0s[b])
                 for b in range(NCORES)],
                dict(Wproj=cT(inp["L0_attn_Wproj"]), subw=cT(inp["L0_attn_subln_w"]),
                     Rt=_alibi_table()))
    x1s = [r[b]["x1"] for b in range(NCORES)]

    r = _launch("l0ffnA", [dict(x1=x1s[b]) for b in range(NCORES)],
                dict(routerT=cT(inp["L0_ffn_router_w"].T),
                     sgT=cT(inp["L0_ffn_shared_gate"].T),
                     suT=cT(inp["L0_ffn_shared_up"].T),
                     wgT=cT(inp["L0_ffn_w_gate"].transpose(0, 2, 1)),
                     wuT=cT(inp["L0_ffn_w_up"].transpose(0, 2, 1))))
    wdT_all = np.concatenate(
        [inp["L0_ffn_shared_down"].T]
        + [inp["L0_ffn_w_down"][e].T for e in range(8)], axis=0)
    r = _launch("l0ffnB",
                [dict(hsT_all=r[b]["hsT_all"], x1=x1s[b]) for b in range(NCORES)],
                dict(wdT_all=cT(wdT_all)))
    x2s = [r[b]["x2"] for b in range(NCORES)]
    del x1s

    r = _launch("l1preA", [dict(x2=x2s[b]) for b in range(NCORES)],
                dict(inprojT=cT(inp["L1_attn_in_proj_w"].T),
                     convw=cT(inp["L1_attn_conv_w"].reshape(4096, 4)),
                     convb=cT(inp["L1_attn_conv_b"]),
                     ln1w=cT(inp["L1_attn_ln1_w"]), ln1b=cT(inp["L1_attn_ln1_b"])))
    xcs = [r[b]["xcT"] for b in range(NCORES)]
    zs = [r[b]["zT"] for b in range(NCORES)]
    r = _launch("l1preB", [dict(xcT=xcs[b]) for b in range(NCORES)],
                dict(xprojT=cT(inp["L1_attn_x_proj_w"].T),
                     dtprojT=cT(inp["L1_attn_dt_proj_w"].T),
                     dtprojb=cT(inp["L1_attn_dt_proj_b"])))
    r = _launch("l1scan",
                [dict(xcT=xcs[b], dtT=r[b]["dtT"], BT=r[b]["BT"], CTo=r[b]["CTo"],
                      zT=zs[b]) for b in range(NCORES)],
                dict(A_log=cT(inp["L1_attn_A_log"]), D_skip=cT(inp["L1_attn_D_skip"])))
    del xcs, zs
    r = _launch("l1out",
                [dict(yfT=r[b]["yfT"], x2=x2s[b]) for b in range(NCORES)],
                dict(outprojT=cT(inp["L1_attn_out_proj_w"].T)))
    xmids = [r[b]["xmid"] for b in range(NCORES)]
    r = _launch("l1ffA", [dict(xmid=xmids[b]) for b in range(NCORES)],
                dict(ffprojT=cT(inp["L1_attn_ff_proj_w"].T),
                     ln2w=cT(inp["L1_attn_ln2_w"]), ln2b=cT(inp["L1_attn_ln2_b"])))
    r = _launch("l1ffB",
                [dict(hsT=r[b]["hsT"], x2=x2s[b], xmid=xmids[b])
                 for b in range(NCORES)],
                dict(ffoutT=cT(inp["L1_attn_ff_out_w"].T)))
    del x2s, xmids
    r = _launch("l1ffn", [dict(x3=r[b]["x3"]) for b in range(NCORES)],
                dict(factors=cT(inp["L1_ffn_factors"]), bias=cT(inp["L1_ffn_bias"])))
    return np.stack([r[b]["out"] for b in range(NCORES)], axis=0)


FUSED_ORDER = ["l0attnA", "l0attnB", "l0ffnA", "l0ffnB", "l1preA", "l1preB",
               "l1scan", "l1out", "l1ffA", "l1ffB", "l1ffn"]
_PRODUCED = {"qT", "kT", "vT", "x1", "hsT_all", "comb_d", "x2", "xcT", "zT",
             "dtT", "BT", "CTo", "yfT", "xmid", "hsT", "x3"}


def build_fused():
    nc = bass.Bass("TRN2", target_bir_lowering=False)
    io = {}
    for ph in FUSED_ORDER:
        for name, shape, dt, kind in PHASE_IO[ph]:
            if name in io:
                continue
            if name == "out":
                k = "ExternalOutput"
            elif name in _PRODUCED:
                k = "Internal"
            else:
                k = "ExternalInput"
            io[name] = nc.dram_tensor(name, shape, dt, kind=k).ap()
    for ph in FUSED_ORDER:
        with nc.reset_on_exit():
            globals()["phase_" + ph](nc, io, px=ph + "_")
    return nc


def kernel_fused(**inp):
    cT = np.ascontiguousarray
    nc = _get_nc("fused")
    shared = dict(
        Wq=cT(inp["L0_attn_Wq"]), Wk=cT(inp["L0_attn_Wk"]),
        Wv=cT(inp["L0_attn_Wv"]), qrw=cT(inp["L0_attn_q_rms_w"]),
        krw=cT(inp["L0_attn_k_rms_w"]), Wproj=cT(inp["L0_attn_Wproj"]),
        subw=cT(inp["L0_attn_subln_w"]), Rt=_alibi_table(),
        routerT=cT(inp["L0_ffn_router_w"].T),
        sgT=cT(inp["L0_ffn_shared_gate"].T), suT=cT(inp["L0_ffn_shared_up"].T),
        wgT=cT(inp["L0_ffn_w_gate"].transpose(0, 2, 1)),
        wuT=cT(inp["L0_ffn_w_up"].transpose(0, 2, 1)),
        wdT_all=cT(np.concatenate(
            [inp["L0_ffn_shared_down"].T]
            + [inp["L0_ffn_w_down"][e].T for e in range(8)], axis=0)),
        inprojT=cT(inp["L1_attn_in_proj_w"].T),
        convw=cT(inp["L1_attn_conv_w"].reshape(4096, 4)),
        convb=cT(inp["L1_attn_conv_b"]),
        ln1w=cT(inp["L1_attn_ln1_w"]), ln1b=cT(inp["L1_attn_ln1_b"]),
        xprojT=cT(inp["L1_attn_x_proj_w"].T),
        dtprojT=cT(inp["L1_attn_dt_proj_w"].T),
        dtprojb=cT(inp["L1_attn_dt_proj_b"]),
        A_log=cT(inp["L1_attn_A_log"]), D_skip=cT(inp["L1_attn_D_skip"]),
        outprojT=cT(inp["L1_attn_out_proj_w"].T),
        ffprojT=cT(inp["L1_attn_ff_proj_w"].T),
        ln2w=cT(inp["L1_attn_ln2_w"]), ln2b=cT(inp["L1_attn_ln2_b"]),
        ffoutT=cT(inp["L1_attn_ff_out_w"].T),
        factors=cT(inp["L1_ffn_factors"]), bias=cT(inp["L1_ffn_bias"]),
    )
    in_maps = [dict(shared, x0=np.ascontiguousarray(inp["L0_attn_x"][b]))
               for b in range(NCORES)]
    res = run_bass_kernel_spmd(nc, in_maps, core_ids=list(range(NCORES)))
    return np.stack([res.results[b]["out"] for b in range(NCORES)], axis=0)


def kernel(**inputs):
    return kernel_fused(**inputs)
